# version 02
# baseline (speedup 1.0000x reference)
import math
import numpy as np
import ml_dtypes
from contextlib import ExitStack

import concourse.bass as bass
import concourse.mybir as mybir
from concourse.bass_utils import run_bass_kernel_spmd

F32 = mybir.dt.float32
BF16 = mybir.dt.bfloat16
AF = mybir.ActivationFunctionType
ALU = mybir.AluOpType
AX = mybir.AxisListType

T = 4096
D = 1024
NC = 8
TT = 512
NTT = T // TT

ENGS = ("pe", "dve", "act", "pool", "sp")
DMA_ENGS = ("sp", "act", "pool")


class TL:
    __slots__ = ("name", "sem", "count")

    def __init__(self, name, sem):
        self.name = name
        self.sem = sem
        self.count = 0


class Buf:
    __slots__ = ("name", "last_w", "readers")

    def __init__(self, name):
        self.name = name
        self.last_w = None
        self.readers = {}

    def reset(self):
        self.last_w = None
        self.readers = {}


class Prog:
    def __init__(self, nc, stack):
        self.nc = nc
        self.ops = {k: [] for k in ENGS}
        self.tls = {}
        for name in ("pe", "dve", "act", "pool", "sp", "d_sp", "d_act", "d_pool"):
            sem = stack.enter_context(nc.semaphore("tl_" + name))
            self.tls[name] = TL(name, sem)
        self.waited = {k: {} for k in ENGS}
        self.n_units = 0

    def emit(self, ek, fn, r=(), w=(), dma=False, deps=(), tag=""):
        tl_exec = self.tls[ek]
        tl_done = self.tls["d_" + ek] if dma else tl_exec
        waited = self.waited[ek]
        waits = {}

        def need(dep):
            if dep is None:
                return
            tl, c = dep
            if tl is tl_exec and ek == "pe" and not dma:
                return
            if waited.get(tl, 0) >= c:
                return
            if waits.get(tl, 0) >= c:
                return
            waits[tl] = c

        for b in r:
            need(b.last_w)
        for b in w:
            need(b.last_w)
            for tl, c in b.readers.items():
                need((tl, c))
        for dep in deps:
            need(dep)

        for tl, c in waits.items():
            waited[tl] = max(waited.get(tl, 0), c)

        inc = 16 if dma else 1
        tl_done.count += inc
        cnt = tl_done.count
        wl = [(tl.sem, c) for tl, c in waits.items()]
        sem_done = tl_done.sem

        def run(eng, wl=wl, fn=fn, sem_done=sem_done, inc=inc, tag=tag):
            for sem, c in wl:
                eng.wait_ge(sem, c)
            inst = fn(eng)
            assert inst is not None, f"unit {tag} returned no instruction"
            inst.then_inc(sem_done, inc)

        self.ops[ek].append(run)
        self.n_units += 1

        for b in w:
            b.last_w = (tl_done, cnt)
            b.readers = {}
        for b in r:
            prev = b.readers.get(tl_done, 0)
            if cnt > prev:
                b.readers[tl_done] = cnt
        return (tl_done, cnt)

    def fence_deps(self):
        return [(tl, tl.count) for tl in self.tls.values()]

    def finalize(self):
        nc = self.nc
        with nc.Block() as block:
            @block.tensor
            def _(eng):
                for f in self.ops["pe"]:
                    f(eng)

            @block.vector
            def _(eng):
                for f in self.ops["dve"]:
                    f(eng)

            @block.scalar
            def _(eng):
                for f in self.ops["act"]:
                    f(eng)

            @block.gpsimd
            def _(eng):
                for f in self.ops["pool"]:
                    f(eng)

            @block.sync
            def _(eng):
                for f in self.ops["sp"]:
                    f(eng)


def chunks_of(total, size):
    out = []
    off = 0
    while off < total:
        out.append((off, min(size, total - off)))
        off += size
    return out


class Ctx:

    def __init__(self, nc, stack):
        self.nc = nc
        self.stack = stack
        self.p = Prog(nc, stack)
        self.bufs = {}
        self.tensors = {}
        self.consts_np = {}
        self._const_handles = {}
        self._ccol_vals = {}
        self.uid = 0

    def sb(self, name, shape, dtype):
        t = self.stack.enter_context(self.nc.sbuf_tensor(name, list(shape), dtype))
        self.tensors[name] = t
        self.bufs[name] = Buf(name)
        return t

    def ps(self, name, shape, dtype=F32):
        t = self.stack.enter_context(self.nc.psum_tensor(name, list(shape), dtype))
        self.tensors[name] = t
        self.bufs[name] = Buf(name)
        return t

    def dram(self, name, shape, dtype, kind="Internal"):
        t = self.nc.dram_tensor(name, list(shape), dtype, kind=kind)
        self.tensors[name] = t
        self.bufs[name] = Buf(name)
        return t

    def buf(self, name):
        if name not in self.bufs:
            self.bufs[name] = Buf(name)
        return self.bufs[name]

    def const(self, name, arr):
        if name not in self._const_handles:
            h = self.nc.inline_tensor(np.ascontiguousarray(arr), name="c_" + name)
            self._const_handles[name] = h
            self.bufs["c_" + name] = Buf("c_" + name)
            self.bufs["c_" + name].last_w = None
        return self._const_handles[name]

    def emit(self, *a, **k):
        return self.p.emit(*a, **k)

    def load(self, dst_ap, src_ap, wbuf, rbufs=(), q="sp", tag="load"):
        return self.emit(q, lambda e: e.dma_start(dst_ap, src_ap),
                         r=list(rbufs), w=[wbuf], dma=True, tag=tag)

    def store(self, dst_ap, src_ap, rbuf, wbufs=(), q="act", tag="store"):
        return self.emit(q, lambda e: e.dma_start(dst_ap, src_ap),
                         r=[rbuf], w=list(wbufs), dma=True, tag=tag)



def cast_weight(ctx, name, src_dram, src_buf, rows, cols):
    dst = ctx.dram("wbf_" + name, (rows, cols), BF16)
    dbuf = ctx.buf("wbf_" + name)
    step = max(1, (1 << 21) // (cols * 4))
    for r0 in range(0, rows, step):
        r1 = min(rows, r0 + step)
        ctx.emit("pool",
                 lambda e, r0=r0, r1=r1: e.dma_start(dst[r0:r1, :], src_dram[r0:r1, :]),
                 r=[src_buf], w=[dbuf], dma=True, tag=f"cast_{name}_{r0}")
    return dst, dbuf


def linear(ctx, *, tag, lhsT, rhs, K, M, N, evac,
           m_tile=1024, n_tile=TT, wslots=None, psums=None,
           k_chunk=128, m_base=0, k_base=0, k_base_rhs=None):
    kchunks = chunks_of(K, k_chunk)
    kidx = {k0: i for i, (k0, _) in enumerate(kchunks)}
    if k_base_rhs is None:
        k_base_rhs = k_base
    if wslots is None:
        wslots = ("wsl0", "wsl1")
    if psums is None:
        psums = ("pf0", "pf1")
    pcycle = [(ctx.tensors[s], ctx.bufs[s]) for s in psums]
    pi = 0
    ri = 0

    rhs_kind = rhs[0]
    if rhs_kind == "sbuf":
        _, rt0, rb0, rcol0 = rhs
        rbufs0 = list(rb0) if isinstance(rb0, (list, tuple)) else [rb0]

    lk = lhsT[0]
    if lk == "sbuf":
        _, wt, wb, wcol = lhsT

    mslices = chunks_of(M, m_tile)
    for si, (m0, msz) in enumerate(mslices):
        if lk in ("dram_km", "dram_mk"):
            _, wh, whb = lhsT
            sname = wslots[si % len(wslots)]
            wt = ctx.tensors[sname]
            wb = ctx.bufs[sname]
            assert len(kchunks) * m_tile <= wt.shape[1], f"{tag}: wslot too small"
            for ki, (k0, ksz) in enumerate(kchunks):
                if lk == "dram_km":
                    ctx.emit("sp", lambda e, k0=k0 + k_base, ksz=ksz, ki=ki, m0=m0 + m_base, msz=msz, wt=wt, wh=wh: e.dma_start(
                        wt[0:ksz, ki * m_tile: ki * m_tile + msz],
                        wh[k0:k0 + ksz, m0:m0 + msz]),
                        r=[whb], w=[wb], dma=True, tag=tag + "_wld")
                else:
                    ctx.emit("sp", lambda e, k0=k0 + k_base, ksz=ksz, ki=ki, m0=m0 + m_base, msz=msz, wt=wt, wh=wh: e.dma_start(
                        wt[0:ksz, ki * m_tile: ki * m_tile + msz],
                        wh[m0:m0 + msz, k0:k0 + ksz], transpose=True),
                        r=[whb], w=[wb], dma=True, tag=tag + "_wldT")

            def wcol(k_off, gm, m0=m0, m_tile=m_tile, kidx=kidx):
                return kidx[k_off] * m_tile + (gm - m0)

        for n0, nsz in chunks_of(N, n_tile):
            if rhs_kind == "dram":
                _, rh, rhb = rhs
                rname = "rsl" + str(ri % 2)
                ri += 1
                rt = ctx.tensors[rname]
                rb = ctx.bufs[rname]
                assert len(kchunks) * n_tile <= rt.shape[1], f"{tag}: rslot too small"
                for ki, (k0, ksz) in enumerate(kchunks):
                    ctx.emit("sp", lambda e, k0=k0 + k_base_rhs, ksz=ksz, ki=ki, n0=n0, nsz=nsz, rt=rt, rh=rh: e.dma_start(
                        rt[0:ksz, ki * n_tile: ki * n_tile + nsz],
                        rh[k0:k0 + ksz, n0:n0 + nsz]),
                        r=[rhb], w=[rb], dma=True, tag=tag + "_rld")
                rbufs = [rb]

                def rcol(k_off, nn, n0=n0, n_tile=n_tile, kidx=kidx):
                    return kidx[k_off] * n_tile + (nn - n0)
            else:
                rt, rbufs, rcol = rt0, rbufs0, rcol0

            for mc0, mcsz in chunks_of(msz, 128):
                gm0 = m0 + mc0
                ps_t, ps_b = pcycle[pi % len(pcycle)]
                pi += 1

                def mmgroup(e, kchunks=kchunks, wt=wt, wcol=wcol, rt=rt, rcol=rcol,
                            gm0=gm0, mcsz=mcsz, n0=n0, nsz=nsz, ps_t=ps_t):
                    inst = None
                    nk = len(kchunks)
                    for ki, (k0, ksz) in enumerate(kchunks):
                        wc = wcol(k0, gm0)
                        rc = rcol(k0, n0)
                        inst = e.matmul(ps_t[0:mcsz, 0:nsz],
                                        wt[0:ksz, wc:wc + mcsz],
                                        rt[0:ksz, rc:rc + nsz],
                                        start=(ki == 0), stop=(ki == nk - 1))
                    return inst

                ctx.emit("pe", mmgroup, r=[wb] + list(rbufs), w=[ps_b], tag=tag + "_mm")
                evac(ctx, gm0, mcsz, n0, nsz, ps_t, ps_b)


def evac_simple(dst_fn, engine="act", func=None, bias=None, scale=1.0, out_extra=None):
    def evac(ctx, m0, msz, n0, nsz, ps_t, ps_b):
        dst_ap, dst_b, extra = dst_fn(m0, msz, n0, nsz)
        rbufs = [ps_b] + list(extra)
        if engine == "act":
            if bias is not None:
                bap, bb = bias(m0, msz)
                rbufs.append(bb)
                f = func if func is not None else AF.Identity
                ctx.emit("act", lambda e, f=f: e.activation(
                    dst_ap, ps_t[0:msz, 0:nsz], f, bias=bap, scale=scale),
                    r=rbufs, w=[dst_b], tag="evac")
            elif func is not None:
                ctx.emit("act", lambda e: e.activation(
                    dst_ap, ps_t[0:msz, 0:nsz], func, scale=scale),
                    r=rbufs, w=[dst_b], tag="evac")
            elif scale != 1.0:
                ctx.emit("act", lambda e: e.mul(dst_ap, ps_t[0:msz, 0:nsz], scale),
                         r=rbufs, w=[dst_b], tag="evac")
            else:
                ctx.emit("act", lambda e: e.copy(dst_ap, ps_t[0:msz, 0:nsz]),
                         r=rbufs, w=[dst_b], tag="evac")
        else:
            if scale != 1.0:
                ctx.emit("dve", lambda e: e.tensor_scalar_mul(dst_ap, ps_t[0:msz, 0:nsz], scale),
                         r=rbufs, w=[dst_b], tag="evac")
            else:
                ctx.emit("dve", lambda e: e.tensor_copy(dst_ap, ps_t[0:msz, 0:nsz]),
                         r=rbufs, w=[dst_b], tag="evac")
    return evac



def emit_transpose_in(ctx, xin, xin_buf, xh, xh_buf, n_rows=T, n_cols=D):
    af = ctx.tensors["af"]
    tin = af[:, 0:1024]; tin_b = ctx.buf("v_tp_in")
    tout = af[:, 1024:2048]; tout_b = ctx.buf("v_tp_out")
    ident = ctx.tensors["identf"]; ident_b = ctx.bufs["identf"]
    ps = ctx.tensors["pw"]; ps_b = ctx.bufs["pw"]
    for tc in range(n_rows // 128):
        ctx.load(tin[:, 0:n_cols], xin[tc * 128:(tc + 1) * 128, :], tin_b,
                 rbufs=[xin_buf], tag="tpin_ld")
        for c in range(n_cols // 128):
            ctx.emit("pe", lambda e, c=c: e.transpose(
                ps[0:128, 0:128], tin[0:128, c * 128:(c + 1) * 128], ident[:, 0:128]),
                r=[tin_b, ident_b], w=[ps_b], tag="tpin_tr")
            ctx.emit("dve", lambda e, c=c: e.tensor_copy(
                tout[:, c * 128:(c + 1) * 128], ps[0:128, 0:128]),
                r=[ps_b], w=[tout_b], tag="tpin_ev")
        for c in range(n_cols // 128):
            ctx.store(xh[c * 128:(c + 1) * 128, tc * 128:(tc + 1) * 128],
                      tout[:, c * 128:(c + 1) * 128], tout_b, wbufs=[xh_buf],
                      q="act", tag="tpin_st")


def emit_transpose_out(ctx, yh, yh_buf, yout, yout_buf, n_rows=D, n_cols=T):
    af = ctx.tensors["af"]
    tin = af[:, 0:1024]; tin_b = ctx.buf("v_tp_in")
    tout = af[:, 1024:2048]; tout_b = ctx.buf("v_tp_out")
    ident = ctx.tensors["identf"]; ident_b = ctx.bufs["identf"]
    ps = ctx.tensors["pw"]; ps_b = ctx.bufs["pw"]
    for tc in range(n_cols // 128):
        for c in range(n_rows // 128):
            ctx.load(tin[:, c * 128:(c + 1) * 128],
                     yh[c * 128:(c + 1) * 128, tc * 128:(tc + 1) * 128], tin_b,
                     rbufs=[yh_buf], tag="tpo_ld")
        for c in range(n_rows // 128):
            ctx.emit("pe", lambda e, c=c: e.transpose(
                ps[0:128, 0:128], tin[0:128, c * 128:(c + 1) * 128], ident[:, 0:128]),
                r=[tin_b, ident_b], w=[ps_b], tag="tpo_tr")
            ctx.emit("dve", lambda e, c=c: e.tensor_copy(
                tout[:, c * 128:(c + 1) * 128], ps[0:128, 0:128]),
                r=[ps_b], w=[tout_b], tag="tpo_ev")
        ctx.store(yout[tc * 128:(tc + 1) * 128, 0:n_rows], tout[:, 0:n_rows],
                  tout_b, wbufs=[yout_buf], q="act", tag="tpo_st")



def alloc_standard(ctx):
    c = ctx
    c.sb("xn", (128, NC * T), BF16)
    c.sb("wsl0", (128, 8192), BF16)
    c.sb("wsl1", (128, 8192), BF16)
    c.sb("af", (128, 11776), F32)
    c.sb("ab", (128, 12288), BF16)
    c.sb("rsl0", (128, 4096), BF16)
    c.sb("rsl1", (128, 4096), BF16)
    c.sb("rows", (96, 5568), F32)
    c.sb("smalls", (128, 512), F32)
    c.sb("smallsb", (128, 512), BF16)
    c.ps("pf0", (128, 512), F32)
    c.ps("pf1", (128, 512), F32)
    c.ps("pf2", (128, 512), F32)
    c.ps("pf3", (128, 512), F32)
    c.ps("pf4", (128, 512), F32)
    c.ps("pf5", (128, 512), F32)
    c.ps("pw", (128, 512), F32)
    c.ps("pb", (128, 1024), BF16)
    rem = ctx.nc.sbuf_bytes_remaining
    print("SBUF bytes remaining after alloc:", rem)


def view(ctx, arena, name, col0, width, rows=128, row0=0):
    t = ctx.tensors[arena]
    ap = t[row0:row0 + rows, col0:col0 + width]
    b = ctx.buf("v_" + name)
    return ap, b


def new_phase(ctx):
    deps = ctx.p.fence_deps()
    ctx.phase_deps = deps
    for n, b in ctx.bufs.items():
        if n.startswith("v_"):
            b.last_w = None
            b.readers = {}
    return deps


def _ctx_emit(self, ek, fn, r=(), w=(), dma=False, deps=(), tag=""):
    pd = getattr(self, "phase_deps", ())
    if pd:
        deps = list(deps) + list(pd)
    return self.p.emit(ek, fn, r=r, w=w, dma=dma, deps=deps, tag=tag)

Ctx.emit = _ctx_emit



SM_ONESROW = 0

def emit_load_smalls(ctx):
    onesc = np.ones((128, 1), np.float32)
    h = ctx.const("ones128c", onesc)
    ctx.load(ctx.tensors["smalls"][:, 0:1], h[0:128, 0:1], ctx.buf("sm_ones"), tag="sm1")
    onesr = np.ones((1, 128), np.float32)
    h2 = ctx.const("ones128r", onesr)
    ctx.load(ctx.tensors["smalls"][0:1, 8:136], h2[0:1, 0:128], ctx.buf("sm_onesr"), tag="sm2")
    bp = np.zeros((2, 128), np.float32)
    bp[0, 0:64] = 1.0
    bp[1, 64:128] = 1.0
    h3 = ctx.const("blockpat", bp)
    ctx.load(ctx.tensors["smalls"][0:2, 136:264], h3[0:2, 0:128], ctx.buf("sm_bp"), tag="sm3")
    identf = np.eye(128, dtype=np.float32)
    h4 = ctx.const("identf", identf)
    ctx.sb("identf", (128, 128), F32)
    ctx.load(ctx.tensors["identf"][:], h4[0:128, 0:128], ctx.bufs["identf"], tag="sm4")
    identb = np.eye(128, dtype=ml_dtypes.bfloat16)
    h5 = ctx.const("identb", identb)
    ctx.sb("identb", (128, 128), BF16)
    ctx.load(ctx.tensors["identb"][:], h5[0:128, 0:128], ctx.bufs["identb"], tag="sm5")

def ones_col(ctx):
    return ctx.tensors["smalls"][:, 0:1], ctx.buf("sm_ones")

def ones_col_k(ctx, k):
    return ctx.tensors["smalls"][0:k, 0:1], ctx.buf("sm_ones")

def ones_row(ctx, m=128):
    return ctx.tensors["smalls"][0:1, 8:8 + m], ctx.buf("sm_onesr")

def block_pat(ctx):
    return ctx.tensors["smalls"][0:2, 136:264], ctx.buf("sm_bp")



def r_ts(ctx, out_ap, out_b, in_ap, in_b, s1, op0, s2=None, op1=ALU.bypass, eng="dve", extra_r=()):
    ctx.emit(eng, lambda e: e.tensor_scalar(out_ap, in_ap, s1, s2, op0, op1),
             r=[in_b] + list(extra_r), w=[out_b], tag="r_ts")

def r_tt(ctx, out_ap, out_b, a_ap, a_b, b_ap, b_b, op, eng="dve"):
    ctx.emit(eng, lambda e: e.tensor_tensor(out_ap, a_ap, b_ap, op),
             r=[a_b, b_b], w=[out_b], tag="r_tt")

def r_stt(ctx, out_ap, out_b, in0_ap, in0_b, scalar, in1_ap, in1_b, op0, op1, eng="dve", extra_r=()):
    ctx.emit(eng, lambda e: e.scalar_tensor_tensor(out_ap, in0_ap, scalar, in1_ap, op0, op1),
             r=[in0_b, in1_b] + list(extra_r), w=[out_b], tag="r_stt")

def r_act(ctx, out_ap, out_b, in_ap, in_b, func, bias=0.0, scale=1.0, extra_r=()):
    if isinstance(bias, float) and bias == 0.0:
        ctx.emit("act", lambda e: e.activation(out_ap, in_ap, func, scale=scale),
                 r=[in_b] + list(extra_r), w=[out_b], tag="r_act")
    else:
        ctx.emit("act", lambda e: e.activation(out_ap, in_ap, func, bias=bias, scale=scale),
                 r=[in_b] + list(extra_r), w=[out_b], tag="r_act")

def r_recip(ctx, out_ap, out_b, in_ap, in_b):
    ctx.emit("dve", lambda e: e.reciprocal(out_ap, in_ap), r=[in_b], w=[out_b], tag="r_recip")


def bcast_cols(ctx, ps_name, pairs, nsz=TT):
    ps_t = ctx.tensors[ps_name]
    ps_b = ctx.bufs[ps_name]
    rb = []
    for (ca, cb), (ra, rb_) in pairs:
        rb += [cb, rb_]

    def fn(e, pairs=pairs, ps_t=ps_t, nsz=nsz):
        inst = None
        n = len(pairs)
        for i, ((ca, cb), (ra, rb_)) in enumerate(pairs):
            P = ca.shape[1]
            inst = e.matmul(ps_t[0:P, 0:nsz], ca, ra, start=(i == 0), stop=(i == n - 1))
        return inst
    ctx.emit("pe", fn, r=rb, w=[ps_b], tag="bcast")
    return ps_t, ps_b



def emit_ln(ctx, src_fn, n_tok=T, eps=1e-5, dst_col=None, affine=None, dst=None):
    af = ctx.tensors["af"]
    if dst is None:
        xnt = ctx.tensors["xn"]
        xnb = ctx.bufs["xn"]
        dst = (xnt, xnb, lambda c, t0: c * T + t0)
    dt, db, dcol = dst
    rows = ctx.tensors["rows"]
    xb_b = ctx.buf("v_ln_xblk")
    sq_b = ctx.buf("v_ln_sq")
    r_sum = ctx.buf("v_ln_rsum")
    r_msq = ctx.buf("v_ln_rmsq")
    r_a = ctx.buf("v_ln_ra")
    r_b = ctx.buf("v_ln_rb")
    oc, ocb = ones_col(ctx)
    orow, orb = ones_row(ctx, 128)
    pf4 = ctx.tensors["pf4"]; pf4b = ctx.bufs["pf4"]
    pf5 = ctx.tensors["pf5"]; pf5b = ctx.bufs["pf5"]
    pf2 = ctx.tensors["pf2"]; pf2b = ctx.bufs["pf2"]
    pf3 = ctx.tensors["pf3"]; pf3b = ctx.bufs["pf3"]

    for ti, (t0, tsz) in enumerate(chunks_of(n_tok, TT)):
        for c in range(NC):
            xa = af[:, c * TT:(c + 1) * TT]
            src_fn(c, t0, tsz, xa, xb_b)
            sq = af[:, NC * TT: NC * TT + TT]
            ctx.emit("act", lambda e, xa=xa, sq=sq: e.activation(sq[:, 0:tsz], xa[:, 0:tsz], AF.Square),
                     r=[xb_b], w=[sq_b], tag="ln_sq")
            ctx.emit("pe", lambda e, xa=xa, c=c, tsz=tsz: e.matmul(
                pf4[0:1, 0:tsz], oc, xa[:, 0:tsz], start=(c == 0), stop=(c == NC - 1)),
                r=[xb_b, ocb], w=[pf4b], tag="ln_mmA")
            ctx.emit("pe", lambda e, sq=sq, c=c, tsz=tsz: e.matmul(
                pf5[0:1, 0:tsz], oc, sq[:, 0:tsz], start=(c == 0), stop=(c == NC - 1)),
                r=[sq_b, ocb], w=[pf5b], tag="ln_mmB")
        mu = rows[0:1, 0:tsz]
        msq = rows[0:1, 512:512 + tsz]
        ra = rows[0:1, 1536:1536 + tsz]
        rb2 = rows[0:1, 2048:2048 + tsz]
        r_ts(ctx, mu, r_sum, pf4[0:1, 0:tsz], pf4b, 1.0 / D, ALU.mult)
        r_ts(ctx, msq, r_msq, pf5[0:1, 0:tsz], pf5b, 1.0 / D, ALU.mult)
        tmp = rows[0:1, 1024:1024 + tsz]
        tmpb = ctx.buf("v_ln_tmp")
        r_act(ctx, tmp, tmpb, mu, r_sum, AF.Square)
        r_tt(ctx, ra, r_a, msq, r_msq, tmp, tmpb, ALU.subtract)
        r_ts(ctx, ra, r_a, ra, r_a, eps, ALU.add)
        r_recip(ctx, ra, r_a, ra, r_a)
        r_act(ctx, ra, r_a, ra, r_a, AF.Sqrt)
        r_stt(ctx, rb2, r_b, mu, r_sum, -1.0, ra, r_a, ALU.mult, ALU.mult)
        bcast_cols(ctx, "pf2", [((orow, orb), (ra, r_a))], nsz=tsz)
        bcast_cols(ctx, "pf3", [((orow, orb), (rb2, r_b))], nsz=tsz)
        for c in range(NC):
            xa = af[:, c * TT:(c + 1) * TT]
            tmpt = af[:, NC * TT + TT: NC * TT + 2 * TT]
            tmpt_b = ctx.buf("v_ln_tmpt")
            ctx.emit("dve", lambda e, xa=xa, tsz=tsz, tmpt=tmpt: e.tensor_tensor(
                tmpt[:, 0:tsz], xa[:, 0:tsz], pf2[0:128, 0:tsz], ALU.mult),
                r=[xb_b, pf2b], w=[tmpt_b], tag="ln_mul")
            dap = dt[:, dcol(c, t0): dcol(c, t0) + tsz]
            if affine is None:
                ctx.emit("dve", lambda e, tmpt=tmpt, dap=dap, tsz=tsz: e.tensor_tensor(
                    dap, tmpt[:, 0:tsz], pf3[0:128, 0:tsz], ALU.add),
                    r=[tmpt_b, pf3b], w=[db], tag="ln_add")
            else:
                gap, gb = affine[0](c)
                bap, bb = affine[1](c)
                ctx.emit("dve", lambda e, tmpt=tmpt, tsz=tsz: e.tensor_tensor(
                    tmpt[:, 0:tsz], tmpt[:, 0:tsz], pf3[0:128, 0:tsz], ALU.add),
                    r=[tmpt_b, pf3b], w=[tmpt_b], tag="ln_add2")
                ctx.emit("dve", lambda e, tmpt=tmpt, dap=dap, tsz=tsz, gap=gap, bap=bap: e.tensor_scalar(
                    dap, tmpt[:, 0:tsz], gap, bap, ALU.mult, ALU.add),
                    r=[tmpt_b, gb, bb], w=[db], tag="ln_aff")



def dft_consts():
    n = np.arange(D)
    ang = 2.0 * np.pi * np.outer(n, n) / D
    C = np.cos(ang)
    S = np.sin(ang)
    return C.astype(ml_dtypes.bfloat16), S.astype(ml_dtypes.bfloat16)


def emit_layer1(ctx, H, xh, xhb, xh2, xh2b):
    af = ctx.tensors["af"]
    ab = ctx.tensors["ab"]
    rows = ctx.tensors["rows"]
    smalls = ctx.tensors["smalls"]
    smallsb = ctx.tensors["smallsb"]

    wbf = {}
    for nm in ("Wr", "Wk", "Wv", "Ww", "Wo", "Wfr", "Wfi"):
        wbf[nm] = cast_weight(ctx, "l1" + nm, H[nm][0], H[nm][1], D, D)
    bias_b = ctx.buf("v_l1_bias")
    for c in range(NC):
        ctx.load(smalls[:, 280 + c:281 + c], H["bias"][0][c * 128:(c + 1) * 128, 0:1],
                 bias_b, rbufs=[H["bias"][1]], tag="l1_biasld")

    Cb, Sb = dft_consts()
    hC = ctx.const("dftC", Cb)
    hS = ctx.const("dftS", Sb)
    cCb = ctx.buf("c_dftC")
    cSb = ctx.buf("c_dftS")
    bh = np.zeros((D, 16), np.float32)
    for h in range(16):
        bh[h * 64:(h + 1) * 64, h] = 1.0
    hBH = ctx.const("bh16", bh.astype(ml_dtypes.bfloat16))
    cBHb = ctx.buf("c_bh16")
    bpall = np.zeros((16, NC * 128), np.float32)
    for c in range(NC):
        bpall[0, c * 128:c * 128 + 64] = 0
    for c in range(NC):
        for j in range(2):
            bpall[2 * c + j, c * 128 + j * 64: c * 128 + (j + 1) * 64] = 1.0
    hBP = ctx.const("bpall", bpall)

    new_phase(ctx)
    def src_x(c, t0, tsz, dst_ap, dst_b):
        ctx.load(dst_ap[:, 0:tsz], xh[c * 128:(c + 1) * 128, t0:t0 + tsz], dst_b,
                 rbufs=[xhb], tag="ln_ld")
    emit_ln(ctx, src_x)

    if DEBUG:
        xnt0 = ctx.tensors["xn"]
        for c in range(NC):
            tap(ctx, "xn_c%d" % c, xnt0[:, c * T: c * T + 512], ctx.bufs["xn"], (128, 512))

    new_phase(ctx)
    arf_b = ctx.buf("v_l1_arf")
    stg_b = [ctx.buf("v_l1_stg0"), ctx.buf("v_l1_stg1")]
    Arbf = ctx.dram("l1_Arbf", (D, D), BF16, kind=("ExternalOutput" if DEBUG else "Internal")); Arbf_b = ctx.buf("l1_Arbf")
    Aibf = ctx.dram("l1_Aibf", (D, D), BF16, kind=("ExternalOutput" if DEBUG else "Internal")); Aibf_b = ctx.buf("l1_Aibf")
    Mbf = ctx.dram("l1_Mbf", (D, D), BF16, kind=("ExternalOutput" if DEBUG else "Internal")); Mbf_b = ctx.buf("l1_Mbf")

    def build_sum(tag, lhs1, rhs1, lhs2, rhs2, outdram, outdram_b, sign2):
        def evac1(c2, m0, msz, n0, nsz, ps_t, ps_b):
            dst = af[:, (m0 // 128) * 1024 + n0: (m0 // 128) * 1024 + n0 + nsz]
            c2.emit("dve", lambda e: e.tensor_scalar_mul(dst[0:msz, 0:nsz], ps_t[0:msz, 0:nsz], 1.0 / D),
                    r=[ps_b], w=[arf_b], tag=tag + "_ev1")
        linear(ctx, tag=tag + "a", lhsT=lhs1, rhs=rhs1, K=D, M=D, N=D,
               evac=evac1, m_tile=1024, n_tile=512)
        si = [0]
        def evac2(c2, m0, msz, n0, nsz, ps_t, ps_b):
            src = af[:, (m0 // 128) * 1024 + n0: (m0 // 128) * 1024 + n0 + nsz]
            s = si[0] % 2
            si[0] += 1
            dst = ab[:, s * 512: s * 512 + nsz]
            c2.emit("dve", lambda e: e.scalar_tensor_tensor(
                dst[0:msz, 0:nsz], ps_t[0:msz, 0:nsz], sign2 / D, src[0:msz, 0:nsz],
                ALU.mult, ALU.add), r=[ps_b, arf_b], w=[stg_b[s]], tag=tag + "_ev2")
            c2.store(outdram[m0:m0 + msz, n0:n0 + nsz], dst[0:msz, 0:nsz], stg_b[s],
                     wbufs=[outdram_b], q="act", tag=tag + "_st")
        linear(ctx, tag=tag + "b", lhsT=lhs2, rhs=rhs2, K=D, M=D, N=D,
               evac=evac2, m_tile=1024, n_tile=512)

    build_sum("l1Ar", ("dram_km", hC, cCb), ("dram", wbf["Wfr"][0], wbf["Wfr"][1]),
              ("dram_km", hS, cSb), ("dram", wbf["Wfi"][0], wbf["Wfi"][1]), Arbf, Arbf_b, +1.0)
    build_sum("l1Ai", ("dram_km", hC, cCb), ("dram", wbf["Wfi"][0], wbf["Wfi"][1]),
              ("dram_km", hS, cSb), ("dram", wbf["Wfr"][0], wbf["Wfr"][1]), Aibf, Aibf_b, -1.0)

    new_phase(ctx)
    def evacM1(c2, m0, msz, n0, nsz, ps_t, ps_b):
        dst = af[:, (m0 // 128) * 1024 + n0: (m0 // 128) * 1024 + n0 + nsz]
        c2.emit("dve", lambda e: e.tensor_copy(dst[0:msz, 0:nsz], ps_t[0:msz, 0:nsz]),
                r=[ps_b], w=[arf_b], tag="l1M_ev1")
    linear(ctx, tag="l1Ma", lhsT=("dram_mk", Arbf, Arbf_b), rhs=("dram", hC, cCb),
           K=D, M=D, N=D, evac=evacM1, m_tile=1024, n_tile=512)
    sj = [0]
    def evacM2(c2, m0, msz, n0, nsz, ps_t, ps_b):
        src = af[:, (m0 // 128) * 1024 + n0: (m0 // 128) * 1024 + n0 + nsz]
        s = sj[0] % 2
        sj[0] += 1
        dst = ab[:, s * 512: s * 512 + nsz]
        c2.emit("dve", lambda e: e.scalar_tensor_tensor(
            dst[0:msz, 0:nsz], ps_t[0:msz, 0:nsz], -1.0, src[0:msz, 0:nsz],
            ALU.mult, ALU.add), r=[ps_b, arf_b], w=[stg_b[s]], tag="l1M_ev2")
        c2.store(Mbf[m0:m0 + msz, n0:n0 + nsz], dst[0:msz, 0:nsz], stg_b[s],
                 wbufs=[Mbf_b], q="act", tag="l1M_st")
    linear(ctx, tag="l1Mb", lhsT=("dram_mk", Aibf, Aibf_b), rhs=("dram", hS, cSb),
           K=D, M=D, N=D, evac=evacM2, m_tile=1024, n_tile=512)

    wks_b = ctx.buf("v_l1_wks")
    def evac_wks(c2, m0, msz, n0, nsz, ps_t, ps_b):
        dst = smallsb[:, (m0 // 128) * 16 + n0: (m0 // 128) * 16 + n0 + nsz]
        c2.emit("dve", lambda e: e.tensor_copy(dst[0:msz, 0:nsz], ps_t[0:msz, 0:nsz]),
                r=[ps_b], w=[wks_b], tag="l1_wks_ev")
    linear(ctx, tag="l1wks", lhsT=("dram_km", wbf["Wk"][0], wbf["Wk"][1]),
           rhs=("dram", hBH, cBHb), K=D, M=D, N=16, evac=evac_wks,
           m_tile=1024, n_tile=16)

    kap_b = ctx.buf("v_l1_kap")
    xnt = ctx.tensors["xn"]; xnb = ctx.bufs["xn"]
    def evac_kap(c2, m0, msz, n0, nsz, ps_t, ps_b):
        c2.emit("dve", lambda e: e.tensor_copy(rows[32:48, 1024 + n0:1024 + n0 + nsz], ps_t[0:16, 0:nsz]),
                r=[ps_b], w=[kap_b], tag="l1_kap_ev")
    linear(ctx, tag="l1kap",
           lhsT=("sbuf", smallsb, wks_b, lambda k, m: (k // 128) * 16 + m),
           rhs=("sbuf", xnt, xnb, lambda k, n: (k // 128) * T + n),
           K=D, M=16, N=T, evac=evac_kap, m_tile=16, n_tile=512)

    new_phase(ctx)
    ctx.load(rows[32:48, 0:NC * 128], hBP[0:16, :], ctx.buf("v_bpall"), tag="l1_bpld")
    sc32 = ctx.dram("l1_out32", (D, T), F32, kind=("ExternalOutput" if DEBUG else "Internal"))
    sc32_b = ctx.buf("l1_out32")
    HT = 2048
    r_t = af[:, 0:HT]; r_tb = ctx.buf("v_l1_rt")
    w_t = af[:, HT:2 * HT]; w_tb = ctx.buf("v_l1_wt")
    vu_t = af[:, 2 * HT:3 * HT]; vu_tb = ctx.buf("v_l1_vut")
    sg_t = af[:, 3 * HT:4 * HT]; sg_tb = ctx.buf("v_l1_sgt")
    ms_b = ctx.buf("v_l1_ms")
    init_col = af[:, 4 * HT: 4 * HT + 1]
    init_b = ctx.buf("v_l1_init")
    pf2 = ctx.tensors["pf2"]; pf2b = ctx.bufs["pf2"]
    pf4 = ctx.tensors["pf4"]; pf4b = ctx.bufs["pf4"]
    oc, ocb = ones_col(ctx)

    for c in range(NC):
        for hf in range(2):
            nb = hf * HT
            def mk_evac(dst, dstb, func=None, scale=1.0):
                def ev(c2, m0, msz, n0, nsz, ps_t, ps_b):
                    if func is None:
                        c2.emit("act", lambda e: e.copy(dst[:, n0:n0 + nsz], ps_t[0:msz, 0:nsz]),
                                r=[ps_b], w=[dstb], tag="l1_pj_ev")
                    else:
                        c2.emit("act", lambda e: e.activation(dst[:, n0:n0 + nsz], ps_t[0:msz, 0:nsz], func, scale=scale),
                                r=[ps_b], w=[dstb], tag="l1_pj_ev")
                return ev
            rhs = ("sbuf", xnt, xnb, lambda k, n, nb=nb: (k // 128) * T + nb + n)
            linear(ctx, tag="l1_pr", lhsT=("dram_mk", wbf["Wr"][0], wbf["Wr"][1]), rhs=rhs,
                   K=D, M=128, N=HT, evac=mk_evac(r_t, r_tb, AF.Sigmoid),
                   m_tile=128, m_base=c * 128, wslots=("wsl0",), psums=("pf0", "pf1"))
            linear(ctx, tag="l1_pw", lhsT=("dram_mk", wbf["Ww"][0], wbf["Ww"][1]), rhs=rhs,
                   K=D, M=128, N=HT, evac=mk_evac(w_t, w_tb, AF.Sigmoid),
                   m_tile=128, m_base=c * 128, wslots=("wsl1",), psums=("pf0", "pf1"))
            linear(ctx, tag="l1_pv", lhsT=("dram_mk", wbf["Wv"][0], wbf["Wv"][1]), rhs=rhs,
                   K=D, M=128, N=HT, evac=mk_evac(vu_t, vu_tb),
                   m_tile=128, m_base=c * 128, wslots=("wsl0",), psums=("pf0", "pf1"))
            ctx.emit("dve", lambda e: e.tensor_scalar_mul(w_t, w_t, 0.99),
                     r=[w_tb], w=[w_tb], tag="l1_w99")
            for tt in range(HT // TT):
                n0 = tt * TT
                gt0 = nb + n0
                ctx.emit("pe", lambda e, c=c, gt0=gt0: e.matmul(
                    pf2[0:128, 0:TT], rows[32:48, c * 128:(c + 1) * 128],
                    rows[32:48, 1024 + gt0:1024 + gt0 + TT], start=True, stop=True),
                    r=[ctx.buf("v_bpall"), kap_b], w=[pf2b], tag="l1_kapB")
                ctx.emit("dve", lambda e, n0=n0: e.tensor_tensor(
                    vu_t[:, n0:n0 + TT], vu_t[:, n0:n0 + TT], pf2[0:128, 0:TT], ALU.mult),
                    r=[vu_tb, pf2b], w=[vu_tb], tag="l1_umul")
            if hf == 0:
                ctx.emit("dve", lambda e: e.tensor_tensor_scan(
                    sg_t, w_t, vu_t, 0.0, ALU.mult, ALU.add),
                    r=[w_tb, vu_tb], w=[sg_tb], tag="l1_scan")
            else:
                ctx.emit("dve", lambda e: e.tensor_tensor_scan(
                    sg_t, w_t, vu_t, init_col, ALU.mult, ALU.add),
                    r=[w_tb, vu_tb, init_b], w=[sg_tb], tag="l1_scan")
            ctx.emit("dve", lambda e: e.tensor_copy(init_col, sg_t[:, HT - 1:HT]),
                     r=[sg_tb], w=[init_b], tag="l1_init")
            ctx.emit("dve", lambda e: e.tensor_tensor(r_t, r_t, sg_t, ALU.mult),
                     r=[r_tb, sg_tb], w=[r_tb], tag="l1_rout")
            ctx.store(sc32[c * 128:(c + 1) * 128, nb:nb + HT], r_t, r_tb,
                      wbufs=[sc32_b], q="act", tag="l1_spill")
            ctx.emit("act", lambda e: e.activation(w_t, r_t, AF.Square),
                     r=[r_tb], w=[w_tb], tag="l1_sq")
            for tt in range(HT // TT):
                n0 = tt * TT
                gt0 = nb + n0
                ctx.emit("pe", lambda e, n0=n0: e.matmul(
                    pf4[0:1, 0:TT], oc, w_t[:, n0:n0 + TT], start=True, stop=True),
                    r=[w_tb, ocb], w=[pf4b], tag="l1_msmm")
                if c == 0:
                    ctx.emit("dve", lambda e, gt0=gt0: e.tensor_copy(
                        rows[0:1, gt0:gt0 + TT], pf4[0:1, 0:TT]),
                        r=[pf4b], w=[ms_b], tag="l1_msev")
                else:
                    ctx.emit("dve", lambda e, gt0=gt0: e.tensor_tensor(
                        rows[0:1, gt0:gt0 + TT], rows[0:1, gt0:gt0 + TT],
                        pf4[0:1, 0:TT], ALU.add),
                        r=[ms_b, pf4b], w=[ms_b], tag="l1_msacc")

    if DEBUG:
        tap(ctx, "kap", ctx.tensors["rows"][32:48, 1024:1024 + 512], kap_b, (16, 512))
        tap(ctx, "ms", ctx.tensors["rows"][0:1, 0:4096], ms_b, (1, 4096))

    eps32 = float(np.finfo(np.float32).eps)
    arow = rows[0:1, 0:T]
    arow_b = ctx.buf("v_l1_arow")
    for t0, tsz in chunks_of(T, TT):
        r_ts(ctx, arow[:, t0:t0 + tsz], arow_b, rows[0:1, t0:t0 + tsz], ms_b,
             1.0 / D, ALU.mult, eps32, ALU.add)
        r_recip(ctx, arow[:, t0:t0 + tsz], arow_b, arow[:, t0:t0 + tsz], arow_b)
        r_act(ctx, arow[:, t0:t0 + tsz], arow_b, arow[:, t0:t0 + tsz], arow_b, AF.Sqrt)

    if DEBUG:
        tap(ctx, "arow", arow[:, 0:4096], arow_b, (1, 4096))

    new_phase(ctx)
    wsl0 = ctx.tensors["wsl0"]; wsl0b = ctx.bufs["wsl0"]
    wsl1 = ctx.tensors["wsl1"]; wsl1b = ctx.bufs["wsl1"]
    for k in range(NC):
        ctx.emit("sp", lambda e, k=k: e.dma_start(
            wsl0[:, k * 1024:(k + 1) * 1024], wbf["Wo"][0][0:D, k * 128:(k + 1) * 128],
            transpose=True), r=[wbf["Wo"][1]], w=[wsl0b], dma=True, tag="l1_woT")
        ctx.emit("sp", lambda e, k=k: e.dma_start(
            wsl1[:, k * 1024:(k + 1) * 1024], Mbf[k * 128:(k + 1) * 128, 0:D]),
            r=[Mbf_b], w=[wsl1b], dma=True, tag="l1_mld")
    orow, orb = ones_row(ctx, 128)
    pcyc = [("pf0", "pf1"), ]
    normo_b = ctx.buf("v_l1_normo")
    xres_b = ctx.buf("v_l1_xres")
    tmp_b = ctx.buf("v_l1_tmp2")
    out_b = ctx.buf("v_l1_outt")
    for ti in range(NTT):
        t0 = ti * TT
        bcast_cols(ctx, "pf2", [((orow, orb), (arow[:, t0:t0 + TT], arow_b))], nsz=TT)
        for c in range(NC):
            xa = af[:, c * TT:(c + 1) * TT]
            ctx.load(xa, sc32[c * 128:(c + 1) * 128, t0:t0 + TT], ctx.buf("v_l1_oin"),
                     rbufs=[sc32_b], tag="l1_tl_ld")
            ctx.emit("dve", lambda e, xa=xa, c=c: e.tensor_tensor(
                ab[:, c * TT:(c + 1) * TT], xa, pf2[0:128, 0:TT], ALU.mult),
                r=[ctx.buf("v_l1_oin"), pf2b], w=[normo_b], tag="l1_tl_norm")
        for m in range(NC):
            psn = ("pf0", "pf1")[m % 2]
            ps_t = ctx.tensors[psn]; ps_b = ctx.bufs[psn]
            def mmg(e, m=m, t0=t0, ps_t=ps_t):
                inst = None
                for k in range(NC):
                    inst = e.matmul(ps_t[0:128, 0:TT],
                                    wsl0[:, k * 1024 + m * 128: k * 1024 + (m + 1) * 128],
                                    ab[:, k * TT:(k + 1) * TT],
                                    start=(k == 0), stop=False)
                for k in range(NC):
                    inst = e.matmul(ps_t[0:128, 0:TT],
                                    wsl1[:, k * 1024 + m * 128: k * 1024 + (m + 1) * 128],
                                    xnt[:, k * T + t0: k * T + t0 + TT],
                                    start=False, stop=(k == NC - 1))
                return inst
            ctx.emit("pe", mmg, r=[wsl0b, wsl1b, normo_b, xnb], w=[ps_b], tag="l1_tl_mm")
            xr = af[:, NC * TT: NC * TT + TT]
            ctx.load(xr, xh[m * 128:(m + 1) * 128, t0:t0 + TT], xres_b, rbufs=[xhb], tag="l1_tl_x")
            tmp = af[:, NC * TT + TT: NC * TT + 2 * TT]
            ctx.emit("act", lambda e, tmp=tmp, ps_t=ps_t, m=m: e.activation(
                tmp, ps_t[0:128, 0:TT], AF.Identity, bias=smalls[:, 280 + m:281 + m]),
                r=[ps_b, bias_b], w=[tmp_b], tag="l1_tl_bias")
            outt = af[:, NC * TT + 2 * TT: NC * TT + 3 * TT]
            ctx.emit("dve", lambda e, outt=outt, tmp=tmp, xr=xr: e.tensor_tensor(
                outt, tmp, xr, ALU.add), r=[tmp_b, xres_b], w=[out_b], tag="l1_tl_add")
            ctx.store(xh2[m * 128:(m + 1) * 128, t0:t0 + TT], outt, out_b,
                      wbufs=[xh2b], q="act", tag="l1_tl_st")



def finish(ctx):
    deps = ctx.p.fence_deps()
    ctx.emit("sp", lambda e: e.nop(), deps=deps, tag="final_fence")
    ctx.p.finalize()


def build_l1_program():
    nc = bass.Bass("TRN2", target_bir_lowering=False)
    with ExitStack() as stk:
        ctx = Ctx(nc, stk)
        xin = ctx.dram("x_in", (T, D), F32, kind="ExternalInput")
        yout = ctx.dram("y_out", (T, D), F32, kind="ExternalOutput")
        H = {}
        for nm, shape in [("Wr", (D, D)), ("Wk", (D, D)), ("Wv", (D, D)), ("Ww", (D, D)),
                          ("Wo", (D, D)), ("Wfr", (D, D)), ("Wfi", (D, D)), ("bias", (D, 1))]:
            h = ctx.dram("i_" + nm, shape, F32, kind="ExternalInput")
            H[nm] = (h, ctx.buf("i_" + nm))
        xh = ctx.dram("xh", (D, T), F32); xhb = ctx.buf("xh")
        xh2 = ctx.dram("xh2", (D, T), F32); xh2b = ctx.buf("xh2")
        alloc_standard(ctx)
        emit_load_smalls(ctx)
        emit_transpose_in(ctx, xin, ctx.buf("x_in"), xh, xhb)
        new_phase(ctx)
        emit_layer1(ctx, H, xh, xhb, xh2, xh2b)
        new_phase(ctx)
        emit_transpose_out(ctx, xh2, xh2b, yout, ctx.buf("y_out"))
        finish(ctx)
    return nc


DEBUG = False

def tap(ctx, name, src_ap, src_buf, shape):
    if not DEBUG:
        return
    h = ctx.dram("tap_" + name, shape, F32, kind="ExternalOutput")
    ctx.store(h[0:shape[0], 0:shape[1]], src_ap, src_buf, wbufs=[ctx.buf("tap_" + name)],
              q="pool", tag="tap_" + name)



def l2_mask_consts():
    NEG = -1e30
    out = []
    for qs in range(2):
        i = (np.arange(128) + qs * 128)[:, None]
        cc = np.arange(256)[None, :]
        m = np.where(cc < 256 - i, NEG, 0.0)
        out.append(m.astype(ml_dtypes.bfloat16))
    for qs in range(2):
        i = (np.arange(128) + qs * 128)[:, None]
        jb = np.arange(256)[None, :]
        m = np.where(jb >= 255 - i, NEG, 0.0)
        out.append(m.astype(ml_dtypes.bfloat16))
    return np.stack(out)


def emit_ternary_prep(ctx, tag, src, src_b, rows_n, cols_n, outdram, outdram_b):
    af = ctx.tensors["af"]
    ab = ctx.tensors["ab"]
    smalls = ctx.tensors["smalls"]
    oc, ocb = ones_col(ctx)
    orow, orb = ones_row(ctx, 128)
    pf4 = ctx.tensors["pf4"]; pf4b = ctx.bufs["pf4"]
    wt_b = ctx.buf("v_tp_w")
    acc_col = af[:, 0:1]
    acc_b = ctx.buf("v_tp_acc")
    red_col = af[:, 1:2]
    red_b = ctx.buf("v_tp_red")
    thr_col = smalls[:, 300:301]
    thr_b = ctx.buf("v_tp_thr")
    trow = ctx.tensors["rows"][0:1, 4096:4097]
    trow_b = ctx.buf("v_tp_trow")
    first = True
    for r0 in range(0, rows_n, 128):
        for c0 in range(0, cols_n, 1024):
            csz = min(1024, cols_n - c0)
            wt = af[:, 2048:2048 + csz]
            ctx.load(wt[:, 0:csz], src[r0:r0 + 128, c0:c0 + csz], wt_b, rbufs=[src_b], tag=tag + "_p1ld")
            ctx.emit("dve", lambda e, wt=wt, csz=csz: e.reduce_sum(
                red_col, wt[:, 0:csz], axis=AX.X, apply_absolute_value=True),
                r=[wt_b], w=[red_b], tag=tag + "_red")
            if first:
                ctx.emit("dve", lambda e: e.tensor_copy(acc_col, red_col),
                         r=[red_b], w=[acc_b], tag=tag + "_acc0")
                first = False
            else:
                ctx.emit("dve", lambda e: e.tensor_add(acc_col, acc_col, red_col),
                         r=[acc_b, red_b], w=[acc_b], tag=tag + "_acc")
    ctx.emit("pe", lambda e: e.matmul(pf4[0:1, 0:1], oc, acc_col, start=True, stop=True),
             r=[acc_b, ocb], w=[pf4b], tag=tag + "_tot")
    r_ts(ctx, trow, trow_b, pf4[0:1, 0:1], pf4b, 0.5 / (rows_n * cols_n), ALU.mult)
    ctx.emit("pe", lambda e: e.matmul(pf4[0:128, 0:1], orow, trow, start=True, stop=True),
             r=[trow_b, orb], w=[pf4b], tag=tag + "_thrB")
    ctx.emit("dve", lambda e: e.tensor_copy(thr_col, pf4[0:128, 0:1]),
             r=[pf4b], w=[thr_b], tag=tag + "_thrc")
    si = [0]
    for r0 in range(0, rows_n, 128):
        for c0 in range(0, cols_n, 1024):
            csz = min(1024, cols_n - c0)
            wt = af[:, 2048:2048 + csz]
            u = af[:, 4096:4096 + csz]
            u_b = ctx.buf("v_tp_u")
            sg = af[:, 6144:6144 + csz]
            sg_b = ctx.buf("v_tp_sg")
            st = ab[:, (si[0] % 2) * 1024: (si[0] % 2) * 1024 + csz]
            st_b = ctx.buf("v_tp_st" + str(si[0] % 2))
            si[0] += 1
            ctx.load(wt[:, 0:csz], src[r0:r0 + 128, c0:c0 + csz], wt_b, rbufs=[src_b], tag=tag + "_p2ld")
            ctx.emit("act", lambda e, wt=wt, u=u, csz=csz: e.activation(
                u[:, 0:csz], wt[:, 0:csz], AF.Abs),
                r=[wt_b], w=[u_b], tag=tag + "_uabs")
            ctx.emit("dve", lambda e, u=u, csz=csz: e.tensor_single_scalar(
                u[:, 0:csz], u[:, 0:csz], thr_col[:, 0:1], ALU.is_gt),
                r=[u_b, thr_b], w=[u_b], tag=tag + "_u")
            ctx.emit("act", lambda e, wt=wt, sg=sg, csz=csz: e.activation(
                sg[:, 0:csz], wt[:, 0:csz], AF.Sign),
                r=[wt_b], w=[sg_b], tag=tag + "_sg")
            ctx.emit("dve", lambda e, u=u, sg=sg, st=st, csz=csz: e.tensor_tensor(
                st[:, 0:csz], u[:, 0:csz], sg[:, 0:csz], ALU.mult),
                r=[u_b, sg_b], w=[st_b], tag=tag + "_wq")
            ctx.store(outdram[r0:r0 + 128, c0:c0 + csz], st[:, 0:csz], st_b,
                      wbufs=[outdram_b], q="act", tag=tag + "_st")


def emit_rms_norm_spill(ctx, tag, n_tok, dst_dram, dst_b, eps=1e-8):
    af = ctx.tensors["af"]
    ab = ctx.tensors["ab"]
    rows = ctx.tensors["rows"]
    xnt = ctx.tensors["xn"]; xnb = ctx.bufs["xn"]
    oc, ocb = ones_col(ctx)
    orow, orb = ones_row(ctx, 128)
    pf4 = ctx.tensors["pf4"]; pf4b = ctx.bufs["pf4"]
    pf2 = ctx.tensors["pf2"]; pf2b = ctx.bufs["pf2"]
    sq_b = ctx.buf("v_rn_sq")
    r_r = rows[0:1, 4200:4200 + 512]
    r_rb = ctx.buf("v_rn_r")
    st_b = [ctx.buf("v_rn_st0"), ctx.buf("v_rn_st1")]
    for ti, (t0, tsz) in enumerate(chunks_of(n_tok, TT)):
        for c in range(NC):
            sq = af[:, 0:TT]
            ctx.emit("act", lambda e, c=c, t0=t0, tsz=tsz, sq=sq: e.activation(
                sq[:, 0:tsz], xnt[:, c * T + t0: c * T + t0 + tsz], AF.Square),
                r=[xnb], w=[sq_b], tag=tag + "_sq")
            ctx.emit("pe", lambda e, sq=sq, c=c, tsz=tsz: e.matmul(
                pf4[0:1, 0:tsz], oc, sq[:, 0:tsz], start=(c == 0), stop=(c == NC - 1)),
                r=[sq_b, ocb], w=[pf4b], tag=tag + "_mm")
        r_act(ctx, r_r[:, 0:tsz], r_rb, pf4[0:1, 0:tsz], ctx.bufs["pf4"], AF.Sqrt)
        r_ts(ctx, r_r[:, 0:tsz], r_rb, r_r[:, 0:tsz], r_rb, 1.0 / math.sqrt(D), ALU.mult, eps, ALU.add)
        r_recip(ctx, r_r[:, 0:tsz], r_rb, r_r[:, 0:tsz], r_rb)
        bcast_cols(ctx, "pf2", [((orow, orb), (r_r[:, 0:tsz], r_rb))], nsz=tsz)
        for c in range(NC):
            st = ab[:, (c % 2) * TT: (c % 2) * TT + TT]
            ctx.emit("dve", lambda e, c=c, t0=t0, tsz=tsz, st=st: e.tensor_tensor(
                st[:, 0:tsz], xnt[:, c * T + t0:c * T + t0 + tsz], pf2[0:128, 0:tsz], ALU.mult),
                r=[xnb, pf2b], w=[st_b[c % 2]], tag=tag + "_mul")
            ctx.store(dst_dram[c * 128:(c + 1) * 128, t0:t0 + tsz], st[:, 0:tsz],
                      st_b[c % 2], wbufs=[dst_b], q="act", tag=tag + "_st")


def emit_layer2(ctx, H, xh, xhb, xh2, xh2b):
    af = ctx.tensors["af"]
    ab = ctx.tensors["ab"]
    rows = ctx.tensors["rows"]
    smalls = ctx.tensors["smalls"]
    xnt = ctx.tensors["xn"]; xnb = ctx.bufs["xn"]
    oc, ocb = ones_col(ctx)
    orow, orb = ones_row(ctx, 128)
    DFF = 4096
    W_ = 256

    wbf = {}
    for nm in ("Wq", "Wk", "Wv"):
        wbf[nm] = cast_weight(ctx, "l2" + nm, H[nm][0], H[nm][1], D, D)
    wqu = ctx.dram("l2_wqu", (DFF, D), BF16); wqu_b = ctx.buf("l2_wqu")
    wqd = ctx.dram("l2_wqd", (D, DFF), BF16); wqd_b = ctx.buf("l2_wqd")
    emit_ternary_prep(ctx, "l2tu", H["w_up"][0], H["w_up"][1], DFF, D, wqu, wqu_b)
    emit_ternary_prep(ctx, "l2td", H["w_down"][0], H["w_down"][1], D, DFF, wqd, wqd_b)

    bias_b = ctx.buf("v_l2_bias")
    for c in range(NC):
        ctx.load(smalls[:, 280 + c:281 + c], H["bq"][0][c * 128:(c + 1) * 128, 0:1],
                 bias_b, rbufs=[H["bq"][1]], tag="l2_bq")
        ctx.load(smalls[:, 288 + c:289 + c], H["bk"][0][c * 128:(c + 1) * 128, 0:1],
                 bias_b, rbufs=[H["bk"][1]], tag="l2_bk")
    ctx.emit("dve", lambda e: e.tensor_scalar_mul(smalls[:, 280:288], smalls[:, 280:288], 0.125),
             r=[bias_b], w=[bias_b], tag="l2_bqs")
    bv_row = rows[0:1, 4500:4500 + D]
    bv_b = ctx.buf("v_l2_bv")
    ctx.emit("sp", lambda e: e.dma_start(bv_row, H["bv"][0].ap().rearrange("d one -> one d")),
             r=[H["bv"][1]], w=[bv_b], dma=True, tag="l2_bv")

    mk = l2_mask_consts()
    hMK = ctx.const("l2masks", mk.reshape(4 * 128, 256))
    mk_sb = ab[:, 9216:9216 + 1024]
    mk_b = ctx.buf("v_l2_mk")

    new_phase(ctx)
    def src_x(c, t0, tsz, dst_ap, dst_b):
        ctx.load(dst_ap[:, 0:tsz], xh[c * 128:(c + 1) * 128, t0:t0 + tsz], dst_b,
                 rbufs=[xhb], tag="l2ln_ld")
    emit_ln(ctx, src_x)

    new_phase(ctx)
    qT = ctx.dram("l2_qT", (D, T), BF16); qT_b = ctx.buf("l2_qT")
    kT = ctx.dram("l2_kT", (D, T), BF16); kT_b = ctx.buf("l2_kT")
    vtok = ctx.dram("l2_v", (T, D), BF16); vtok_b = ctx.buf("l2_v")
    stg_b = [ctx.buf("v_l2_st0"), ctx.buf("v_l2_st1")]
    sj = [0]

    def mk_proj_evac(dst, dst_b, bias_base, scale):
        def ev(c2, m0, msz, n0, nsz, ps_t, ps_b):
            s = sj[0] % 2; sj[0] += 1
            st = ab[:, s * 512: s * 512 + nsz]
            mchunk = m0 // 128
            c2.emit("act", lambda e: e.activation(
                st[0:msz, 0:nsz], ps_t[0:msz, 0:nsz], AF.Identity,
                bias=smalls[:, bias_base + mchunk:bias_base + mchunk + 1], scale=scale),
                r=[ps_b, bias_b], w=[stg_b[s]], tag="l2pj_ev")
            c2.store(dst[m0:m0 + msz, n0:n0 + nsz], st[0:msz, 0:nsz], stg_b[s],
                     wbufs=[dst_b], q="act", tag="l2pj_st")
        return ev
    rhs_xn = ("sbuf", xnt, xnb, lambda k, n: (k // 128) * T + n)
    linear(ctx, tag="l2q", lhsT=("dram_mk", wbf["Wq"][0], wbf["Wq"][1]), rhs=rhs_xn,
           K=D, M=D, N=T, evac=mk_proj_evac(qT, qT_b, 280, 0.125), m_tile=1024)
    linear(ctx, tag="l2k", lhsT=("dram_mk", wbf["Wk"][0], wbf["Wk"][1]), rhs=rhs_xn,
           K=D, M=D, N=T, evac=mk_proj_evac(kT, kT_b, 288, 1.0), m_tile=1024)
    wsl0 = ctx.tensors["wsl0"]; wsl0b = ctx.bufs["wsl0"]
    for k in range(NC):
        ctx.emit("sp", lambda e, k=k: e.dma_start(
            wsl0[:, k * 1024:(k + 1) * 1024], wbf["Wv"][0][0:D, k * 128:(k + 1) * 128],
            transpose=True), r=[wbf["Wv"][1]], w=[wsl0b], dma=True, tag="l2_wvT")
    pcyc = ("pf0", "pf1")
    pi = [0]
    for tc in range(T // 128):
        for ot in range(2):
            psn = pcyc[pi[0] % 2]; pi[0] += 1
            ps_t = ctx.tensors[psn]; ps_b = ctx.bufs[psn]
            def mmv(e, tc=tc, ot=ot, ps_t=ps_t):
                inst = None
                for k in range(NC):
                    inst = e.matmul(ps_t[0:128, 0:512],
                                    xnt[:, k * T + tc * 128: k * T + (tc + 1) * 128],
                                    wsl0[:, k * 1024 + ot * 512: k * 1024 + ot * 512 + 512],
                                    start=(k == 0), stop=False)
                inst = e.matmul(ps_t[0:128, 0:512], orow,
                                bv_row[:, ot * 512: ot * 512 + 512], start=False, stop=True)
                return inst
            ctx.emit("pe", mmv, r=[xnb, wsl0b, orb, bv_b], w=[ps_b], tag="l2v_mm")
            s = sj[0] % 2; sj[0] += 1
            st = ab[:, s * 512: s * 512 + 512]
            ctx.emit("act", lambda e, st=st, ps_t=ps_t: e.copy(st, ps_t[0:128, 0:512]),
                     r=[ps_b], w=[stg_b[s]], tag="l2v_ev")
            ctx.store(vtok[tc * 128:(tc + 1) * 128, ot * 512:ot * 512 + 512], st,
                      stg_b[s], wbufs=[vtok_b], q="act", tag="l2v_st")

    new_phase(ctx)
    for vv_ in range(4):
        ctx.load(mk_sb[0:128, vv_ * 256:(vv_ + 1) * 256], hMK[vv_ * 128:(vv_ + 1) * 128, :],
                 mk_b, tag="l2_mkld")
    y2 = ctx.dram("l2_att", (D, T), F32); y2_b = ctx.buf("l2_att")
    AB_V = 0
    AB_K = 3072
    AB_Q = 6144
    AB_E = 7168
    AB_PT = 7936
    v_b = ctx.buf("v_l2a_v")
    k_b = ctx.buf("v_l2a_k")
    q_b = ctx.buf("v_l2a_q")
    e_b = ctx.buf("v_l2a_e")
    pt_b = ctx.buf("v_l2a_pt")
    o_b = ctx.buf("v_l2a_o")
    col_b = ctx.buf("v_l2a_cols")
    identb = ctx.tensors["identb"]; identb_b = ctx.bufs["identb"]
    pf2 = ctx.tensors["pf2"]; pf2b = ctx.bufs["pf2"]
    pf3 = ctx.tensors["pf3"]; pf3b = ctx.bufs["pf3"]
    pb = ctx.tensors["pb"]; pb_b = ctx.bufs["pb"]

    NCH = T // W_
    for c in range(NCH):
        kv0 = max(0, (c - 1) * W_)
        kv1 = min(T, (c + 2) * W_)
        for g in range(2):
            for tc6 in range((kv1 - kv0) // 128):
                ctx.load(ab[:, AB_V + tc6 * 512: AB_V + (tc6 + 1) * 512],
                         vtok[kv0 + tc6 * 128: kv0 + (tc6 + 1) * 128, g * 512:(g + 1) * 512],
                         v_b, rbufs=[vtok_b], tag="l2a_vld")
            for cc in range(4):
                ch = g * 4 + cc
                ctx.load(ab[:, AB_K + cc * 768: AB_K + cc * 768 + (kv1 - kv0)],
                         kT[ch * 128:(ch + 1) * 128, kv0:kv1], k_b, rbufs=[kT_b], tag="l2a_kld")
                ctx.load(ab[:, AB_Q + cc * 256: AB_Q + (cc + 1) * 256],
                         qT[ch * 128:(ch + 1) * 128, c * W_:(c + 1) * W_], q_b,
                         rbufs=[qT_b], tag="l2a_qld")
            for hh in range(8):
                h = g * 8 + hh
                cc = hh // 2
                hrow = (hh % 2) * 64
                for qs in range(2):
                    qap = ab[hrow:hrow + 64, AB_Q + cc * 256 + qs * 128: AB_Q + cc * 256 + qs * 128 + 128]
                    def kcol(gk, cc=cc, kv0=kv0):
                        return AB_K + cc * 768 + (gk - kv0)
                    SA0 = (c - 1) * W_
                    def mmSA(e, qap=qap, cc=cc, c=c, kcol=kcol, SA0=SA0, hrow=hrow):
                        inst = None
                        if c == 0:
                            inst = e.matmul(pf2[0:128, 256:512],
                                            qap, ab[hrow:hrow + 64, kcol(0):kcol(0) + 256],
                                            start=True, stop=True)
                        else:
                            inst = e.matmul(pf2[0:128, 0:512],
                                            qap, ab[hrow:hrow + 64, kcol(SA0):kcol(SA0) + 512],
                                            start=True, stop=True)
                        return inst
                    ctx.emit("pe", mmSA, r=[q_b, k_b], w=[pf2b], tag="l2a_mmA")
                    if c == 0:
                        ctx.emit("dve", lambda e, qs=qs: e.tensor_copy(
                            pf2[0:128, 0:256], mk_sb[0:128, qs * 256:(qs + 1) * 256]),
                            r=[mk_b], w=[pf2b], tag="l2a_mk0")
                    if c == NCH - 1:
                        ctx.emit("dve", lambda e, qs=qs: e.tensor_copy(
                            pf3[0:128, 0:256], mk_sb[0:128, 512 + qs * 256: 512 + (qs + 1) * 256]),
                            r=[mk_b], w=[pf3b], tag="l2a_mk15")
                    else:
                        NB0 = (c + 1) * W_
                        ctx.emit("pe", lambda e, qap=qap, kcol=kcol, NB0=NB0, hrow=hrow: e.matmul(
                            pf3[0:128, 0:256], qap, ab[hrow:hrow + 64, kcol(NB0):kcol(NB0) + 256],
                            start=True, stop=True), r=[q_b, k_b], w=[pf3b], tag="l2a_mmB")
                    c_m1 = af[:, 0:1]; c_m2 = af[:, 1:2]; c_nm = af[:, 2:3]
                    c_d1 = af[:, 3:4]; c_d2 = af[:, 4:5]; c_rd = af[:, 5:6]
                    ctx.emit("dve", lambda e: e.reduce_max(c_m1, pf2[0:128, 0:512], axis=AX.X, negate=True),
                             r=[pf2b], w=[col_b], tag="l2a_r1")
                    ctx.emit("dve", lambda e: e.reduce_max(c_m2, pf3[0:128, 0:256], axis=AX.X, negate=True),
                             r=[pf3b], w=[col_b], tag="l2a_r2")
                    ctx.emit("dve", lambda e: e.tensor_tensor(c_nm, c_m1, c_m2, ALU.min),
                             r=[col_b], w=[col_b], tag="l2a_nm")
                    ctx.emit("act", lambda e: e.activation(
                        ab[:, AB_E:AB_E + 512], pf2[0:128, 0:512], AF.Exp,
                        bias=c_nm, accum_out=c_d1), r=[pf2b, col_b], w=[e_b, col_b], tag="l2a_eA")
                    ctx.emit("act", lambda e: e.activation(
                        ab[:, AB_E + 512:AB_E + 768], pf3[0:128, 0:256], AF.Exp,
                        bias=c_nm, accum_out=c_d2), r=[pf3b, col_b], w=[e_b, col_b], tag="l2a_eB")
                    ctx.emit("dve", lambda e: e.tensor_add(c_rd, c_d1, c_d2),
                             r=[col_b], w=[col_b], tag="l2a_dd")
                    ctx.emit("dve", lambda e: e.reciprocal(c_rd, c_rd),
                             r=[col_b], w=[col_b], tag="l2a_rec")
                    ctx.emit("dve", lambda e: e.tensor_scalar_mul(
                        ab[:, AB_E:AB_E + 768], ab[:, AB_E:AB_E + 768], c_rd[:, 0:1]),
                        r=[e_b, col_b], w=[e_b], tag="l2a_pn")
                    for b6 in range(6):
                        ctx.emit("pe", lambda e, b6=b6: e.transpose(
                            pb[0:128, 0:128], ab[:, AB_E + b6 * 128: AB_E + (b6 + 1) * 128],
                            identb[:]), r=[e_b, identb_b], w=[pb_b], tag="l2a_tr")
                        eng = "dve" if b6 % 2 == 0 else "act"
                        if eng == "dve":
                            ctx.emit("dve", lambda e, b6=b6: e.tensor_copy(
                                ab[:, AB_PT + b6 * 128: AB_PT + (b6 + 1) * 128], pb[0:128, 0:128]),
                                r=[pb_b], w=[pt_b], tag="l2a_trev")
                        else:
                            ctx.emit("act", lambda e, b6=b6: e.copy(
                                ab[:, AB_PT + b6 * 128: AB_PT + (b6 + 1) * 128], pb[0:128, 0:128]),
                                r=[pb_b], w=[pt_b], tag="l2a_trev")
                    def mmav(e, c=c, hh=hh, kv0=kv0, kv1=kv1):
                        inst = None
                        first = True
                        for tc6 in range(6):
                            gk = (c - 1) * W_ + tc6 * 128
                            if gk < 0 or gk >= T:
                                continue
                            vslot = (gk - kv0) // 128
                            inst = e.matmul(
                                pf2[0:64, 0:128],
                                ab[:, AB_V + vslot * 512 + (hh % 8) * 64: AB_V + vslot * 512 + (hh % 8) * 64 + 64],
                                ab[:, AB_PT + tc6 * 128: AB_PT + (tc6 + 1) * 128],
                                start=first, stop=(tc6 == 5 or gk + 128 >= min(kv1, (c + 2) * W_)))
                            first = False
                        return inst
                    ctx.emit("pe", mmav, r=[v_b, pt_b], w=[pf2b], tag="l2a_av")
                    ost = af[:, 8 + (sj[0] % 2) * 128: 8 + (sj[0] % 2) * 128 + 128]
                    osb = stg_b[sj[0] % 2]; sj[0] += 1
                    ctx.emit("act", lambda e, ost=ost: e.copy(ost[0:64, 0:128], pf2[0:64, 0:128]),
                             r=[pf2b], w=[osb], tag="l2a_oev")
                    ctx.store(y2[h * 64:(h + 1) * 64, c * W_ + qs * 128: c * W_ + qs * 128 + 128],
                              ost[0:64, 0:128], osb, wbufs=[y2_b], q="act", tag="l2a_ost")

    new_phase(ctx)
    xnorm = ctx.dram("l2_xnorm", (D, T), BF16); xnorm_b = ctx.buf("l2_xnorm")
    emit_rms_norm_spill(ctx, "l2rn", T, xnorm, xnorm_b)
    new_phase(ctx)
    hbuf = ctx.dram("l2_h", (DFF, T), BF16); hbuf_b = ctx.buf("l2_h")
    ms2_b = ctx.buf("v_l2_ms2")
    pf4 = ctx.tensors["pf4"]; pf4b = ctx.bufs["pf4"]
    first_ms = [True] * NTT
    sq2_b = ctx.buf("v_l2_sq2")

    def evac_up(c2, m0, msz, n0, nsz, ps_t, ps_b):
        s = sj[0] % 2; sj[0] += 1
        st = ab[:, s * 512: s * 512 + nsz]
        c2.emit("act", lambda e: e.activation(st[0:msz, 0:nsz], ps_t[0:msz, 0:nsz], AF.Gelu),
                r=[ps_b], w=[stg_b[s]], tag="l2up_ev")
        c2.store(hbuf[m0:m0 + msz, n0:n0 + nsz], st[0:msz, 0:nsz], stg_b[s],
                 wbufs=[hbuf_b], q="act", tag="l2up_st")
        sq = af[:, 1024:1024 + nsz]
        c2.emit("act", lambda e: e.activation(sq[0:msz, 0:nsz], st[0:msz, 0:nsz], AF.Square),
                r=[stg_b[s]], w=[sq2_b], tag="l2up_sq")
        c2.emit("pe", lambda e: e.matmul(pf4[0:1, 0:nsz], oc[0:msz, 0:1], sq[0:msz, 0:nsz],
                                         start=True, stop=True),
                r=[sq2_b, ocb], w=[pf4b], tag="l2up_msmm")
        ti = n0 // TT
        if first_ms[ti]:
            first_ms[ti] = False
            c2.emit("dve", lambda e: e.tensor_copy(rows[0:1, n0:n0 + nsz], pf4[0:1, 0:nsz]),
                    r=[pf4b], w=[ms2_b], tag="l2up_ms0")
        else:
            c2.emit("dve", lambda e: e.tensor_tensor(rows[0:1, n0:n0 + nsz],
                                                     rows[0:1, n0:n0 + nsz], pf4[0:1, 0:nsz], ALU.add),
                    r=[ms2_b, pf4b], w=[ms2_b], tag="l2up_msa")

    linear(ctx, tag="l2up", lhsT=("dram_mk", wqu, wqu_b), rhs=("dram", xnorm, xnorm_b),
           K=D, M=DFF, N=T, evac=evac_up, m_tile=1024)

    r2_b = ctx.buf("v_l2_r2")
    for t0, tsz in chunks_of(T, TT):
        r_act(ctx, rows[0:1, t0:t0 + tsz], r2_b, rows[0:1, t0:t0 + tsz], ms2_b, AF.Sqrt)
        r_ts(ctx, rows[0:1, t0:t0 + tsz], r2_b, rows[0:1, t0:t0 + tsz], r2_b,
             1.0 / math.sqrt(DFF), ALU.mult, 1e-8, ALU.add)
        r_recip(ctx, rows[0:1, t0:t0 + tsz], r2_b, rows[0:1, t0:t0 + tsz], r2_b)

    new_phase(ctx)
    l2acc = ctx.dram("l2_acc", (D, T), F32); l2acc_b = ctx.buf("l2_acc")
    for kq in range(4):
        def evac_dn(c2, m0, msz, n0, nsz, ps_t, ps_b, kq=kq):
            s = sj[0] % 2; sj[0] += 1
            st = af[:, 1024 + s * 512: 1024 + s * 512 + nsz]
            c2.emit("act", lambda e: e.copy(st[0:msz, 0:nsz], ps_t[0:msz, 0:nsz]),
                    r=[ps_b], w=[stg_b[s]], tag="l2dn_ev")
            if kq == 0:
                c2.store(l2acc[m0:m0 + msz, n0:n0 + nsz], st[0:msz, 0:nsz], stg_b[s],
                         wbufs=[l2acc_b], q="pool", tag="l2dn_st")
            else:
                c2.emit("pool", lambda e: e.dma_start(
                    l2acc[m0:m0 + msz, n0:n0 + nsz], st[0:msz, 0:nsz], accum_op=ALU.add),
                    r=[stg_b[s]], w=[l2acc_b], dma=True, tag="l2dn_acc")
        linear(ctx, tag="l2dn%d" % kq, lhsT=("dram_mk", wqd, wqd_b), rhs=("dram", hbuf, hbuf_b),
               K=1024, M=D, N=T, evac=evac_dn, m_tile=1024, k_base=kq * 1024)

    new_phase(ctx)
    xr_b = ctx.buf("v_l2_xr")
    at_b = ctx.buf("v_l2_at")
    ac_b = ctx.buf("v_l2_ac")
    ot_b = ctx.buf("v_l2_ot")
    for ti in range(NTT):
        t0 = ti * TT
        bcast_cols(ctx, "pf2", [((orow, orb), (rows[0:1, t0:t0 + TT], r2_b))], nsz=TT)
        for m in range(NC):
            xr = af[:, 0:TT]
            at = af[:, TT:2 * TT]
            ac = af[:, 2 * TT:3 * TT]
            ot = af[:, 3 * TT:4 * TT]
            ctx.load(xr, xh[m * 128:(m + 1) * 128, t0:t0 + TT], xr_b, rbufs=[xhb], tag="l2t_x")
            ctx.load(at, y2[m * 128:(m + 1) * 128, t0:t0 + TT], at_b, rbufs=[y2_b], tag="l2t_a")
            ctx.load(ac, l2acc[m * 128:(m + 1) * 128, t0:t0 + TT], ac_b, rbufs=[l2acc_b], tag="l2t_c")
            ctx.emit("dve", lambda e, ac=ac, ot=ot: e.tensor_tensor(ot, ac, pf2[0:128, 0:TT], ALU.mult),
                     r=[ac_b, pf2b], w=[ot_b], tag="l2t_m")
            ctx.emit("dve", lambda e, ot=ot, xr=xr: e.tensor_tensor(ot, ot, xr, ALU.add),
                     r=[ot_b, xr_b], w=[ot_b], tag="l2t_a1")
            ctx.emit("dve", lambda e, ot=ot, at=at: e.tensor_tensor(ot, ot, at, ALU.add),
                     r=[ot_b, at_b], w=[ot_b], tag="l2t_a2")
            ctx.store(xh2[m * 128:(m + 1) * 128, t0:t0 + TT], ot, ot_b,
                      wbufs=[xh2b], q="act", tag="l2t_st")


def build_l2_program():
    nc = bass.Bass("TRN2", target_bir_lowering=False)
    with ExitStack() as stk:
        ctx = Ctx(nc, stk)
        xin = ctx.dram("x_in", (T, D), F32, kind="ExternalInput")
        yout = ctx.dram("y_out", (T, D), F32, kind="ExternalOutput")
        H = {}
        for nm, shape in [("Wq", (D, D)), ("bq", (D, 1)), ("Wk", (D, D)), ("bk", (D, 1)),
                          ("Wv", (D, D)), ("bv", (D, 1)), ("w_up", (4096, D)), ("w_down", (D, 4096))]:
            h = ctx.dram("i_" + nm, shape, F32, kind="ExternalInput")
            H[nm] = (h, ctx.buf("i_" + nm))
        xh = ctx.dram("xh", (D, T), F32); xhb = ctx.buf("xh")
        xh2 = ctx.dram("xh2", (D, T), F32); xh2b = ctx.buf("xh2")
        alloc_standard(ctx)
        emit_load_smalls(ctx)
        emit_transpose_in(ctx, xin, ctx.buf("x_in"), xh, xhb)
        new_phase(ctx)
        emit_layer2(ctx, H, xh, xhb, xh2, xh2b)
        new_phase(ctx)
        emit_transpose_out(ctx, xh2, xh2b, yout, ctx.buf("y_out"))
        finish(ctx)
    return nc



def l0_trig_consts():
    DHEAD = 64
    inv = 10000.0 ** (-(np.arange(DHEAD) / DHEAD))
    pos = np.arange(T)
    fr = np.outer(inv, pos)
    return (np.cos(fr).astype(np.float32), np.sin(fr).astype(np.float32),
            (np.cos(fr) / 8.0).astype(np.float32), (np.sin(fr) / 8.0).astype(np.float32))


def emit_softplus(ctx, out_ap, out_b, in_ap, in_b, tmp_ap, tmp_b, n):
    ctx.emit("act", lambda e: e.activation(tmp_ap, in_ap, AF.Abs), r=[in_b], w=[tmp_b], tag="sp_abs")
    ctx.emit("act", lambda e: e.activation(tmp_ap, tmp_ap, AF.Exp, scale=-1.0), r=[tmp_b], w=[tmp_b], tag="sp_exp")
    onec, onecb = ones_col(ctx)
    P = in_ap.shape[0]
    ctx.emit("act", lambda e: e.activation(tmp_ap, tmp_ap, AF.Ln, bias=onec[0:P, 0:1]),
             r=[tmp_b, onecb], w=[tmp_b], tag="sp_ln")
    ctx.emit("dve", lambda e: e.tensor_relu(out_ap, in_ap), r=[in_b], w=[out_b], tag="sp_relu")
    ctx.emit("dve", lambda e: e.tensor_add(out_ap, out_ap, tmp_ap), r=[out_b, tmp_b], w=[out_b], tag="sp_add")


def emit_layer0(ctx, H, xh, xhb, xh2, xh2b):
    af = ctx.tensors["af"]
    ab = ctx.tensors["ab"]
    rows = ctx.tensors["rows"]
    smalls = ctx.tensors["smalls"]
    xnt = ctx.tensors["xn"]; xnb = ctx.bufs["xn"]
    oc, ocb = ones_col(ctx)
    orow, orb = ones_row(ctx, 128)
    identf = ctx.tensors["identf"]; identf_b = ctx.bufs["identf"]
    pf0 = ctx.tensors["pf0"]; pf0b = ctx.bufs["pf0"]
    pf1 = ctx.tensors["pf1"]; pf1b = ctx.bufs["pf1"]
    pf2 = ctx.tensors["pf2"]; pf2b = ctx.bufs["pf2"]
    pf3 = ctx.tensors["pf3"]; pf3b = ctx.bufs["pf3"]
    pf4 = ctx.tensors["pf4"]; pf4b = ctx.bufs["pf4"]
    pf5 = ctx.tensors["pf5"]; pf5b = ctx.bufs["pf5"]
    NH, DH = 16, 64
    E, FF = 8, 2048

    wbf = {}
    for nm in ("wq", "wk", "wv", "wo"):
        wbf[nm] = cast_weight(ctx, "l0" + nm, H[nm][0], H[nm][1], D, D)
    wbf["router"] = cast_weight(ctx, "l0rt", H["router_w"][0], H["router_w"][1], E, D)
    wbf["wg"] = cast_weight(ctx, "l0wg", H["w_gate"][0], H["w_gate"][1], E * FF, D)
    wbf["wu"] = cast_weight(ctx, "l0wu", H["w_up"][0], H["w_up"][1], E * FF, D)
    wbf["wd"] = cast_weight(ctx, "l0wd", H["w_down"][0], H["w_down"][1], E * D, FF)

    cosf_np, sinf_np, qcos_np, qsin_np = l0_trig_consts()
    hCF = ctx.const("l0cosf", cosf_np)
    hSF = ctx.const("l0sinf", sinf_np)
    hQC = ctx.const("l0qcos", qcos_np)
    hQS = ctx.const("l0qsin", qsin_np)

    new_phase(ctx)
    def src_x(c, t0, tsz, dst_ap, dst_b):
        ctx.load(dst_ap[:, 0:tsz], xh[c * 128:(c + 1) * 128, t0:t0 + tsz], dst_b,
                 rbufs=[xhb], tag="l0ln_ld")
    emit_ln(ctx, src_x)

    new_phase(ctx)
    qT = ctx.dram("l0_qT", (D, T), BF16); qT_b = ctx.buf("l0_qT")
    kT = ctx.dram("l0_kT", (D, T), BF16); kT_b = ctx.buf("l0_kT")
    vtok = ctx.dram("l0_v", (T, D), BF16); vtok_b = ctx.buf("l0_v")
    stg_b = [ctx.buf("v_l0_st0"), ctx.buf("v_l0_st1")]
    sj = [0]

    def mk_spill_evac(dst, dst_b):
        def ev(c2, m0, msz, n0, nsz, ps_t, ps_b):
            s = sj[0] % 2; sj[0] += 1
            st = ab[:, s * 512: s * 512 + nsz]
            c2.emit("act", lambda e: e.copy(st[0:msz, 0:nsz], ps_t[0:msz, 0:nsz]),
                    r=[ps_b], w=[stg_b[s]], tag="l0pj_ev")
            c2.store(dst[m0:m0 + msz, n0:n0 + nsz], st[0:msz, 0:nsz], stg_b[s],
                     wbufs=[dst_b], q="act", tag="l0pj_st")
        return ev
    rhs_xn = ("sbuf", xnt, xnb, lambda k, n: (k // 128) * T + n)
    linear(ctx, tag="l0q", lhsT=("dram_mk", wbf["wq"][0], wbf["wq"][1]), rhs=rhs_xn,
           K=D, M=D, N=T, evac=mk_spill_evac(qT, qT_b), m_tile=1024)
    linear(ctx, tag="l0k", lhsT=("dram_mk", wbf["wk"][0], wbf["wk"][1]), rhs=rhs_xn,
           K=D, M=D, N=T, evac=mk_spill_evac(kT, kT_b), m_tile=1024)
    wsl0 = ctx.tensors["wsl0"]; wsl0b = ctx.bufs["wsl0"]
    wsl1 = ctx.tensors["wsl1"]; wsl1b = ctx.bufs["wsl1"]
    for k in range(NC):
        ctx.emit("sp", lambda e, k=k: e.dma_start(
            wsl0[:, k * 1024:(k + 1) * 1024], wbf["wv"][0][0:D, k * 128:(k + 1) * 128],
            transpose=True), r=[wbf["wv"][1]], w=[wsl0b], dma=True, tag="l0_wvT")
    pi = [0]
    for tc in range(T // 128):
        for ot in range(2):
            psn = ("pf0", "pf1")[pi[0] % 2]; pi[0] += 1
            ps_t = ctx.tensors[psn]; ps_b = ctx.bufs[psn]
            def mmv(e, tc=tc, ot=ot, ps_t=ps_t):
                inst = None
                for k in range(NC):
                    inst = e.matmul(ps_t[0:128, 0:512],
                                    xnt[:, k * T + tc * 128: k * T + (tc + 1) * 128],
                                    wsl0[:, k * 1024 + ot * 512: k * 1024 + ot * 512 + 512],
                                    start=(k == 0), stop=(k == NC - 1))
                return inst
            ctx.emit("pe", mmv, r=[xnb, wsl0b], w=[ps_b], tag="l0v_mm")
            s = sj[0] % 2; sj[0] += 1
            st = ab[:, s * 512: s * 512 + 512]
            ctx.emit("act", lambda e, st=st, ps_t=ps_t: e.copy(st, ps_t[0:128, 0:512]),
                     r=[ps_b], w=[stg_b[s]], tag="l0v_ev")
            ctx.store(vtok[tc * 128:(tc + 1) * 128, ot * 512:ot * 512 + 512], st,
                      stg_b[s], wbufs=[vtok_b], q="act", tag="l0v_st")

    new_phase(ctx)
    bang = af[0:16, 9984:10048]
    bang_b = ctx.buf("v_l0_bang")
    tmpa = af[0:16, 10048:10112]
    tmpa_b = ctx.buf("v_l0_tmpa")
    ctx.load(bang[0:16, 0:64], H["pope_bias"][0][0:16, 0:64], bang_b,
             rbufs=[H["pope_bias"][1]], tag="l0_bld")
    PI = math.pi
    ctx.emit("dve", lambda e: e.tensor_scalar(bang, bang, -2 * PI, 0.0, ALU.max, ALU.min),
             r=[bang_b], w=[bang_b], tag="l0_bclip")
    for which, bias_add, dstc in (("cb", PI / 2, 304), ("sb", 0.0, 320)):
        ctx.emit("dve", lambda e, bias_add=bias_add: e.tensor_single_scalar(
            tmpa, bang, bias_add, ALU.add), r=[bang_b], w=[tmpa_b], tag="l0_bu")
        msk = af[0:16, 10112:10176]
        msk_b = ctx.buf("v_l0_msk")
        ctx.emit("dve", lambda e: e.tensor_single_scalar(msk, tmpa, -PI, ALU.is_lt),
                 r=[tmpa_b], w=[msk_b], tag="l0_bm")
        ctx.emit("dve", lambda e: e.scalar_tensor_tensor(tmpa, msk, 2 * PI, tmpa, ALU.mult, ALU.add),
                 r=[msk_b, tmpa_b], w=[tmpa_b], tag="l0_bw")
        ctx.emit("act", lambda e: e.activation(tmpa, tmpa, AF.Sin), r=[tmpa_b], w=[tmpa_b], tag="l0_bsin")
        ctx.emit("pe", lambda e: e.transpose(pf5[0:64, 0:16], tmpa, identf[0:16, 0:16]),
                 r=[tmpa_b, identf_b], w=[pf5b], tag="l0_btr")
        ctx.emit("dve", lambda e, dstc=dstc: e.tensor_copy(smalls[0:64, dstc:dstc + 16], pf5[0:64, 0:16]),
                 r=[pf5b], w=[ctx.buf("v_l0_cbsb")], tag="l0_bev")

    ktab_b = ctx.buf("v_l0_ktab")
    ctx.load(af[0:64, 0:4096], hCF[0:64, 0:T], ktab_b, tag="l0_ktld")
    ctx.load(af[0:64, 4096:8192], hSF[0:64, 0:T], ktab_b, tag="l0_stld")

    att = ctx.dram("l0_att", (D, T), BF16); att_b = ctx.buf("l0_att")
    AB_KE = 0
    AB_VP = 4096
    AB_QE = 6208
    AB_E = 6720
    AB_KS = 7808
    km_b = ctx.buf("v_l0_km")
    ke_b = ctx.buf("v_l0_ke")
    qe_b = ctx.buf("v_l0_qe")
    vp_b = ctx.buf("v_l0_vp")
    e_b = ctx.buf("v_l0_E")
    qsl_b = ctx.buf("v_l0_qsl")
    ksl_b = ctx.buf("v_l0_ksl")
    kx_b = ctx.buf("v_l0_kx")
    oev_b = ctx.buf("v_l0_oev")
    qm_b = ctx.buf("v_l0_qm")
    sp_b = ctx.buf("v_l0_sptmp")
    w1b = ctx.buf("v_l0_w1")
    w2b = ctx.buf("v_l0_w2")
    nm_b = ctx.buf("v_l0_nm")
    VPW = 65

    scr_b = ctx.buf("v_l0_scr")
    for h in range(NH):
        ch, hrow = h // 2, (h % 2) * 64
        cbc = smalls[0:64, 304 + h:305 + h]
        sbc = smalls[0:64, 320 + h:321 + h]
        cb_b = ctx.buf("v_l0_cbsb")
        ksl = ab[0:64, AB_KS:AB_KS + 1024]
        for p4 in range(4):
            c0 = p4 * 1024
            ctx.load(ksl[0:64, 0:1024], kT[ch * 128 + hrow: ch * 128 + hrow + 64, c0:c0 + 1024],
                     ksl_b, rbufs=[kT_b], tag="l0a_kld")
            emit_softplus(ctx, rows[0:64, c0:c0 + 1024], km_b, ksl[0:64, 0:1024], ksl_b,
                          af[0:64, 10240:11264], scr_b, 1024)
        kxc = af[0:64, 11263:11264]
        ctx.emit("dve", lambda e: e.reduce_max(kxc, rows[0:64, 0:T], axis=AX.X),
                 r=[km_b], w=[kx_b], tag="l0a_kmax")
        ctx.emit("dve", lambda e: e.tensor_single_scalar(kxc, kxc, 0.125, ALU.mult),
                 r=[kx_b], w=[kx_b], tag="l0a_kmaxs")
        w1 = af[0:64, 8192:8704]
        w2 = af[0:64, 8704:9216]
        for tt in range(NTT):
            t0 = tt * TT
            for half in range(2):
                a_tab = af[0:64, (0 if half == 0 else 4096) + t0: (0 if half == 0 else 4096) + t0 + TT]
                b_tab = af[0:64, (4096 if half == 0 else 0) + t0: (4096 if half == 0 else 0) + t0 + TT]
                op = ALU.subtract if half == 0 else ALU.add
                ctx.emit("dve", lambda e, a_tab=a_tab, cbc=cbc: e.tensor_scalar_mul(w1, a_tab, cbc),
                         r=[ktab_b, cb_b, scr_b], w=[scr_b], tag="l0a_ke1")
                ctx.emit("dve", lambda e, b_tab=b_tab, sbc=sbc: e.tensor_scalar_mul(w2, b_tab, sbc),
                         r=[ktab_b, cb_b, scr_b], w=[scr_b], tag="l0a_ke2")
                ctx.emit("dve", lambda e, op=op: e.tensor_tensor(w1, w1, w2, op),
                         r=[scr_b], w=[scr_b], tag="l0a_ke3")
                dst = ab[half * 64:(half + 1) * 64, AB_KE + t0:AB_KE + t0 + TT]
                ctx.emit("dve", lambda e, dst=dst, t0=t0: e.tensor_tensor(
                    dst, w1, rows[0:64, t0:t0 + TT], ALU.mult),
                    r=[scr_b, km_b], w=[ke_b], tag="l0a_ke4")
        ctx.emit("dve", lambda e: e.memset(ab[:, AB_VP:AB_VP + 32 * VPW], 1.0),
                 r=[], w=[vp_b], tag="l0a_vones")
        for kv in range(32):
            ctx.load(ab[:, AB_VP + kv * VPW: AB_VP + kv * VPW + 64],
                     vtok[kv * 128:(kv + 1) * 128, h * 64:(h + 1) * 64],
                     vp_b, rbufs=[vtok_b], tag="l0a_vld")
        for qt in range(NTT):
            t0 = qt * TT
            qsl = ab[0:64, AB_QE:AB_QE + TT]
            ctx.load(qsl[0:64, 0:TT], qT[ch * 128 + hrow: ch * 128 + hrow + 64, t0:t0 + TT],
                     qsl_b, rbufs=[qT_b], tag="l0a_qld")
            qc_t = af[0:64, 9216:9728]
            qs_t = af[0:64, 9728:10240]
            ctx.load(qc_t, hQC[0:64, t0:t0 + TT], scr_b, tag="l0a_qcld")
            ctx.load(qs_t, hQS[0:64, t0:t0 + TT], scr_b, tag="l0a_qsld")
            qm = af[0:64, 10240:10752]
            emit_softplus(ctx, qm, scr_b, qsl[0:64, 0:TT], qsl_b, af[0:64, 10752:11264], scr_b, TT)
            ctx.emit("pe", lambda e, qm=qm: e.matmul(pf3[0:1, 0:TT], kxc, qm, start=True, stop=True),
                     r=[kx_b, scr_b], w=[pf3b], tag="l0a_negm_mm")
            negm = af[0:1, 11264:11264 + TT]
            ctx.emit("dve", lambda e, negm=negm: e.tensor_single_scalar(
                negm, pf3[0:1, 0:TT], -1.0, ALU.mult), r=[pf3b], w=[nm_b], tag="l0a_negm")
            for half in range(2):
                tab = qc_t if half == 0 else qs_t
                dst = ab[half * 64:(half + 1) * 64, AB_QE + TT:AB_QE + 2 * TT]
                ctx.emit("dve", lambda e, tab=tab, dst=dst: e.tensor_tensor(dst, qm, tab, ALU.mult),
                         r=[scr_b], w=[qe_b], tag="l0a_qe")
            eb2 = [ctx.buf("v_l0_E0"), ctx.buf("v_l0_E1")]
            for kv in range(33):
                if kv < 32:
                    psn, psb = (pf0, pf0b) if kv % 2 == 0 else (pf1, pf1b)
                    def mmS(e, kv=kv, psn=psn):
                        e.matmul(psn[0:128, 0:TT], ab[:, AB_KE + kv * 128: AB_KE + (kv + 1) * 128],
                                 ab[:, AB_QE + TT:AB_QE + 2 * TT], start=True, stop=False)
                        return e.matmul(psn[0:128, 0:TT], orow, negm, start=False, stop=True)
                    ctx.emit("pe", mmS, r=[ke_b, qe_b, orb, nm_b], w=[psb], tag="l0a_mmS")
                    eslot = kv % 2
                    ctx.emit("act", lambda e, psn=psn, eslot=eslot: e.activation(
                        ab[:, AB_E + eslot * TT:AB_E + eslot * TT + TT], psn[0:128, 0:TT], AF.Exp),
                        r=[psb], w=[eb2[eslot]], tag="l0a_exp")
                if kv >= 1:
                    kvp = kv - 1
                    eslot = kvp % 2
                    ctx.emit("pe", lambda e, kvp=kvp, eslot=eslot: e.matmul(
                        pf2[0:VPW, 0:TT], ab[:, AB_VP + kvp * VPW: AB_VP + (kvp + 1) * VPW],
                        ab[:, AB_E + eslot * TT:AB_E + eslot * TT + TT],
                        start=(kvp == 0), stop=(kvp == 31)),
                        r=[vp_b, eb2[eslot]], w=[pf2b], tag="l0a_av")
            rd = negm
            ctx.emit("dve", lambda e, rd=rd: e.reciprocal(rd, pf2[64:65, 0:TT]),
                     r=[pf2b], w=[nm_b], tag="l0a_rd")
            ctx.emit("pe", lambda e, rd=rd: e.matmul(pf3[0:64, 0:TT], orow[0:1, 0:64], rd,
                                                     start=True, stop=True),
                     r=[nm_b, orb], w=[pf3b], tag="l0a_rdB")
            ost = af[0:64, 8192:8704]
            ctx.emit("act", lambda e, ost=ost: e.copy(ost, pf2[0:64, 0:TT]),
                     r=[pf2b], w=[scr_b], tag="l0a_oev")
            stb = ab[0:64, 8832:8832 + TT]
            ctx.emit("dve", lambda e, ost=ost, stb=stb: e.tensor_tensor(
                stb, ost, pf3[0:64, 0:TT], ALU.mult), r=[scr_b, pf3b], w=[ctx.buf("v_l0_ostb")], tag="l0a_on")
            ctx.store(att[h * 64:(h + 1) * 64, t0:t0 + TT], stb, ctx.buf("v_l0_ostb"),
                      wbufs=[att_b], q="act", tag="l0a_ost")

    new_phase(ctx)
    rtT_b = ctx.buf("v_l0_rtT")
    rsl0 = ctx.tensors["rsl0"]; rsl0b = ctx.bufs["rsl0"]
    ctx.load(rsl0[0:8, 0:1024], wbf["router"][0][0:8, 0:D], rtT_b, rbufs=[wbf["router"][1]], tag="l0g_rld")
    smallsb = ctx.tensors["smallsb"]
    smb_b = ctx.buf("v_l0_smb")
    pb = ctx.tensors["pb"]; pb_b = ctx.bufs["pb"]
    identb = ctx.tensors["identb"]; identb_b = ctx.bufs["identb"]
    for c in range(NC):
        ctx.emit("pe", lambda e, c=c: e.transpose(
            pb[0:128, 0:8], rsl0[0:8, c * 128:(c + 1) * 128], identb[0:8, 0:8]),
            r=[rtT_b, identb_b], w=[pb_b], tag="l0g_rtr")
        ctx.emit("dve", lambda e, c=c: e.tensor_copy(smallsb[:, c * 8:(c + 1) * 8], pb[0:128, 0:8]),
                 r=[pb_b], w=[smb_b], tag="l0g_rev")
    selc = np.zeros((8, 8 * 128), np.float32)
    for e_ in range(8):
        selc[e_, e_ * 128:(e_ + 1) * 128] = 1.0
    hSEL = ctx.const("l0sel", selc)
    sel_b = ctx.buf("v_l0_sel")
    ctx.load(rows[0:8, 4200:4200 + 1024], hSEL[0:8, :], sel_b, tag="l0g_selld")
    dw_b = ctx.buf("v_l0_dw")
    g_b = ctx.buf("v_l0_g")
    JE = 0.01
    for tc in range(T // 128):
        def mmg(e, tc=tc):
            inst = None
            for k in range(NC):
                inst = e.matmul(pf4[0:128, 0:8],
                                xnt[:, k * T + tc * 128: k * T + (tc + 1) * 128],
                                smallsb[:, k * 8:(k + 1) * 8], start=(k == 0), stop=(k == NC - 1))
            return inst
        ctx.emit("pe", mmg, r=[xnb, smb_b], w=[pf4b], tag="l0g_mm")
        sG = af[:, 10688:10696]
        m1 = af[:, 10696:10697]
        f1 = af[:, 10697:10698]
        e1 = af[:, 10698:10706]
        o1 = af[:, 10706:10714]
        d1 = af[:, 10714:10715]
        s2 = af[:, 10715:10723]
        m2 = af[:, 10723:10724]
        e2 = af[:, 10724:10732]
        o2 = af[:, 10732:10740]
        d2 = af[:, 10740:10741]
        keep = af[:, 10741:10749]
        gg = af[:, 10749:10750]
        dw = af[:, 10750:10758]
        ctx.emit("dve", lambda e, sG=sG: e.tensor_copy(sG, pf4[0:128, 0:8]), r=[pf4b], w=[g_b], tag="l0g_cp")
        ctx.emit("dve", lambda e, sG=sG, m1=m1: e.reduce_max(m1, sG, axis=AX.X), r=[g_b], w=[g_b], tag="l0g_m1")
        ctx.emit("dve", lambda e, sG=sG, keep=keep, m1=m1: e.tensor_scalar(
            keep, sG, m1[:, 0:1], -1.0, ALU.subtract, ALU.mult), r=[g_b], w=[g_b], tag="l0g_d")
        ctx.emit("act", lambda e, sG=sG, e1=e1: e.activation(e1, sG, AF.Abs),
                 r=[g_b], w=[g_b], tag="l0g_abs")
        ctx.emit("dve", lambda e, e1=e1, m1=m1, o1=o1: e.tensor_scalar_max(o1, e1, m1[:, 0:1]),
                 r=[g_b], w=[g_b], tag="l0g_fac")
        ctx.emit("dve", lambda e, o1=o1, keep=keep: e.scalar_tensor_tensor(
            keep, o1, -2 * JE, keep, ALU.mult, ALU.add), r=[g_b], w=[g_b], tag="l0g_k1a")
        ctx.emit("dve", lambda e, keep=keep: e.tensor_single_scalar(keep, keep, 0.0, ALU.is_le),
                 r=[g_b], w=[g_b], tag="l0g_k1b")
        nm1 = af[:, 10758:10759]
        ctx.emit("dve", lambda e, m1=m1, nm1=nm1: e.tensor_single_scalar(nm1, m1, -1.0, ALU.mult),
                 r=[g_b], w=[g_b], tag="l0g_nm1")
        ctx.emit("act", lambda e, sG=sG, e1=e1, nm1=nm1: e.activation(
            e1, sG, AF.Exp, bias=nm1[:, 0:1]), r=[g_b], w=[g_b], tag="l0g_e1")
        ctx.emit("dve", lambda e, e1=e1, keep=keep: e.tensor_tensor(e1, e1, keep, ALU.mult),
                 r=[g_b], w=[g_b], tag="l0g_e1m")
        ctx.emit("dve", lambda e, e1=e1, d1=d1: e.reduce_sum(d1, e1, axis=AX.X), r=[g_b], w=[g_b], tag="l0g_d1")
        ctx.emit("dve", lambda e, d1=d1: e.reciprocal(d1, d1), r=[g_b], w=[g_b], tag="l0g_r1")
        ctx.emit("dve", lambda e, sG=sG, o1=o1, m1=m1: e.tensor_scalar(
            o1, sG, m1[:, 0:1], None, ALU.is_ge), r=[g_b], w=[g_b], tag="l0g_o1")
        ctx.emit("dve", lambda e, sG=sG, o1=o1, s2=s2: e.scalar_tensor_tensor(
            s2, o1, -1e30, sG, ALU.mult, ALU.add), r=[g_b], w=[g_b], tag="l0g_s2")
        ctx.emit("dve", lambda e, s2=s2, m2=m2: e.reduce_max(m2, s2, axis=AX.X), r=[g_b], w=[g_b], tag="l0g_m2")
        ctx.emit("dve", lambda e, sG=sG, keep=keep, m2=m2: e.tensor_scalar(
            keep, sG, m2[:, 0:1], -1.0, ALU.subtract, ALU.mult), r=[g_b], w=[g_b], tag="l0g_d2a")
        ctx.emit("act", lambda e, sG=sG, e2=e2: e.activation(e2, sG, AF.Abs),
                 r=[g_b], w=[g_b], tag="l0g_abs2")
        ctx.emit("dve", lambda e, e2=e2, m2=m2, o2=o2: e.tensor_scalar_max(o2, e2, m2[:, 0:1]),
                 r=[g_b], w=[g_b], tag="l0g_fac2")
        ctx.emit("dve", lambda e, o2=o2, keep=keep: e.scalar_tensor_tensor(
            keep, o2, -2 * JE, keep, ALU.mult, ALU.add), r=[g_b], w=[g_b], tag="l0g_k2a")
        ctx.emit("dve", lambda e, keep=keep: e.tensor_single_scalar(keep, keep, 0.0, ALU.is_le),
                 r=[g_b], w=[g_b], tag="l0g_k2b")
        nm2 = af[:, 10759:10760]
        ctx.emit("dve", lambda e, m2=m2, nm2=nm2: e.tensor_single_scalar(nm2, m2, -1.0, ALU.mult),
                 r=[g_b], w=[g_b], tag="l0g_nm2")
        ctx.emit("act", lambda e, s2=s2, e2=e2, nm2=nm2: e.activation(
            e2, s2, AF.Exp, bias=nm2[:, 0:1]), r=[g_b], w=[g_b], tag="l0g_e2")
        ctx.emit("dve", lambda e, e2=e2, keep=keep: e.tensor_tensor(e2, e2, keep, ALU.mult),
                 r=[g_b], w=[g_b], tag="l0g_e2m")
        ctx.emit("dve", lambda e, e2=e2, d2=d2: e.reduce_sum(d2, e2, axis=AX.X), r=[g_b], w=[g_b], tag="l0g_d2")
        ctx.emit("dve", lambda e, d2=d2: e.reciprocal(d2, d2), r=[g_b], w=[g_b], tag="l0g_r2")
        ctx.emit("dve", lambda e, s2=s2, o2=o2, m2=m2: e.tensor_scalar(
            o2, s2, m2[:, 0:1], None, ALU.is_ge), r=[g_b], w=[g_b], tag="l0g_o2")
        ctx.emit("dve", lambda e, o1=o1, d1=d1, dw=dw: e.tensor_scalar_mul(dw, o1, d1[:, 0:1]),
                 r=[g_b], w=[dw_b], tag="l0g_dw1")
        ctx.emit("dve", lambda e, o2=o2, d2=d2: e.tensor_scalar_mul(o2, o2, d2[:, 0:1]),
                 r=[g_b], w=[g_b], tag="l0g_dw2b")
        ctx.emit("dve", lambda e, o2=o2, dw=dw: e.tensor_tensor(dw, dw, o2, ALU.add),
                 r=[g_b, dw_b], w=[dw_b], tag="l0g_dw3")
        ctx.emit("pe", lambda e, dw=dw: e.transpose(pf5[0:8, 0:128], dw, identf[:]),
                 r=[dw_b, identf_b], w=[pf5b], tag="l0g_tr")
        ctx.emit("dve", lambda e, tc=tc: e.tensor_copy(rows[0:8, tc * 128:(tc + 1) * 128],
                                                       pf5[0:8, 0:128]),
                 r=[pf5b], w=[ctx.buf("v_l0_dwT")], tag="l0g_trev")


    new_phase(ctx)
    l0acc = ctx.dram("l0_acc", (D, T), F32); l0acc_b = ctx.buf("l0_acc")
    rsl0 = ctx.tensors["rsl0"]; rsl0b = ctx.bufs["rsl0"]
    rsl1 = ctx.tensors["rsl1"]; rsl1b = ctx.bufs["rsl1"]
    AB_H = 0
    gb_sb = af[:, 8192:8704]
    gb_b = ctx.buf("v_l0_gb")
    sact = af[:, 8704:9216]
    sact_b = ctx.buf("v_l0_sact")
    sg2 = af[:, 9216:9728]
    sg2_b = ctx.buf("v_l0_sg2")
    dwT_b = ctx.buf("v_l0_dwT")
    h_b = ctx.buf("v_l0_h")
    dst_b = ctx.buf("v_l0_dnst")
    first_acc = set()
    for e_ in range(E):
        for fh in range(2):
            for k in range(NC):
                ctx.emit("sp", lambda e, k=k, e_=e_, fh=fh: e.dma_start(
                    wsl0[:, k * 1024:(k + 1) * 1024],
                    wbf["wg"][0][e_ * FF + fh * 1024: e_ * FF + fh * 1024 + 1024,
                                 k * 128:(k + 1) * 128], transpose=True),
                    r=[wbf["wg"][1]], w=[wsl0b], dma=True, tag="l0m_wg")
                ctx.emit("sp", lambda e, k=k, e_=e_, fh=fh: e.dma_start(
                    wsl1[:, k * 1024:(k + 1) * 1024],
                    wbf["wu"][0][e_ * FF + fh * 1024: e_ * FF + fh * 1024 + 1024,
                                 k * 128:(k + 1) * 128], transpose=True),
                    r=[wbf["wu"][1]], w=[wsl1b], dma=True, tag="l0m_wu")
            for fi in range(8):
                rslt = rsl0 if fi < 4 else rsl1
                rslb = rsl0b if fi < 4 else rsl1b
                ctx.emit("sp", lambda e, fi=fi, e_=e_, fh=fh, rslt=rslt: e.dma_start(
                    rslt[:, (fi % 4) * 1024:(fi % 4) * 1024 + 1024],
                    wbf["wd"][0][e_ * D: (e_ + 1) * D,
                                 fh * 1024 + fi * 128: fh * 1024 + (fi + 1) * 128], transpose=True),
                    r=[wbf["wd"][1]], w=[rslb], dma=True, tag="l0m_wd")
            for tg in range(NTT):
                t0 = tg * TT
                ctx.emit("pe", lambda e, e_=e_, t0=t0: e.matmul(
                    pf3[0:128, 0:TT], rows[0:8, 4200 + e_ * 128: 4200 + (e_ + 1) * 128],
                    rows[0:8, t0:t0 + TT], start=True, stop=True),
                    r=[sel_b, dwT_b], w=[pf3b], tag="l0m_gB")
                ctx.emit("act", lambda e: e.copy(gb_sb, pf3[0:128, 0:TT]),
                         r=[pf3b], w=[gb_b], tag="l0m_gBe")
                for fi in range(8):
                    psa, psab = (pf0, pf0b) if fi % 2 == 0 else (pf1, pf1b)
                    psb_, psbb = (pf4, pf4b) if fi % 2 == 0 else (pf5, pf5b)
                    def mma(e, fi=fi, t0=t0, psa=psa):
                        inst = None
                        for k in range(NC):
                            inst = e.matmul(psa[0:128, 0:TT],
                                            wsl0[:, k * 1024 + fi * 128: k * 1024 + (fi + 1) * 128],
                                            xnt[:, k * T + t0: k * T + t0 + TT],
                                            start=(k == 0), stop=(k == NC - 1))
                        return inst
                    ctx.emit("pe", mma, r=[wsl0b, xnb], w=[psab], tag="l0m_mma")
                    def mmb(e, fi=fi, t0=t0, psb_=psb_):
                        inst = None
                        for k in range(NC):
                            inst = e.matmul(psb_[0:128, 0:TT],
                                            wsl1[:, k * 1024 + fi * 128: k * 1024 + (fi + 1) * 128],
                                            xnt[:, k * T + t0: k * T + t0 + TT],
                                            start=(k == 0), stop=(k == NC - 1))
                        return inst
                    ctx.emit("pe", mmb, r=[wsl1b, xnb], w=[psbb], tag="l0m_mmb")
                    ctx.emit("act", lambda e, psa=psa: e.activation(sact, psa[0:128, 0:TT], AF.Silu),
                             r=[psab], w=[sact_b], tag="l0m_silu")
                    ctx.emit("pool", lambda e: e.tensor_tensor(sg2, sact, gb_sb, ALU.mult),
                             r=[sact_b, gb_b], w=[sg2_b], tag="l0m_sg")
                    ctx.emit("dve", lambda e, fi=fi, psb_=psb_: e.tensor_tensor(
                        ab[:, AB_H + fi * TT: AB_H + (fi + 1) * TT], sg2, psb_[0:128, 0:TT], ALU.mult),
                        r=[sg2_b, psbb], w=[h_b], tag="l0m_h")
                for o in range(NC):
                    pso, psob = (pf0, pf0b) if o % 2 == 0 else (pf1, pf1b)
                    def mmd(e, o=o, pso=pso):
                        inst = None
                        for fi in range(8):
                            rslt = rsl0 if fi < 4 else rsl1
                            inst = e.matmul(pso[0:128, 0:TT],
                                            rslt[:, (fi % 4) * 1024 + o * 128: (fi % 4) * 1024 + (o + 1) * 128],
                                            ab[:, AB_H + fi * TT: AB_H + (fi + 1) * TT],
                                            start=(fi == 0), stop=(fi == 7))
                        return inst
                    ctx.emit("pe", mmd, r=[rsl0b, rsl1b, h_b], w=[psob], tag="l0m_mmd")
                    dstg = af[:, 9728 + (o % 2) * 512: 9728 + (o % 2) * 512 + TT]
                    ctx.emit("act", lambda e, dstg=dstg, pso=pso: e.copy(dstg, pso[0:128, 0:TT]),
                             r=[psob], w=[dst_b], tag="l0m_dev")
                    key = (o, tg)
                    if key not in first_acc:
                        first_acc.add(key)
                        ctx.emit("pool", lambda e, o=o, t0=t0, dstg=dstg: e.dma_start(
                            l0acc[o * 128:(o + 1) * 128, t0:t0 + TT], dstg),
                            r=[dst_b], w=[l0acc_b], dma=True, tag="l0m_dst0")
                    else:
                        ctx.emit("pool", lambda e, o=o, t0=t0, dstg=dstg: e.dma_start(
                            l0acc[o * 128:(o + 1) * 128, t0:t0 + TT], dstg, accum_op=ALU.add),
                            r=[dst_b], w=[l0acc_b], dma=True, tag="l0m_dacc")

    new_phase(ctx)
    for k in range(NC):
        ctx.emit("sp", lambda e, k=k: e.dma_start(
            wsl0[:, k * 1024:(k + 1) * 1024], wbf["wo"][0][0:D, k * 128:(k + 1) * 128],
            transpose=True), r=[wbf["wo"][1]], w=[wsl0b], dma=True, tag="l0t_woT")
    at_b = ctx.buf("v_l0t_at")
    xr_b = ctx.buf("v_l0t_xr")
    ac_b = ctx.buf("v_l0t_ac")
    tm_b = ctx.buf("v_l0t_tm")
    ot_b = ctx.buf("v_l0t_ot")
    for ti in range(NTT):
        t0 = ti * TT
        for c in range(NC):
            ctx.load(ab[:, c * TT:(c + 1) * TT], att[c * 128:(c + 1) * 128, t0:t0 + TT],
                     at_b, rbufs=[att_b], tag="l0t_atld")
        for m in range(NC):
            psn, psb_ = (pf0, pf0b) if m % 2 == 0 else (pf1, pf1b)
            def mmo(e, m=m, psn=psn):
                inst = None
                for k in range(NC):
                    inst = e.matmul(psn[0:128, 0:TT],
                                    wsl0[:, k * 1024 + m * 128: k * 1024 + (m + 1) * 128],
                                    ab[:, k * TT:(k + 1) * TT], start=(k == 0), stop=(k == NC - 1))
                return inst
            ctx.emit("pe", mmo, r=[wsl0b, at_b], w=[psb_], tag="l0t_mm")
            xr = af[:, 0:TT]
            ac = af[:, TT:2 * TT]
            tm = af[:, 2 * TT:3 * TT]
            ot = af[:, 3 * TT:4 * TT]
            ctx.load(xr, xh[m * 128:(m + 1) * 128, t0:t0 + TT], xr_b, rbufs=[xhb], tag="l0t_x")
            ctx.load(ac, l0acc[m * 128:(m + 1) * 128, t0:t0 + TT], ac_b, rbufs=[l0acc_b], tag="l0t_ac")
            ctx.emit("dve", lambda e, tm=tm, xr=xr, psn=psn: e.tensor_tensor(tm, xr, psn[0:128, 0:TT], ALU.add),
                     r=[xr_b, psb_], w=[tm_b], tag="l0t_a1")
            ctx.emit("dve", lambda e, tm=tm, ot=ot, ac=ac: e.tensor_tensor(ot, tm, ac, ALU.add),
                     r=[tm_b, ac_b], w=[ot_b], tag="l0t_a2")
            ctx.store(xh2[m * 128:(m + 1) * 128, t0:t0 + TT], ot, ot_b,
                      wbufs=[xh2b], q="act", tag="l0t_st")


def build_l0_program():
    nc = bass.Bass("TRN2", target_bir_lowering=False)
    with ExitStack() as stk:
        ctx = Ctx(nc, stk)
        xin = ctx.dram("x_in", (T, D), F32, kind="ExternalInput")
        yout = ctx.dram("y_out", (T, D), F32, kind="ExternalOutput")
        H = {}
        for nm, shape in [("wq", (D, D)), ("wk", (D, D)), ("wv", (D, D)), ("wo", (D, D)),
                          ("pope_bias", (16, 64)), ("router_w", (8, D)),
                          ("w_gate", (8 * 2048, D)), ("w_up", (8 * 2048, D)),
                          ("w_down", (8 * D, 2048))]:
            h = ctx.dram("i_" + nm, shape, F32, kind="ExternalInput")
            H[nm] = (h, ctx.buf("i_" + nm))
        xh = ctx.dram("xh", (D, T), F32); xhb = ctx.buf("xh")
        xh2 = ctx.dram("xh2", (D, T), F32); xh2b = ctx.buf("xh2")
        alloc_standard(ctx)
        emit_load_smalls(ctx)
        emit_transpose_in(ctx, xin, ctx.buf("x_in"), xh, xhb)
        new_phase(ctx)
        emit_layer0(ctx, H, xh, xhb, xh2, xh2b)
        new_phase(ctx)
        emit_transpose_out(ctx, xh2, xh2b, yout, ctx.buf("y_out"))
        finish(ctx)
    return nc



def l3_attn_consts():
    S = T
    ROT = 32
    inv = 1.0 / 10000.0 ** (np.arange(0, ROT, 2) / ROT)
    t = np.arange(S, dtype=np.float64)
    fr = np.repeat(t[:, None] * inv[None, :], 2, axis=1)
    base = (np.arange(0, ROT, 2) + 0.4 * ROT) / (1.4 * ROT)
    power = (t - S // 2) / 512.0
    scale = np.repeat(base[None, :] ** power[:, None], 2, axis=1)
    cos = np.cos(fr); sin = np.sin(fr)
    A = np.ones((64, S)); B = np.zeros((64, S))
    Ak = np.ones((64, S)); Bk = np.zeros((64, S))
    A[0:32] = (cos * scale).T
    B[0:32] = (sin * scale).T
    Ak[0:32] = (cos / scale).T
    Bk[0:32] = (sin / scale).T
    A128 = np.tile(A, (2, 1)).astype(np.float32)
    B128 = np.tile(B, (2, 1)).astype(np.float32)
    Ak128 = np.tile(Ak, (2, 1)).astype(np.float32)
    Bk128 = np.tile(Bk, (2, 1)).astype(np.float32)
    P = np.zeros((128, 128))
    for blk in range(2):
        o = blk * 64
        for i in range(16):
            P[o + 2 * i, o + 2 * i + 1] = -1.0
            P[o + 2 * i + 1, o + 2 * i] = 1.0
        for j in range(32, 64):
            P[o + j, o + j] = 1.0
    PT = P.T.astype(ml_dtypes.bfloat16)
    hh = np.arange(16)
    gamma = 1.0 - 2.0 ** (-5.0 - hh)
    idx = np.arange(S, dtype=np.float64)
    n_h = np.zeros((16, S))
    for h in range(16):
        rs = (1.0 - gamma[h] ** (idx + 1)) / (1.0 - gamma[h])
        n_h[h] = 1.0 / np.sqrt(rs)
    c_h = (gamma[:, None] ** (idx[None, :] % 128)) * n_h
    c_h = c_h.astype(np.float32)
    C4 = np.zeros((16, 64, 512), np.float64)
    for h in range(16):
        for b in range(4):
            C4[h, :, b * 128:(b + 1) * 128] = gamma[h] ** (128.0 * b)
    C4 = C4.astype(np.float32)
    GCOL = np.zeros((128, 16 * 32), np.float64)
    p = np.arange(128, dtype=np.float64)
    for h in range(16):
        for d2i, d2 in enumerate(range(-3, 29)):
            with np.errstate(under='ignore'):
                GCOL[:, h * 32 + d2i] = gamma[h] ** (-p) * gamma[h] ** (128.0 * d2)
    GCOL = GCOL.astype(np.float32)
    TRI = (np.arange(128)[:, None] <= np.arange(128)[None, :]).astype(np.float32)
    SEL16 = np.zeros((16, 16 * 64), np.float32)
    for h in range(16):
        SEL16[h, h * 64:(h + 1) * 64] = 1.0
    BH16 = np.zeros((16, 8 * 128), np.float32)
    for c in range(8):
        for j2 in range(2):
            BH16[2 * c + j2, c * 128 + j2 * 64: c * 128 + (j2 + 1) * 64] = 1.0
    return A128, B128, Ak128, Bk128, PT, c_h, C4, GCOL, TRI, SEL16, BH16, gamma


def emit_rotary_pass(ctx, tag, srcdram, src_b, hA, hB, htabs_b, hPT):
    af = ctx.tensors["af"]
    ab = ctx.tensors["ab"]
    pb = ctx.tensors["pb"]; pb_b = ctx.bufs["pb"]
    ident = None
    PT_sb = ctx.tensors["smallsb"]
    ctx.load(af[:, 0:4096], hA[0:128, 0:T], htabs_b, tag=tag + "_A")
    ctx.load(af[:, 4096:8192], hB[0:128, 0:T], htabs_b, tag=tag + "_B")
    ctx.load(PT_sb[:, 0:128], hPT[0:128, 0:128], ctx.buf("v_l3_PT"), tag=tag + "_P")
    ch_b = ctx.buf("v_l3_rotch")
    t1_b = ctx.buf("v_l3_rott")
    st_b = ctx.buf("v_l3_rotst")
    pf0 = ctx.tensors["pf0"]; pf0b = ctx.bufs["pf0"]
    for c in range(NC):
        ctx.load(ab[:, 0:4096], srcdram[c * 128:(c + 1) * 128, 0:T], ch_b,
                 rbufs=[src_b], tag=tag + "_ld")
        for tt in range(NTT):
            t0 = tt * TT
            ctx.emit("pe", lambda e, t0=t0: e.matmul(
                pf0[0:128, 0:TT], PT_sb[:, 0:128], ab[:, t0:t0 + TT], start=True, stop=True),
                r=[ctx.buf("v_l3_PT"), ch_b], w=[pf0b], tag=tag + "_mm")
            t1 = af[:, 8192:8704]
            ctx.emit("dve", lambda e, t0=t0, t1=t1: e.tensor_tensor(
                t1, ab[:, t0:t0 + TT], af[:, t0:t0 + TT], ALU.mult),
                r=[ch_b, htabs_b], w=[t1_b], tag=tag + "_m1")
            t2 = af[:, 8704:9216]
            ctx.emit("dve", lambda e, t0=t0, t2=t2: e.tensor_tensor(
                t2, pf0[0:128, 0:TT], af[:, 4096 + t0:4096 + t0 + TT], ALU.mult),
                r=[pf0b, htabs_b], w=[t1_b], tag=tag + "_m2")
            st = ab[:, 4096 + (tt % 2) * TT: 4096 + (tt % 2) * TT + TT]
            ctx.emit("dve", lambda e, t1=t1, t2=t2, st=st: e.tensor_tensor(st, t1, t2, ALU.add),
                     r=[t1_b], w=[st_b], tag=tag + "_m3")
            ctx.store(srcdram[c * 128:(c + 1) * 128, t0:t0 + TT], st, st_b,
                      wbufs=[src_b], q="act", tag=tag + "_st")


def emit_layer3_attn(ctx, H, xh, xhb, y3, y3_b):
    af = ctx.tensors["af"]
    ab = ctx.tensors["ab"]
    rows = ctx.tensors["rows"]
    smalls = ctx.tensors["smalls"]
    xnt = ctx.tensors["xn"]; xnb = ctx.bufs["xn"]
    oc, ocb = ones_col(ctx)
    orow, orb = ones_row(ctx, 128)
    pf0 = ctx.tensors["pf0"]; pf0b = ctx.bufs["pf0"]
    pf1 = ctx.tensors["pf1"]; pf1b = ctx.bufs["pf1"]
    pf2 = ctx.tensors["pf2"]; pf2b = ctx.bufs["pf2"]
    pf3 = ctx.tensors["pf3"]; pf3b = ctx.bufs["pf3"]
    pf4 = ctx.tensors["pf4"]; pf4b = ctx.bufs["pf4"]
    NH = 16

    A128, B128, Ak128, Bk128, PT, c_h, C4, GCOL, TRI, SEL16, BH16, gamma = l3_attn_consts()
    hA = ctx.const("l3A", A128); hB = ctx.const("l3B", B128)
    hAk = ctx.const("l3Ak", Ak128); hBk = ctx.const("l3Bk", Bk128)
    hPT = ctx.const("l3PT", PT)
    hch = ctx.const("l3ch", c_h)
    hC4 = ctx.const("l3C4", C4.reshape(16 * 64, 512))
    hGC = ctx.const("l3GCOL", GCOL)
    hTRI = ctx.const("l3TRI", TRI)
    hSEL = ctx.const("l3SEL16", SEL16)
    hBH = ctx.const("l3BH16", BH16)
    htabs_b = ctx.buf("v_l3_tabs")

    wbf = {}
    for nm in ("Wq", "Wk", "Wv", "Wg", "Wo"):
        wbf[nm] = cast_weight(ctx, "l3" + nm, H[nm][0], H[nm][1], D, D)
    bias_b = ctx.buf("v_l3_bias")
    for c in range(NC):
        ctx.load(smalls[:, 280 + c:281 + c], H["bq"][0][c * 128:(c + 1) * 128, 0:1],
                 bias_b, rbufs=[H["bq"][1]], tag="l3_bq")
        ctx.load(smalls[:, 288 + c:289 + c], H["bk"][0][c * 128:(c + 1) * 128, 0:1],
                 bias_b, rbufs=[H["bk"][1]], tag="l3_bk")
        ctx.load(smalls[:, 296 + c:297 + c], H["bg"][0][c * 128:(c + 1) * 128, 0:1],
                 bias_b, rbufs=[H["bg"][1]], tag="l3_bg")
        ctx.load(smalls[:, 340 + c:341 + c], H["bo"][0][c * 128:(c + 1) * 128, 0:1],
                 bias_b, rbufs=[H["bo"][1]], tag="l3_bo")
    bv_row = rows[0:1, 4500:4500 + D]
    bv_b = ctx.buf("v_l3_bv")
    ctx.emit("sp", lambda e: e.dma_start(bv_row, H["bv"][0].ap().rearrange("d one -> one d")),
             r=[H["bv"][1]], w=[bv_b], dma=True, tag="l3_bv")
    gn_b = ctx.buf("v_l3_gn")
    ctx.load(smalls[0:16, 336:337], H["gn_w"][0][0:16, 0:1], gn_b, rbufs=[H["gn_w"][1]], tag="l3_gnw")
    ctx.load(smalls[0:16, 337:338], H["gn_b"][0][0:16, 0:1], gn_b, rbufs=[H["gn_b"][1]], tag="l3_gnb")

    qT = ctx.dram("l3_qT", (D, T), BF16); qT_b = ctx.buf("l3_qT")
    kT = ctx.dram("l3_kT", (D, T), BF16); kT_b = ctx.buf("l3_kT")
    gT = ctx.dram("l3_gT", (D, T), BF16); gT_b = ctx.buf("l3_gT")
    vtok = ctx.dram("l3_v", (T, D), BF16); vtok_b = ctx.buf("l3_v")
    ret = ctx.dram("l3_ret", (D, T), BF16); ret_b = ctx.buf("l3_ret")
    stg_b = [ctx.buf("v_l3_st0"), ctx.buf("v_l3_st1")]
    sj = [0]

    def mk_proj_evac(dst, dst_b, bias_base, func=None):
        def ev(c2, m0, msz, n0, nsz, ps_t, ps_b):
            s = sj[0] % 2; sj[0] += 1
            st = ab[:, s * 512: s * 512 + nsz]
            mchunk = m0 // 128
            if func is None:
                c2.emit("act", lambda e: e.activation(
                    st[0:msz, 0:nsz], ps_t[0:msz, 0:nsz], AF.Identity,
                    bias=smalls[:, bias_base + mchunk:bias_base + mchunk + 1]),
                    r=[ps_b, bias_b], w=[stg_b[s]], tag="l3pj_ev")
            else:
                c2.emit("act", lambda e: e.activation(
                    st[0:msz, 0:nsz], ps_t[0:msz, 0:nsz], func,
                    bias=smalls[:, bias_base + mchunk:bias_base + mchunk + 1]),
                    r=[ps_b, bias_b], w=[stg_b[s]], tag="l3pj_ev")
            c2.store(dst[m0:m0 + msz, n0:n0 + nsz], st[0:msz, 0:nsz], stg_b[s],
                     wbufs=[dst_b], q="act", tag="l3pj_st")
        return ev
    rhs_xn = ("sbuf", xnt, xnb, lambda k, n: (k // 128) * T + n)
    linear(ctx, tag="l3q", lhsT=("dram_km", wbf["Wq"][0], wbf["Wq"][1]), rhs=rhs_xn,
           K=D, M=D, N=T, evac=mk_proj_evac(qT, qT_b, 280), m_tile=1024)
    linear(ctx, tag="l3k", lhsT=("dram_km", wbf["Wk"][0], wbf["Wk"][1]), rhs=rhs_xn,
           K=D, M=D, N=T, evac=mk_proj_evac(kT, kT_b, 288), m_tile=1024)
    linear(ctx, tag="l3g", lhsT=("dram_km", wbf["Wg"][0], wbf["Wg"][1]), rhs=rhs_xn,
           K=D, M=D, N=T, evac=mk_proj_evac(gT, gT_b, 296, AF.Silu), m_tile=1024)
    wsl0 = ctx.tensors["wsl0"]; wsl0b = ctx.bufs["wsl0"]
    for k in range(NC):
        ctx.emit("sp", lambda e, k=k: e.dma_start(
            wsl0[:, k * 1024:(k + 1) * 1024],
            wbf["Wv"][0][k * 128:(k + 1) * 128, 0:D]),
            r=[wbf["Wv"][1]], w=[wsl0b], dma=True, tag="l3_wv")
    pi = [0]
    for tc in range(T // 128):
        for ot in range(2):
            psn, psb_ = (pf0, pf0b) if pi[0] % 2 == 0 else (pf1, pf1b)
            pi[0] += 1
            def mmv(e, tc=tc, ot=ot, psn=psn):
                for k in range(NC):
                    e.matmul(psn[0:128, 0:512],
                             xnt[:, k * T + tc * 128: k * T + (tc + 1) * 128],
                             wsl0[:, k * 1024 + ot * 512: k * 1024 + ot * 512 + 512],
                             start=(k == 0), stop=False)
                return e.matmul(psn[0:128, 0:512], orow,
                                bv_row[:, ot * 512: ot * 512 + 512], start=False, stop=True)
            ctx.emit("pe", mmv, r=[xnb, wsl0b, orb, bv_b], w=[psb_], tag="l3v_mm")
            s = sj[0] % 2; sj[0] += 1
            st = ab[:, s * 512: s * 512 + 512]
            ctx.emit("act", lambda e, st=st, psn=psn: e.copy(st, psn[0:128, 0:512]),
                     r=[psb_], w=[stg_b[s]], tag="l3v_ev")
            ctx.store(vtok[tc * 128:(tc + 1) * 128, ot * 512:ot * 512 + 512], st,
                      stg_b[s], wbufs=[vtok_b], q="act", tag="l3v_st")

    new_phase(ctx)
    emit_rotary_pass(ctx, "l3rq", qT, qT_b, hA, hB, htabs_b, hPT)
    new_phase(ctx)
    emit_rotary_pass(ctx, "l3rk", kT, kT_b, hAk, hBk, htabs_b, hPT)

    new_phase(ctx)
    ctx.load(af[:, 8192:8192 + 512], hGC[0:128, 0:512], ctx.buf("v_l3_gc"), tag="l3_gcld")
    ctx.load(af[0:128, 8704:8704 + 128], hTRI[0:128, 0:128], ctx.buf("v_l3_tri"), tag="l3_trild")
    ctx.load(rows[0:16, 0:1024], hSEL[0:16, :], ctx.buf("v_l3_sel"), tag="l3_selld")
    ctx.load(rows[0:16, 1024:1024 + T], hch[0:16, 0:T], ctx.buf("v_l3_chld"), tag="l3_chld")
    gq_b = ctx.buf("v_l3_gq")
    kq_b = ctx.buf("v_l3_kq")
    vv_b = ctx.buf("v_l3_vv")
    eE_b = ctx.buf("v_l3_E")
    rt_b = ctx.buf("v_l3_rt")
    ac_b = ctx.buf("v_l3_acc")
    for h in range(NH):
        ch_, hrow = h // 2, (h % 2) * 64
        ctx.load(ab[0:64, 0:4096], kT[ch_ * 128 + hrow: ch_ * 128 + hrow + 64, 0:T],
                 kq_b, rbufs=[kT_b], tag="l3r_kld")
        ctx.load(ab[0:64, 4096:8192], qT[ch_ * 128 + hrow: ch_ * 128 + hrow + 64, 0:T],
                 gq_b, rbufs=[qT_b], tag="l3r_qld")
        for kv in range(32):
            ctx.load(ab[:, 8192 + kv * 64: 8192 + (kv + 1) * 64],
                     vtok[kv * 128:(kv + 1) * 128, h * 64:(h + 1) * 64],
                     vv_b, rbufs=[vtok_b], tag="l3r_vld")
        ctx.load(af[0:64, 11264:11776], hC4[h * 64:(h + 1) * 64, 0:512],
                 ctx.buf("v_l3_c4"), tag="l3r_c4ld")
        for qt in range(NTT):
            t0 = qt * TT
            c4ap = af[0:64, 11264:11776]
            qstg = ab[0:64, 10240:10752]
            ctx.emit("dve", lambda e, t0=t0, c4ap=c4ap, qstg=qstg: e.tensor_tensor(
                qstg, ab[0:64, 4096 + t0:4096 + t0 + TT], c4ap, ALU.mult),
                r=[gq_b, ctx.buf("v_l3_c4")], w=[ctx.buf("v_l3_qstg")], tag="l3r_qc4")
            kcmax = 4 * qt + 3
            for kc in range(kcmax + 1):
                psn, psb_ = (pf0, pf0b) if kc % 2 == 0 else (pf1, pf1b)
                ctx.emit("pe", lambda e, kc=kc, psn=psn, qstg=qstg: e.matmul(
                    psn[0:128, 0:TT], ab[0:64, kc * 128:(kc + 1) * 128], qstg,
                    start=True, stop=True), r=[kq_b, ctx.buf("v_l3_qstg")], w=[psb_], tag="l3r_mmS")
                d2 = 4 * qt - kc
                gcol = af[:, 8192 + h * 32 + (d2 + 3): 8192 + h * 32 + (d2 + 3) + 1]
                Et = ab[0:128, 10752 + (kc % 2) * 512: 10752 + (kc % 2) * 512 + 512]
                eb = ctx.buf("v_l3_E" + str(kc % 2))
                if kc < 4 * qt:
                    ctx.emit("act", lambda e, psn=psn, gcol=gcol, Et=Et: e.activation(
                        Et, psn[0:128, 0:TT], AF.Identity, scale=gcol),
                        r=[psb_, ctx.buf("v_l3_gc")], w=[eb], tag="l3r_Eev")
                else:
                    j = kc - 4 * qt
                    if j > 0:
                        ctx.emit("dve", lambda e, Et=Et, j=j: e.memset(Et[:, 0:j * 128], 0.0),
                                 r=[], w=[eb], tag="l3r_Ez")
                    ctx.emit("dve", lambda e, psn=psn, gcol=gcol, Et=Et, j=j: e.scalar_tensor_tensor(
                        Et[:, j * 128:(j + 1) * 128], psn[0:128, j * 128:(j + 1) * 128],
                        gcol, af[0:128, 8704:8704 + 128], ALU.mult, ALU.mult),
                        r=[psb_, ctx.buf("v_l3_gc"), ctx.buf("v_l3_tri")], w=[eb], tag="l3r_Etri")
                    if j < 3:
                        ctx.emit("act", lambda e, psn=psn, gcol=gcol, Et=Et, j=j: e.activation(
                            Et[:, (j + 1) * 128:512], psn[0:128, (j + 1) * 128:512],
                            AF.Identity, scale=gcol),
                            r=[psb_, ctx.buf("v_l3_gc")], w=[eb], tag="l3r_Eev2")
                ctx.emit("pe", lambda e, kc=kc, Et=Et, kcmax=kcmax: e.matmul(
                    pf2[0:64, 0:TT], ab[:, 8192 + kc * 64: 8192 + (kc + 1) * 64], Et,
                    start=(kc == 0), stop=(kc == kcmax)), r=[vv_b, eb], w=[pf2b], tag="l3r_av")
            ctx.emit("pe", lambda e, h=h, t0=t0: e.matmul(
                pf3[0:64, 0:TT], rows[0:16, h * 64:(h + 1) * 64],
                rows[0:16, 1024 + t0:1024 + t0 + TT], start=True, stop=True),
                r=[ctx.buf("v_l3_sel"), ctx.buf("v_l3_chld")], w=[pf3b], tag="l3r_chB")
            rt = af[:, 9216 + 0:9216 + 512]
            ctx.emit("act", lambda e, rt=rt: e.copy(rt[0:64, 0:TT], pf2[0:64, 0:TT]),
                     r=[pf2b], w=[rt_b], tag="l3r_rev")
            ctx.emit("dve", lambda e, rt=rt: e.tensor_tensor(rt[0:64, 0:TT], rt[0:64, 0:TT],
                                                             pf3[0:64, 0:TT], ALU.mult),
                     r=[rt_b, pf3b], w=[rt_b], tag="l3r_rmul")
            sumc = af[0:64, 9984 + h:9984 + h + 1]
            sqsumc = af[0:64, 10016 + h:10016 + h + 1]
            red = af[0:64, 10048:10049]
            sq = af[0:64, 10752:10752 + 512]
            ctx.emit("dve", lambda e, red=red, rt=rt: e.reduce_sum(red[0:64, 0:1], rt[0:64, 0:TT], axis=AX.X),
                     r=[rt_b], w=[ac_b], tag="l3r_s1")
            if qt == 0:
                ctx.emit("dve", lambda e, sumc=sumc, red=red: e.tensor_copy(sumc, red[0:64, 0:1]),
                         r=[ac_b], w=[ac_b], tag="l3r_s2")
            else:
                ctx.emit("dve", lambda e, sumc=sumc, red=red: e.tensor_add(sumc, sumc, red[0:64, 0:1]),
                         r=[ac_b], w=[ac_b], tag="l3r_s2b")
            ctx.emit("act", lambda e, sq=sq, rt=rt: e.activation(sq[0:64, 0:TT], rt[0:64, 0:TT],
                                                                 AF.Square, accum_out=red[0:64, 0:1]),
                     r=[rt_b], w=[ac_b], tag="l3r_sq")
            if qt == 0:
                ctx.emit("dve", lambda e, sqsumc=sqsumc, red=red: e.tensor_copy(sqsumc, red[0:64, 0:1]),
                         r=[ac_b], w=[ac_b], tag="l3r_s3")
            else:
                ctx.emit("dve", lambda e, sqsumc=sqsumc, red=red: e.tensor_add(sqsumc, sqsumc, red[0:64, 0:1]),
                         r=[ac_b], w=[ac_b], tag="l3r_s3b")
            rst = ab[0:64, 11776:12288]
            ctx.emit("dve", lambda e, rst=rst, rt=rt: e.tensor_copy(rst, rt[0:64, 0:TT]),
                     r=[rt_b], w=[ctx.buf("v_l3_rst")], tag="l3r_rstg")
            ctx.store(ret[h * 64:(h + 1) * 64, t0:t0 + TT], rst, ctx.buf("v_l3_rst"),
                      wbufs=[ret_b], q="act", tag="l3r_rst")
    return wbf, qT, qT_b, kT, kT_b, gT, gT_b, vtok, vtok_b, ret, ret_b


def emit_layer3_attn_tail(ctx, H, wbf, gT, gT_b, ret, ret_b, y3, y3_b, hBH):
    af = ctx.tensors["af"]
    ab = ctx.tensors["ab"]
    rows = ctx.tensors["rows"]
    smalls = ctx.tensors["smalls"]
    oc, ocb = ones_col(ctx)
    pf4 = ctx.tensors["pf4"]; pf4b = ctx.bufs["pf4"]
    pf5 = ctx.tensors["pf5"]; pf5b = ctx.bufs["pf5"]
    pf0 = ctx.tensors["pf0"]; pf0b = ctx.bufs["pf0"]
    pf1 = ctx.tensors["pf1"]; pf1b = ctx.bufs["pf1"]
    identf = ctx.tensors["identf"]; identf_b = ctx.bufs["identf"]
    ac_b = ctx.buf("v_l3_acc")
    gn_b = ctx.buf("v_l3_gn")
    bias_b = ctx.buf("v_l3_bias")
    new_phase(ctx)
    rowsum = rows[0:1, 5120:5136]
    rowsq = rows[0:1, 5136:5152]
    tb = ctx.buf("v_l3_tot")
    ctx.emit("pe", lambda e: e.matmul(pf4[0:1, 0:16], oc[0:64, 0:1], af[0:64, 9984:10000],
                                      start=True, stop=True), r=[ac_b, ocb], w=[pf4b], tag="l3t_t1")
    ctx.emit("dve", lambda e: e.tensor_copy(rowsum, pf4[0:1, 0:16]), r=[pf4b], w=[tb], tag="l3t_t1e")
    ctx.emit("pe", lambda e: e.matmul(pf4[0:1, 0:16], oc[0:64, 0:1], af[0:64, 10016:10032],
                                      start=True, stop=True), r=[ac_b, ocb], w=[pf4b], tag="l3t_t2")
    ctx.emit("dve", lambda e: e.tensor_copy(rowsq, pf4[0:1, 0:16]), r=[pf4b], w=[tb], tag="l3t_t2e")
    scol = af[0:16, 10048:10049]
    sqcol = af[0:16, 10049:10050]
    ctx.emit("pe", lambda e: e.transpose(pf5[0:16, 0:1], rowsum, identf[0:1, 0:1]),
             r=[tb, identf_b], w=[pf5b], tag="l3t_tr1")
    ctx.emit("dve", lambda e: e.tensor_copy(scol, pf5[0:16, 0:1]), r=[pf5b], w=[tb], tag="l3t_tr1e")
    ctx.emit("pe", lambda e: e.transpose(pf5[0:16, 0:1], rowsq, identf[0:1, 0:1]),
             r=[tb, identf_b], w=[pf5b], tag="l3t_tr2")
    ctx.emit("dve", lambda e: e.tensor_copy(sqcol, pf5[0:16, 0:1]), r=[pf5b], w=[tb], tag="l3t_tr2e")
    NTOT = float(T * 64)
    mu = af[0:16, 10050:10051]
    msq = af[0:16, 10051:10052]
    va = af[0:16, 10052:10053]
    acol = af[0:16, 10053:10054]
    bcol = af[0:16, 10054:10055]
    ctx.emit("dve", lambda e: e.tensor_single_scalar(mu, scol, 1.0 / NTOT, ALU.mult),
             r=[tb], w=[tb], tag="l3t_mu")
    ctx.emit("dve", lambda e: e.tensor_single_scalar(msq, sqcol, 1.0 / NTOT, ALU.mult),
             r=[tb], w=[tb], tag="l3t_msq")
    ctx.emit("act", lambda e: e.activation(va, mu, AF.Square), r=[tb], w=[tb], tag="l3t_mu2")
    ctx.emit("dve", lambda e: e.tensor_tensor(va, msq, va, ALU.subtract), r=[tb], w=[tb], tag="l3t_var")
    ctx.emit("dve", lambda e: e.tensor_single_scalar(va, va, 1e-5, ALU.add), r=[tb], w=[tb], tag="l3t_vej")
    ctx.emit("dve", lambda e: e.reciprocal(va, va), r=[tb], w=[tb], tag="l3t_rec")
    ctx.emit("act", lambda e: e.activation(va, va, AF.Sqrt), r=[tb], w=[tb], tag="l3t_rstd")
    ctx.emit("dve", lambda e: e.tensor_tensor(acol, va, smalls[0:16, 336:337], ALU.mult),
             r=[tb, gn_b], w=[tb], tag="l3t_a")
    ctx.emit("dve", lambda e: e.scalar_tensor_tensor(bcol, mu, -1.0, acol, ALU.mult, ALU.mult),
             r=[tb], w=[tb], tag="l3t_b1")
    ctx.emit("dve", lambda e: e.tensor_tensor(bcol, bcol, smalls[0:16, 337:338], ALU.add),
             r=[tb, gn_b], w=[tb], tag="l3t_b2")
    ctx.load(rows[0:16, 0:1024], hBH[0:16, :], ctx.buf("v_l3_bh"), tag="l3t_bhld")
    AC = af[:, 10056:10064]
    BC = af[:, 10064:10072]
    acb = ctx.buf("v_l3_ACB")
    for c in range(NC):
        ctx.emit("pe", lambda e, c=c: e.matmul(pf5[0:128, 0:1], rows[0:16, c * 128:(c + 1) * 128],
                                               acol, start=True, stop=True),
                 r=[ctx.buf("v_l3_bh"), tb], w=[pf5b], tag="l3t_ac")
        ctx.emit("dve", lambda e, c=c: e.tensor_copy(AC[:, c:c + 1], pf5[0:128, 0:1]),
                 r=[pf5b], w=[acb], tag="l3t_ace")
        ctx.emit("pe", lambda e, c=c: e.matmul(pf5[0:128, 0:1], rows[0:16, c * 128:(c + 1) * 128],
                                               bcol, start=True, stop=True),
                 r=[ctx.buf("v_l3_bh"), tb], w=[pf5b], tag="l3t_bc")
        ctx.emit("dve", lambda e, c=c: e.tensor_copy(BC[:, c:c + 1], pf5[0:128, 0:1]),
                 r=[pf5b], w=[acb], tag="l3t_bce")
    wsl0 = ctx.tensors["wsl0"]; wsl0b = ctx.bufs["wsl0"]
    for k in range(NC):
        ctx.emit("sp", lambda e, k=k: e.dma_start(
            wsl0[:, k * 1024:(k + 1) * 1024], wbf["Wo"][0][k * 128:(k + 1) * 128, 0:D]),
            r=[wbf["Wo"][1]], w=[wsl0b], dma=True, tag="l3t_wo")
    rt_b2 = ctx.buf("v_l3t_rt")
    gt_b2 = ctx.buf("v_l3t_gt")
    st_b2 = ctx.buf("v_l3t_st")
    for ti in range(NTT):
        t0 = ti * TT
        for c in range(NC):
            ctx.load(ab[:, c * TT:(c + 1) * TT], ret[c * 128:(c + 1) * 128, t0:t0 + TT],
                     rt_b2, rbufs=[ret_b], tag="l3t_rld")
            ctx.load(ab[:, 4096 + c * TT:4096 + (c + 1) * TT], gT[c * 128:(c + 1) * 128, t0:t0 + TT],
                     gt_b2, rbufs=[gT_b], tag="l3t_gld")
            ctx.emit("dve", lambda e, c=c: e.tensor_scalar(
                ab[:, c * TT:(c + 1) * TT], ab[:, c * TT:(c + 1) * TT],
                AC[:, c:c + 1], BC[:, c:c + 1], ALU.mult, ALU.add),
                r=[rt_b2, acb], w=[rt_b2], tag="l3t_norm")
            ctx.emit("dve", lambda e, c=c: e.tensor_tensor(
                ab[:, c * TT:(c + 1) * TT], ab[:, c * TT:(c + 1) * TT],
                ab[:, 4096 + c * TT:4096 + (c + 1) * TT], ALU.mult),
                r=[rt_b2, gt_b2], w=[rt_b2], tag="l3t_gmul")
        for m in range(NC):
            psn, psb_ = (pf0, pf0b) if m % 2 == 0 else (pf1, pf1b)
            def mmo(e, m=m, psn=psn):
                inst = None
                for k in range(NC):
                    inst = e.matmul(psn[0:128, 0:TT],
                                    wsl0[:, k * 1024 + m * 128: k * 1024 + (m + 1) * 128],
                                    ab[:, k * TT:(k + 1) * TT], start=(k == 0), stop=(k == NC - 1))
                return inst
            ctx.emit("pe", mmo, r=[wsl0b, rt_b2], w=[psb_], tag="l3t_mm")
            st = af[:, 10752 + (m % 2) * 512:10752 + (m % 2) * 512 + TT]
            ctx.emit("act", lambda e, st=st, psn=psn, m=m: e.activation(
                st, psn[0:128, 0:TT], AF.Identity, bias=smalls[:, 340 + m:341 + m]),
                r=[psb_, bias_b], w=[st_b2], tag="l3t_ev")
            ctx.store(y3[m * 128:(m + 1) * 128, t0:t0 + TT], st, st_b2,
                      wbufs=[y3_b], q="act", tag="l3t_yst")


def l3_ffn_consts():
    DH = 64
    WIN = 512
    half_idx = np.arange(0, DH, 2, dtype=np.float64)
    inv = 1.0 / 10000.0 ** (half_idx / DH)
    t = np.arange(T, dtype=np.float64)
    fr = t[:, None] * inv[None, :]
    fr = np.concatenate([fr, fr], axis=1)
    base = (half_idx + 0.4 * DH) / (1.4 * DH)
    power = (t - T // 2) / float(WIN // 2)
    sc = base[None, :] ** power[:, None]
    sc = np.concatenate([sc, sc], axis=1)
    cos = np.cos(fr); sin = np.sin(fr)
    Aq = (cos * sc).T; Bq = (sin * sc).T
    Ak = (cos / sc).T; Bk = (sin / sc).T
    tile2 = lambda M: np.tile(M, (2, 1)).astype(np.float32)
    P = np.zeros((128, 128))
    for blk in range(2):
        o = blk * 64
        for i in range(32):
            P[o + i, o + 32 + i] = -1.0
            P[o + 32 + i, o + i] = 1.0
    PT2 = P.T.astype(ml_dtypes.bfloat16)
    masks = np.zeros((8, 128, 1024), np.float32)
    NEG = -1e30
    for w0 in range(2):
        for b in range(4):
            i_l = b * 128 + np.arange(128)[:, None]
            jj = np.arange(1024)[None, :]
            bad = (jj < i_l) | (jj > i_l + 512)
            if w0 == 0:
                bad = bad | (jj < 512)
            masks[w0 * 4 + b] = np.where(bad, NEG, 0.0)
    BP8 = np.zeros((8, 4 * 128), np.float32)
    for c in range(4):
        for j2 in range(2):
            BP8[2 * c + j2, c * 128 + j2 * 64: c * 128 + (j2 + 1) * 64] = 1.0
    BP8T = BP8.T.astype(ml_dtypes.bfloat16)
    return tile2(Aq), tile2(Bq), tile2(Ak), tile2(Bk), PT2, masks, BP8, BP8T


def emit_l2norm_rotary(ctx, tag, srcdram, src_b, hA, hB, hPT2, hBP8, hBP8T, sc_col, sc_col_b, eps_clip=1e-12):
    af = ctx.tensors["af"]
    ab = ctx.tensors["ab"]
    rows = ctx.tensors["rows"]
    smallsb = ctx.tensors["smallsb"]
    oc, ocb = ones_col(ctx)
    pf0 = ctx.tensors["pf0"]; pf0b = ctx.bufs["pf0"]
    pf1 = ctx.tensors["pf1"]; pf1b = ctx.bufs["pf1"]
    pf4 = ctx.tensors["pf4"]; pf4b = ctx.bufs["pf4"]
    htabs_b = ctx.buf("v_l3f_tabs")
    bp_b = ctx.buf("v_l3f_bp")
    nb = ctx.buf("v_l3f_n")
    chb = ctx.buf("v_l3f_ch")
    t1b = ctx.buf("v_l3f_t1")
    stb = ctx.buf("v_l3f_st")
    ctx.load(af[:, 0:4096], hA[0:128, 0:T], htabs_b, tag=tag + "_A")
    ctx.load(af[:, 4096:8192], hB[0:128, 0:T], htabs_b, tag=tag + "_B")
    ctx.load(smallsb[:, 128:256], hPT2[0:128, 0:128], ctx.buf("v_l3_PT2"), tag=tag + "_P")
    ctx.load(rows[0:8, 0:512], hBP8[0:8, :], bp_b, tag=tag + "_BP")
    for c4_ in range(4):
        ctx.load(smallsb[0:128, 256 + c4_ * 8:256 + (c4_ + 1) * 8],
                 hBP8T[c4_ * 128:(c4_ + 1) * 128, 0:8], bp_b, tag=tag + "_BPT")
    for c in range(4):
        ctx.load(ab[:, 0:4096], srcdram[c * 128:(c + 1) * 128, 0:T], chb,
                 rbufs=[src_b], tag=tag + "_ld1")
        for tt in range(NTT):
            t0 = tt * TT
            sq = ab[:, 4096:4608]
            ctx.emit("act", lambda e, t0=t0, sq=sq: e.activation(sq, ab[:, t0:t0 + TT], AF.Square),
                     r=[chb], w=[t1b], tag=tag + "_sq")
            ctx.emit("pe", lambda e, c=c, sq=sq: e.matmul(
                pf4[0:8, 0:TT], smallsb[:, 256 + c * 8:256 + (c + 1) * 8], sq, start=True, stop=True),
                r=[bp_b, t1b], w=[pf4b], tag=tag + "_nmm")
            if c == 0:
                ctx.emit("dve", lambda e, t0=t0: e.tensor_copy(rows[0:8, 1024 + t0:1024 + t0 + TT],
                                                               pf4[0:8, 0:TT]),
                         r=[pf4b], w=[nb], tag=tag + "_nev")
            else:
                ctx.emit("dve", lambda e, t0=t0: e.tensor_tensor(
                    rows[0:8, 1024 + t0:1024 + t0 + TT], rows[0:8, 1024 + t0:1024 + t0 + TT],
                    pf4[0:8, 0:TT], ALU.add), r=[nb, pf4b], w=[nb], tag=tag + "_nacc")
    for t0, tsz in chunks_of(T, TT):
        r_act(ctx, rows[0:8, 1024 + t0:1024 + t0 + tsz], nb, rows[0:8, 1024 + t0:1024 + t0 + tsz], nb, AF.Sqrt)
        r_ts(ctx, rows[0:8, 1024 + t0:1024 + t0 + tsz], nb, rows[0:8, 1024 + t0:1024 + t0 + tsz], nb,
             eps_clip, ALU.max)
        r_recip(ctx, rows[0:8, 1024 + t0:1024 + t0 + tsz], nb, rows[0:8, 1024 + t0:1024 + t0 + tsz], nb)
    for c in range(4):
        ctx.load(ab[:, 0:4096], srcdram[c * 128:(c + 1) * 128, 0:T], chb,
                 rbufs=[src_b], tag=tag + "_ld2")
        for tt in range(NTT):
            t0 = tt * TT
            ctx.emit("pe", lambda e, c=c, t0=t0: e.matmul(
                pf0[0:128, 0:TT], rows[0:8, c * 128:(c + 1) * 128],
                rows[0:8, 1024 + t0:1024 + t0 + TT], start=True, stop=True),
                r=[bp_b, nb], w=[pf0b], tag=tag + "_rqB")
            t1 = af[:, 8192:8704]
            ctx.emit("dve", lambda e, t0=t0, t1=t1: e.tensor_scalar_mul(
                t1, ab[:, t0:t0 + TT], sc_col), r=[chb, sc_col_b], w=[t1b], tag=tag + "_sc")
            ctx.emit("dve", lambda e, t1=t1: e.tensor_tensor(t1, t1, pf0[0:128, 0:TT], ALU.mult),
                     r=[t1b, pf0b], w=[t1b], tag=tag + "_rq")
            nstg = ab[:, 4096:4608]
            ctx.emit("dve", lambda e, t1=t1, nstg=nstg: e.tensor_copy(nstg, t1),
                     r=[t1b], w=[t1b], tag=tag + "_nst")
            ctx.emit("pe", lambda e, nstg=nstg: e.matmul(
                pf1[0:128, 0:TT], smallsb[:, 128:256], nstg, start=True, stop=True),
                r=[ctx.buf("v_l3_PT2"), t1b], w=[pf1b], tag=tag + "_rot")
            t2 = af[:, 8704:9216]
            ctx.emit("dve", lambda e, t0=t0, t1=t1, t2=t2: e.tensor_tensor(
                t2, t1, af[:, t0:t0 + TT], ALU.mult), r=[t1b, htabs_b], w=[t1b], tag=tag + "_m1")
            t3 = af[:, 9216:9728]
            ctx.emit("dve", lambda e, t0=t0, t3=t3: e.tensor_tensor(
                t3, pf1[0:128, 0:TT], af[:, 4096 + t0:4096 + t0 + TT], ALU.mult),
                r=[pf1b, htabs_b], w=[t1b], tag=tag + "_m2")
            st = ab[:, 4608 + (tt % 2) * TT:4608 + (tt % 2) * TT + TT]
            ctx.emit("dve", lambda e, t2=t2, t3=t3, st=st: e.tensor_tensor(st, t2, t3, ALU.add),
                     r=[t1b], w=[stb], tag=tag + "_m3")
            ctx.store(srcdram[c * 128:(c + 1) * 128, t0:t0 + TT], st, stb,
                      wbufs=[src_b], q="act", tag=tag + "_st")


def emit_layer3_ffn(ctx, H, xinner, xinner_b):
    af = ctx.tensors["af"]
    ab = ctx.tensors["ab"]
    rows = ctx.tensors["rows"]
    smalls = ctx.tensors["smalls"]
    smallsb = ctx.tensors["smallsb"]
    xnt = ctx.tensors["xn"]; xnb = ctx.bufs["xn"]
    oc, ocb = ones_col(ctx)
    orow, orb = ones_row(ctx, 128)
    identb = ctx.tensors["identb"]; identb_b = ctx.bufs["identb"]
    pf0 = ctx.tensors["pf0"]; pf0b = ctx.bufs["pf0"]
    pf1 = ctx.tensors["pf1"]; pf1b = ctx.bufs["pf1"]
    pf2 = ctx.tensors["pf2"]; pf2b = ctx.bufs["pf2"]
    pb = ctx.tensors["pb"]; pb_b = ctx.bufs["pb"]
    INNER, HID = 512, 2730
    WIN = 512
    NW = T // WIN
    Aq, Bq, Ak, Bk, PT2, masks, BP8, BP8T = l3_ffn_consts()
    hBP8T = ctx.const("l3fBP8T", BP8T)
    hAq = ctx.const("l3fAq", Aq); hBq = ctx.const("l3fBq", Bq)
    hAk = ctx.const("l3fAk", Ak); hBk = ctx.const("l3fBk", Bk)
    hPT2 = ctx.const("l3fPT2", PT2)
    hMK = ctx.const("l3fmasks", masks.reshape(8 * 128, 1024))
    hBP8 = ctx.const("l3fBP8", BP8)

    wbf = {}
    wbf["wqkv"] = cast_weight(ctx, "l3fqkv", H["wqkv"][0], H["wqkv"][1], 2 * D, 3 * INNER)
    wbf["wo"] = cast_weight(ctx, "l3fwo", H["wo"][0], H["wo"][1], 2 * INNER, D)
    wbf["w1"] = cast_weight(ctx, "l3fw1", H["w1"][0], H["w1"][1], 2 * D, 2 * HID)
    wbf["w2"] = cast_weight(ctx, "l3fw2", H["w2"][0], H["w2"][1], 2 * HID, D)

    q2 = ctx.dram("l3f_q", (INNER, T), BF16); q2_b = ctx.buf("l3f_q")
    k2 = ctx.dram("l3f_k", (INNER, T), BF16); k2_b = ctx.buf("l3f_k")
    v2 = ctx.dram("l3f_v", (T, INNER), BF16); v2_b = ctx.buf("l3f_v")
    o2 = ctx.dram("l3f_o", (INNER, T), BF16); o2_b = ctx.buf("l3f_o")
    hg = ctx.dram("l3f_hg", (HID, T), BF16); hg_b = ctx.buf("l3f_hg")
    h2 = ctx.dram("l3f_h2", (HID, T), BF16); h2_b = ctx.buf("l3f_h2")
    stg_b = [ctx.buf("v_l3f_st0"), ctx.buf("v_l3f_st1")]
    sj = [0]

    for li in range(2):
        new_phase(ctx)
        hx_b = ctx.buf("v_l3f_hx")
        for c in range(NC):
            for tt in range(NTT):
                t0 = tt * TT
                if c < 4:
                    ctx.emit("pool", lambda e, c=c, t0=t0: e.dma_start(
                        xnt[:, c * T + t0: c * T + t0 + TT],
                        xinner[c * 128:(c + 1) * 128, t0:t0 + TT]),
                        r=[xinner_b], w=[hx_b], dma=True, tag="l3f_hx")
                elif t0 > 0:
                    ctx.emit("pool", lambda e, c=c, t0=t0: e.dma_start(
                        xnt[:, c * T + t0: c * T + t0 + TT],
                        xinner[c * 128:(c + 1) * 128, t0 - 1:t0 + TT - 1]),
                        r=[xinner_b], w=[hx_b], dma=True, tag="l3f_hxs")
                else:
                    ctx.emit("pool", lambda e, c=c: e.dma_start(
                        xnt[:, c * T + 1: c * T + TT],
                        xinner[c * 128:(c + 1) * 128, 0:TT - 1]),
                        r=[xinner_b], w=[hx_b], dma=True, tag="l3f_hx0")
                    ctx.emit("dve", lambda e, c=c: e.memset(xnt[:, c * T: c * T + 1], 0.0),
                             r=[], w=[hx_b], tag="l3f_hx0z")
        rhs_hx = ("sbuf", xnt, hx_b, lambda k, n: (k // 128) * T + n)

        def mk_sp_evac(dst, dst_b):
            def ev(c2, m0, msz, n0, nsz, ps_t, ps_b):
                s = sj[0] % 2; sj[0] += 1
                st = ab[:, s * 512: s * 512 + nsz]
                c2.emit("act", lambda e: e.copy(st[0:msz, 0:nsz], ps_t[0:msz, 0:nsz]),
                        r=[ps_b], w=[stg_b[s]], tag="l3f_pjev")
                c2.store(dst[m0:m0 + msz, n0:n0 + nsz], st[0:msz, 0:nsz], stg_b[s],
                         wbufs=[dst_b], q="act", tag="l3f_pjst")
            return ev
        linear(ctx, tag="l3fq", lhsT=("dram_km", wbf["wqkv"][0], wbf["wqkv"][1]), rhs=rhs_hx,
               K=D, M=INNER, N=T, evac=mk_sp_evac(q2, q2_b), m_tile=512, k_base=li * D)
        linear(ctx, tag="l3fk", lhsT=("dram_km", wbf["wqkv"][0], wbf["wqkv"][1]), rhs=rhs_hx,
               K=D, M=INNER, N=T, evac=mk_sp_evac(k2, k2_b), m_tile=512, k_base=li * D, m_base=INNER)
        wsl0 = ctx.tensors["wsl0"]; wsl0b = ctx.bufs["wsl0"]
        for k in range(NC):
            ctx.emit("sp", lambda e, k=k, li=li: e.dma_start(
                wsl0[:, k * 512:(k + 1) * 512],
                wbf["wqkv"][0][li * D + k * 128: li * D + (k + 1) * 128, 2 * INNER:3 * INNER]),
                r=[wbf["wqkv"][1]], w=[wsl0b], dma=True, tag="l3f_wv")
        for tc in range(T // 128):
            psn, psb_ = (pf0, pf0b) if tc % 2 == 0 else (pf1, pf1b)
            def mmv(e, tc=tc, psn=psn):
                inst = None
                for k in range(NC):
                    inst = e.matmul(psn[0:128, 0:512],
                                    xnt[:, k * T + tc * 128: k * T + (tc + 1) * 128],
                                    wsl0[:, k * 512:(k + 1) * 512],
                                    start=(k == 0), stop=(k == NC - 1))
                return inst
            ctx.emit("pe", mmv, r=[hx_b, wsl0b], w=[psb_], tag="l3f_vmm")
            s = sj[0] % 2; sj[0] += 1
            st = ab[:, s * 512: s * 512 + 512]
            ctx.emit("act", lambda e, st=st, psn=psn: e.copy(st, psn[0:128, 0:512]),
                     r=[psb_], w=[stg_b[s]], tag="l3f_vev")
            ctx.store(v2[tc * 128:(tc + 1) * 128, 0:INNER], st, stg_b[s],
                      wbufs=[v2_b], q="act", tag="l3f_vst")

        qs_col = smalls[:, 460:461]
        ks_col = smalls[:, 461:462]
        sc_b = ctx.buf("v_l3f_qs")
        ctx.load(smalls[0:64, 460:461], H["q_scale"][0][li * 64:(li + 1) * 64, 0:1], sc_b,
                 rbufs=[H["q_scale"][1]], tag="l3f_qs1")
        ctx.emit("sp", lambda e: e.dma_start(smalls[64:128, 460:461],
                                             H["q_scale"][0][li * 64:(li + 1) * 64, 0:1]),
                 r=[H["q_scale"][1]], w=[sc_b], dma=True, tag="l3f_qs3")
        ctx.load(smalls[0:64, 461:462], H["k_scale"][0][li * 64:(li + 1) * 64, 0:1], sc_b,
                 rbufs=[H["k_scale"][1]], tag="l3f_ks1")
        ctx.emit("sp", lambda e: e.dma_start(smalls[64:128, 461:462],
                                             H["k_scale"][0][li * 64:(li + 1) * 64, 0:1]),
                 r=[H["k_scale"][1]], w=[sc_b], dma=True, tag="l3f_ks2")
        ctx.emit("dve", lambda e: e.tensor_single_scalar(qs_col, qs_col, 8.0, ALU.mult),
                 r=[sc_b], w=[sc_b], tag="l3f_qs8")

        new_phase(ctx)
        emit_l2norm_rotary(ctx, "l3fnq", q2, q2_b, hAq, hBq, hPT2, hBP8, hBP8T, qs_col, sc_b)
        new_phase(ctx)
        emit_l2norm_rotary(ctx, "l3fnk", k2, k2_b, hAk, hBk, hPT2, hBP8, hBP8T, ks_col, sc_b)

        new_phase(ctx)
        mk_b = ctx.buf("v_l3f_mk")
        for v_ in range(8):
            ctx.load(af[:, v_ * 1024:(v_ + 1) * 1024], hMK[v_ * 128:(v_ + 1) * 128, :],
                     mk_b, tag="l3f_mkld")
        q_b2 = ctx.buf("v_l3f_aq")
        k_b2 = ctx.buf("v_l3f_ak")
        v_b2 = ctx.buf("v_l3f_av")
        e_b2 = ctx.buf("v_l3f_aE")
        pt_b2 = ctx.buf("v_l3f_aPT")
        col_b2 = ctx.buf("v_l3f_acol")
        AB_Q, AB_K, AB_V, AB_E, AB_PT = 0, 2048, 6144, 10240, 11264
        for w in range(NW):
            kv0 = max(0, (w - 1) * WIN)
            for cc in range(4):
                ctx.load(ab[:, AB_Q + cc * 512:AB_Q + (cc + 1) * 512],
                         q2[cc * 128:(cc + 1) * 128, w * WIN:(w + 1) * WIN], q_b2,
                         rbufs=[q2_b], tag="l3f_aqld")
                koff = AB_K + cc * 1024 + (512 if w == 0 else 0)
                ctx.load(ab[:, koff:AB_K + (cc + 1) * 1024],
                         k2[cc * 128:(cc + 1) * 128, kv0:(w + 1) * WIN], k_b2,
                         rbufs=[k2_b], tag="l3f_akld")
            nv = 4 if w == 0 else 8
            for v_ in range(nv):
                voff = AB_V + (v_ + (4 if w == 0 else 0)) * 512
                ctx.load(ab[:, voff:voff + 512],
                         v2[kv0 + v_ * 128:kv0 + (v_ + 1) * 128, 0:INNER], v_b2,
                         rbufs=[v2_b], tag="l3f_avld")
            for h in range(8):
                ch_, hrow = h // 2, (h % 2) * 64
                for qs in range(4):
                    qap = ab[hrow:hrow + 64, AB_Q + ch_ * 512 + qs * 128:AB_Q + ch_ * 512 + qs * 128 + 128]
                    if w > 0:
                        ctx.emit("pe", lambda e, qap=qap, ch_=ch_, hrow=hrow: e.matmul(
                            pf0[0:128, 0:512], qap,
                            ab[hrow:hrow + 64, AB_K + ch_ * 1024:AB_K + ch_ * 1024 + 512],
                            start=True, stop=True), r=[q_b2, k_b2], w=[pf0b], tag="l3f_mmA")
                    ctx.emit("pe", lambda e, qap=qap, ch_=ch_, hrow=hrow: e.matmul(
                        pf1[0:128, 0:512], qap,
                        ab[hrow:hrow + 64, AB_K + ch_ * 1024 + 512:AB_K + (ch_ + 1) * 1024],
                        start=True, stop=True), r=[q_b2, k_b2], w=[pf1b], tag="l3f_mmB")
                    mi = (0 if w == 0 else 4) + qs
                    if w == 0:
                        ctx.emit("dve", lambda e, qs=qs: e.tensor_copy(
                            pf0[0:128, 0:512], af[:, (0 * 4 + qs) * 1024:(0 * 4 + qs) * 1024 + 512]),
                            r=[mk_b], w=[pf0b], tag="l3f_mkc")
                        ctx.emit("dve", lambda e, qs=qs: e.tensor_tensor(
                            pf1[0:128, 0:512], pf1[0:128, 0:512],
                            af[:, (0 * 4 + qs) * 1024 + 512:(0 * 4 + qs) * 1024 + 1024], ALU.add),
                            r=[mk_b, pf1b], w=[pf1b], tag="l3f_mka")
                    else:
                        ctx.emit("dve", lambda e, qs=qs: e.tensor_tensor(
                            pf0[0:128, 0:512], pf0[0:128, 0:512],
                            af[:, (4 + qs) * 1024:(4 + qs) * 1024 + 512], ALU.add),
                            r=[mk_b, pf0b], w=[pf0b], tag="l3f_mka0")
                        ctx.emit("dve", lambda e, qs=qs: e.tensor_tensor(
                            pf1[0:128, 0:512], pf1[0:128, 0:512],
                            af[:, (4 + qs) * 1024 + 512:(4 + qs) * 1024 + 1024], ALU.add),
                            r=[mk_b, pf1b], w=[pf1b], tag="l3f_mka1")
                    m1 = af[:, 8192:8193]; m2 = af[:, 8193:8194]; nm = af[:, 8194:8195]
                    d1 = af[:, 8195:8196]; d2 = af[:, 8196:8197]; rd = af[:, 8197:8198]
                    ctx.emit("dve", lambda e, m1=m1: e.reduce_max(m1, pf0[0:128, 0:512], axis=AX.X, negate=True),
                             r=[pf0b], w=[col_b2], tag="l3f_r1")
                    ctx.emit("dve", lambda e, m2=m2: e.reduce_max(m2, pf1[0:128, 0:512], axis=AX.X, negate=True),
                             r=[pf1b], w=[col_b2], tag="l3f_r2")
                    ctx.emit("dve", lambda e, m1=m1, m2=m2, nm=nm: e.tensor_tensor(nm, m1, m2, ALU.min),
                             r=[col_b2], w=[col_b2], tag="l3f_nm")
                    ctx.emit("act", lambda e, nm=nm, d1=d1: e.activation(
                        ab[:, AB_E:AB_E + 512], pf0[0:128, 0:512], AF.Exp, bias=nm, accum_out=d1),
                        r=[pf0b, col_b2], w=[e_b2, col_b2], tag="l3f_eA")
                    ctx.emit("act", lambda e, nm=nm, d2=d2: e.activation(
                        ab[:, AB_E + 512:AB_E + 1024], pf1[0:128, 0:512], AF.Exp, bias=nm, accum_out=d2),
                        r=[pf1b, col_b2], w=[e_b2, col_b2], tag="l3f_eB")
                    ctx.emit("dve", lambda e, d1=d1, d2=d2, rd=rd: e.tensor_add(rd, d1, d2),
                             r=[col_b2], w=[col_b2], tag="l3f_dd")
                    ctx.emit("dve", lambda e, rd=rd: e.reciprocal(rd, rd),
                             r=[col_b2], w=[col_b2], tag="l3f_rec")
                    ctx.emit("dve", lambda e, rd=rd: e.tensor_scalar_mul(
                        ab[:, AB_E:AB_E + 1024], ab[:, AB_E:AB_E + 1024], rd[:, 0:1]),
                        r=[e_b2, col_b2], w=[e_b2], tag="l3f_pn")
                    for b8 in range(8):
                        if w == 0 and b8 < 4:
                            continue
                        ctx.emit("pe", lambda e, b8=b8: e.transpose(
                            pb[0:128, 0:128], ab[:, AB_E + b8 * 128:AB_E + (b8 + 1) * 128],
                            identb[:]), r=[e_b2, identb_b], w=[pb_b], tag="l3f_tr")
                        eng = "dve" if b8 % 2 == 0 else "act"
                        if eng == "dve":
                            ctx.emit("dve", lambda e, b8=b8: e.tensor_copy(
                                ab[:, AB_PT + b8 * 128:AB_PT + (b8 + 1) * 128], pb[0:128, 0:128]),
                                r=[pb_b], w=[pt_b2], tag="l3f_trev")
                        else:
                            ctx.emit("act", lambda e, b8=b8: e.copy(
                                ab[:, AB_PT + b8 * 128:AB_PT + (b8 + 1) * 128], pb[0:128, 0:128]),
                                r=[pb_b], w=[pt_b2], tag="l3f_trev")
                    def mmav(e, w=w, h=h):
                        inst = None
                        first = True
                        rng = range(4, 8) if w == 0 else range(8)
                        last = 7
                        for b8 in rng:
                            inst = e.matmul(
                                pf2[0:64, 0:128],
                                ab[:, AB_V + b8 * 512 + h * 64:AB_V + b8 * 512 + h * 64 + 64],
                                ab[:, AB_PT + b8 * 128:AB_PT + (b8 + 1) * 128],
                                start=first, stop=(b8 == last))
                            first = False
                        return inst
                    ctx.emit("pe", mmav, r=[v_b2, pt_b2], w=[pf2b], tag="l3f_av")
                    ost = ab[0:64, AB_E + 896:AB_E + 1024]
                    ctx.emit("act", lambda e, ost=ost: e.copy(ost, pf2[0:64, 0:128]),
                             r=[pf2b], w=[e_b2], tag="l3f_oev")
                    ctx.store(o2[h * 64:(h + 1) * 64, w * WIN + qs * 128:w * WIN + qs * 128 + 128],
                              ost, e_b2, wbufs=[o2_b], q="act", tag="l3f_ost")

        new_phase(ctx)
        def evac_wo(c2, m0, msz, n0, nsz, ps_t, ps_b):
            s = sj[0] % 2; sj[0] += 1
            st = af[:, 8704 + s * 512:8704 + s * 512 + nsz]
            c2.emit("act", lambda e: e.copy(st[0:msz, 0:nsz], ps_t[0:msz, 0:nsz]),
                    r=[ps_b], w=[stg_b[s]], tag="l3f_woev")
            c2.emit("pool", lambda e: e.dma_start(
                xinner[m0:m0 + msz, n0:n0 + nsz], st[0:msz, 0:nsz], accum_op=ALU.add),
                r=[stg_b[s]], w=[xinner_b], dma=True, tag="l3f_woacc")
        linear(ctx, tag="l3fwo", lhsT=("dram_km", wbf["wo"][0], wbf["wo"][1]),
               rhs=("dram", o2, o2_b), K=INNER, M=D, N=T, evac=evac_wo,
               m_tile=1024, k_base=li * INNER, k_base_rhs=0)

        new_phase(ctx)
        lnb_b = ctx.buf("v_l3f_lnb")
        for c in range(NC):
            ctx.load(smalls[:, 348 + c:349 + c],
                     H["ln_g"][0][li * D + c * 128: li * D + (c + 1) * 128, 0:1],
                     lnb_b, rbufs=[H["ln_g"][1]], tag="l3f_lng")
            ctx.load(smalls[:, 356 + c:357 + c],
                     H["ln_b"][0][li * D + c * 128: li * D + (c + 1) * 128, 0:1],
                     lnb_b, rbufs=[H["ln_b"][1]], tag="l3f_lnbld")
        def src_shift(c, t0, tsz, dst_ap, dst_b):
            if c < 4:
                ctx.load(dst_ap[:, 0:tsz], xinner[c * 128:(c + 1) * 128, t0:t0 + tsz], dst_b,
                         rbufs=[xinner_b], tag="l3fln_ld")
            elif t0 > 0:
                ctx.load(dst_ap[:, 0:tsz], xinner[c * 128:(c + 1) * 128, t0 - 1:t0 + tsz - 1], dst_b,
                         rbufs=[xinner_b], tag="l3fln_lds")
            else:
                ctx.load(dst_ap[:, 1:tsz], xinner[c * 128:(c + 1) * 128, 0:tsz - 1], dst_b,
                         rbufs=[xinner_b], tag="l3fln_ld0")
                ctx.emit("dve", lambda e: e.memset(dst_ap[:, 0:1], 0.0), r=[], w=[dst_b], tag="l3fln_z")
        emit_ln(ctx, src_shift,
                affine=(lambda c: (smalls[:, 348 + c:349 + c], lnb_b),
                        lambda c: (smalls[:, 356 + c:357 + c], lnb_b)))

        new_phase(ctx)
        rhs_xn2 = ("sbuf", xnt, xnb, lambda k, n: (k // 128) * T + n)
        b1_b = ctx.buf("v_l3f_b1")
        def evac_g(c2, m0, msz, n0, nsz, ps_t, ps_b):
            bcol = af[0:msz, 9728:9729]
            c2.load(bcol, H["b1"][0][li * 2 * HID + HID + m0: li * 2 * HID + HID + m0 + msz, 0:1],
                    b1_b, rbufs=[H["b1"][1]], q="sp", tag="l3f_b1g")
            s = sj[0] % 2; sj[0] += 1
            st = ab[:, s * 512: s * 512 + nsz]
            c2.emit("act", lambda e: e.activation(st[0:msz, 0:nsz], ps_t[0:msz, 0:nsz],
                                                  AF.Gelu, bias=bcol),
                    r=[ps_b, b1_b], w=[stg_b[s]], tag="l3f_gev")
            c2.store(hg[m0:m0 + msz, n0:n0 + nsz], st[0:msz, 0:nsz], stg_b[s],
                     wbufs=[hg_b], q="act", tag="l3f_gst")
        linear(ctx, tag="l3fg", lhsT=("dram_km", wbf["w1"][0], wbf["w1"][1]), rhs=rhs_xn2,
               K=D, M=HID, N=T, evac=evac_g, m_tile=512, k_base=li * D, m_base=HID)
        hgt_b = ctx.buf("v_l3f_hgt")
        def evac_a(c2, m0, msz, n0, nsz, ps_t, ps_b):
            bcol = af[0:msz, 9729:9730]
            c2.load(bcol, H["b1"][0][li * 2 * HID + m0: li * 2 * HID + m0 + msz, 0:1],
                    b1_b, rbufs=[H["b1"][1]], q="sp", tag="l3f_b1a")
            hgt = ab[0:msz, 4096 + (sj[0] % 2) * 512:4096 + (sj[0] % 2) * 512 + nsz]
            c2.load(hgt, hg[m0:m0 + msz, n0:n0 + nsz], hgt_b, rbufs=[hg_b], q="sp", tag="l3f_hgld")
            tmp = af[0:msz, 9984:9984 + nsz]
            c2.emit("act", lambda e: e.activation(tmp[0:msz, 0:nsz], ps_t[0:msz, 0:nsz],
                                                  AF.Identity, bias=bcol),
                    r=[ps_b, b1_b], w=[ctx.buf("v_l3f_tmp")], tag="l3f_aev")
            s = sj[0] % 2; sj[0] += 1
            st = ab[:, s * 512: s * 512 + nsz]
            c2.emit("dve", lambda e, hgt=hgt: e.tensor_tensor(st[0:msz, 0:nsz], tmp[0:msz, 0:nsz],
                                                              hgt[0:msz, 0:nsz], ALU.mult),
                    r=[ctx.buf("v_l3f_tmp"), hgt_b], w=[stg_b[s]], tag="l3f_amul")
            c2.store(h2[m0:m0 + msz, n0:n0 + nsz], st[0:msz, 0:nsz], stg_b[s],
                     wbufs=[h2_b], q="act", tag="l3f_ast")
        linear(ctx, tag="l3fa", lhsT=("dram_km", wbf["w1"][0], wbf["w1"][1]), rhs=rhs_xn2,
               K=D, M=HID, N=T, evac=evac_a, m_tile=512, k_base=li * D, m_base=0)

        new_phase(ctx)
        b2_b = ctx.buf("v_l3f_b2")
        for c in range(NC):
            ctx.load(smalls[:, 364 + c:365 + c],
                     H["b2"][0][li * D + c * 128: li * D + (c + 1) * 128, 0:1],
                     b2_b, rbufs=[H["b2"][1]], tag="l3f_b2ld")
        for kp, (kb, ksz) in enumerate([(0, 1024), (1024, 1024), (2048, 682)]):
            def evac_w2(c2, m0, msz, n0, nsz, ps_t, ps_b, kp=kp):
                s = sj[0] % 2; sj[0] += 1
                st = af[:, 8704 + s * 512:8704 + s * 512 + nsz]
                if kp == 0:
                    mch = m0 // 128
                    c2.emit("act", lambda e, mch=mch: e.activation(
                        st[0:msz, 0:nsz], ps_t[0:msz, 0:nsz], AF.Identity,
                        bias=smalls[:, 364 + mch:365 + mch]),
                        r=[ps_b, b2_b], w=[stg_b[s]], tag="l3f_w2ev")
                else:
                    c2.emit("act", lambda e: e.copy(st[0:msz, 0:nsz], ps_t[0:msz, 0:nsz]),
                            r=[ps_b], w=[stg_b[s]], tag="l3f_w2ev2")
                c2.emit("pool", lambda e: e.dma_start(
                    xinner[m0:m0 + msz, n0:n0 + nsz], st[0:msz, 0:nsz], accum_op=ALU.add),
                    r=[stg_b[s]], w=[xinner_b], dma=True, tag="l3f_w2acc")
            linear(ctx, tag="l3fw2_%d_%d" % (li, kp),
                   lhsT=("dram_km", wbf["w2"][0], wbf["w2"][1]), rhs=("dram", h2, h2_b),
                   K=ksz, M=D, N=T, evac=evac_w2, m_tile=1024,
                   k_base=li * HID + kb, k_base_rhs=kb, n_tile=512)


def build_l3_program():
    nc = bass.Bass("TRN2", target_bir_lowering=False)
    with ExitStack() as stk:
        ctx = Ctx(nc, stk)
        xin = ctx.dram("x_in", (T, D), F32, kind="ExternalInput")
        yout = ctx.dram("y_out", (T, D), F32, kind="ExternalOutput")
        H = {}
        shapes = [("Wq", (D, D)), ("bq", (D, 1)), ("Wk", (D, D)), ("bk", (D, 1)),
                  ("Wv", (D, D)), ("bv", (D, 1)), ("Wg", (D, D)), ("bg", (D, 1)),
                  ("Wo", (D, D)), ("bo", (D, 1)), ("gn_w", (16, 1)), ("gn_b", (16, 1)),
                  ("wqkv", (2 * D, 1536)), ("q_scale", (2 * 64, 1)), ("k_scale", (2 * 64, 1)),
                  ("wo2", (2 * 512, D)), ("ln_g", (2 * D, 1)), ("ln_b", (2 * D, 1)),
                  ("w1", (2 * D, 5460)), ("b1", (2 * 5460, 1)), ("w2", (2 * 2730, D)),
                  ("b2", (2 * D, 1))]
        for nm, shape in shapes:
            h = ctx.dram("i_" + nm, shape, F32, kind="ExternalInput")
            H[nm] = (h, ctx.buf("i_" + nm))
        H["wo"] = H["wo2"]
        xh = ctx.dram("xh", (D, T), F32); xhb = ctx.buf("xh")
        xh2 = ctx.dram("xh2", (D, T), F32); xh2b = ctx.buf("xh2")
        y3 = ctx.dram("l3_y3", (D, T), F32); y3_b = ctx.buf("l3_y3")
        xinner = ctx.dram("l3_xi", (D, T), F32); xinner_b = ctx.buf("l3_xi")
        alloc_standard(ctx)
        emit_load_smalls(ctx)
        emit_transpose_in(ctx, xin, ctx.buf("x_in"), xh, xhb)
        new_phase(ctx)
        emit_layer3(ctx, H, xh, xhb, xh2, xh2b, y3, y3_b, xinner, xinner_b)
        new_phase(ctx)
        emit_transpose_out(ctx, xh2, xh2b, yout, ctx.buf("y_out"))
        finish(ctx)
    return nc


def emit_layer3(ctx, H, xh, xhb, xh2, xh2b, y3, y3_b, xinner, xinner_b):
    af = ctx.tensors["af"]
    xnt = ctx.tensors["xn"]; xnb = ctx.bufs["xn"]
    def src_x(c, t0, tsz, dst_ap, dst_b):
        ctx.load(dst_ap[:, 0:tsz], xh[c * 128:(c + 1) * 128, t0:t0 + tsz], dst_b,
                 rbufs=[xhb], tag="l3ln_ld")
    emit_ln(ctx, src_x)
    new_phase(ctx)
    Hattn = {k: H[k] for k in ("Wq", "bq", "Wk", "bk", "Wv", "bv", "Wg", "bg", "Wo", "bo",
                               "gn_w", "gn_b")}
    ret_pack = emit_layer3_attn(ctx, Hattn, xh, xhb, y3, y3_b)
    wbf_a, qT, qT_b, kT, kT_b, gT, gT_b, vtok, vtok_b, ret, ret_b = ret_pack
    from_consts = l3_attn_consts()
    hBH = ctx.const("l3BH16", from_consts[10])
    emit_layer3_attn_tail(ctx, Hattn, wbf_a, gT, gT_b, ret, ret_b, y3, y3_b, hBH)
    new_phase(ctx)
    xi_init_b = ctx.buf("v_l3_xii")
    for c in range(NC):
        ctx.emit("pool", lambda e, c=c: e.dma_start(
            xinner[c * 128:(c + 1) * 128, 0:T], xnt[:, c * T:(c + 1) * T]),
            r=[xnb], w=[xinner_b], dma=True, tag="l3_xiinit")
    Hffn = {k: H[k] for k in ("wqkv", "q_scale", "k_scale", "wo", "ln_g", "ln_b",
                              "w1", "b1", "w2", "b2")}
    emit_layer3_ffn(ctx, Hffn, xinner, xinner_b)
    new_phase(ctx)
    xr_b = ctx.buf("v_l3z_x")
    ya_b = ctx.buf("v_l3z_y")
    fi_b = ctx.buf("v_l3z_f")
    ot_b = ctx.buf("v_l3z_o")
    for ti in range(NTT):
        t0 = ti * TT
        for m in range(NC):
            xr = af[:, 0:TT]
            ya = af[:, TT:2 * TT]
            fi = af[:, 2 * TT:3 * TT]
            ot = af[:, 3 * TT:4 * TT]
            ctx.load(xr, xh[m * 128:(m + 1) * 128, t0:t0 + TT], xr_b, rbufs=[xhb], tag="l3z_x")
            ctx.load(ya, y3[m * 128:(m + 1) * 128, t0:t0 + TT], ya_b, rbufs=[y3_b], tag="l3z_y")
            ctx.load(fi, xinner[m * 128:(m + 1) * 128, t0:t0 + TT], fi_b, rbufs=[xinner_b], tag="l3z_f")
            ctx.emit("dve", lambda e, ot=ot, xr=xr, ya=ya: e.tensor_tensor(ot, xr, ya, ALU.add),
                     r=[xr_b, ya_b], w=[ot_b], tag="l3z_a1")
            ctx.emit("dve", lambda e, ot=ot, fi=fi: e.tensor_tensor(ot, ot, fi, ALU.add),
                     r=[ot_b, fi_b], w=[ot_b], tag="l3z_a2")
            ctx.store(xh2[m * 128:(m + 1) * 128, t0:t0 + TT], ot, ot_b,
                      wbufs=[xh2b], q="act", tag="l3z_st")


def build_full_program():
    nc = bass.Bass("TRN2", target_bir_lowering=False)
    with ExitStack() as stk:
        ctx = Ctx(nc, stk)
        xin = ctx.dram("x_in", (T, D), F32, kind="ExternalInput")
        yout = ctx.dram("y_out", (T, D), F32, kind="ExternalOutput")

        def ins(prefix, shapes):
            out = {}
            for nm, shape in shapes:
                h = ctx.dram("i_%s_%s" % (prefix, nm), shape, F32, kind="ExternalInput")
                out[nm] = (h, ctx.buf("i_%s_%s" % (prefix, nm)))
            return out
        H0 = ins("l0", [("wq", (D, D)), ("wk", (D, D)), ("wv", (D, D)), ("wo", (D, D)),
                        ("pope_bias", (16, 64)), ("router_w", (8, D)),
                        ("w_gate", (8 * 2048, D)), ("w_up", (8 * 2048, D)),
                        ("w_down", (8 * D, 2048))])
        H1 = ins("l1", [("Wr", (D, D)), ("Wk", (D, D)), ("Wv", (D, D)), ("Ww", (D, D)),
                        ("Wo", (D, D)), ("Wfr", (D, D)), ("Wfi", (D, D)), ("bias", (D, 1))])
        H2 = ins("l2", [("Wq", (D, D)), ("bq", (D, 1)), ("Wk", (D, D)), ("bk", (D, 1)),
                        ("Wv", (D, D)), ("bv", (D, 1)), ("w_up", (4096, D)),
                        ("w_down", (D, 4096))])
        H3 = ins("l3", [("Wq", (D, D)), ("bq", (D, 1)), ("Wk", (D, D)), ("bk", (D, 1)),
                        ("Wv", (D, D)), ("bv", (D, 1)), ("Wg", (D, D)), ("bg", (D, 1)),
                        ("Wo", (D, D)), ("bo", (D, 1)), ("gn_w", (16, 1)), ("gn_b", (16, 1)),
                        ("wqkv", (2 * D, 1536)), ("q_scale", (2 * 64, 1)), ("k_scale", (2 * 64, 1)),
                        ("wo2", (2 * 512, D)), ("ln_g", (2 * D, 1)), ("ln_b", (2 * D, 1)),
                        ("w1", (2 * D, 5460)), ("b1", (2 * 5460, 1)), ("w2", (2 * 2730, D)),
                        ("b2", (2 * D, 1))])
        H3["wo"] = H3["wo2"]

        xh_a = ctx.dram("xh_a", (D, T), F32); xh_ab = ctx.buf("xh_a")
        xh_b = ctx.dram("xh_b", (D, T), F32); xh_bb = ctx.buf("xh_b")
        y3 = ctx.dram("l3_y3", (D, T), F32); y3_b = ctx.buf("l3_y3")
        xinner = ctx.dram("l3_xi", (D, T), F32); xinner_b = ctx.buf("l3_xi")
        alloc_standard(ctx)
        emit_load_smalls(ctx)
        emit_transpose_in(ctx, xin, ctx.buf("x_in"), xh_a, xh_ab)
        new_phase(ctx)
        emit_layer0(ctx, H0, xh_a, xh_ab, xh_b, xh_bb)
        new_phase(ctx)
        emit_layer1(ctx, H1, xh_b, xh_bb, xh_a, xh_ab)
        new_phase(ctx)
        emit_layer2(ctx, H2, xh_a, xh_ab, xh_b, xh_bb)
        new_phase(ctx)
        emit_layer3(ctx, H3, xh_b, xh_bb, xh_a, xh_ab, y3, y3_b, xinner, xinner_b)
        new_phase(ctx)
        emit_transpose_out(ctx, xh_a, xh_ab, yout, ctx.buf("y_out"))
        finish(ctx)
        print("fused program units:", ctx.p.n_units)
    return nc



def _as2d(a, shape):
    return np.ascontiguousarray(np.asarray(a, dtype=np.float32).reshape(shape))


def kernel(**inputs):
    ncores = 8
    x = np.ascontiguousarray(np.asarray(inputs["L0_attn_x"], dtype=np.float32))

    l0_w = {"i_wq": _as2d(inputs["L0_attn_wq"], (D, D)),
            "i_wk": _as2d(inputs["L0_attn_wk"], (D, D)),
            "i_wv": _as2d(inputs["L0_attn_wv"], (D, D)),
            "i_wo": _as2d(inputs["L0_attn_wo"], (D, D)),
            "i_pope_bias": _as2d(inputs["L0_attn_pope_bias"], (16, 64)),
            "i_router_w": _as2d(inputs["L0_ffn_router_w"], (8, D)),
            "i_w_gate": _as2d(inputs["L0_ffn_w_gate"], (8 * 2048, D)),
            "i_w_up": _as2d(inputs["L0_ffn_w_up"], (8 * 2048, D)),
            "i_w_down": _as2d(inputs["L0_ffn_w_down"], (8 * D, 2048))}
    l1_w = {"i_Wr": _as2d(inputs["L1_attn_Wr"], (D, D)),
            "i_Wk": _as2d(inputs["L1_attn_Wk"], (D, D)),
            "i_Wv": _as2d(inputs["L1_attn_Wv"], (D, D)),
            "i_Ww": _as2d(inputs["L1_attn_Ww"], (D, D)),
            "i_Wo": _as2d(inputs["L1_attn_Wo"], (D, D)),
            "i_Wfr": _as2d(inputs["L1_ffn_w_freq_real"], (D, D)),
            "i_Wfi": _as2d(inputs["L1_ffn_w_freq_imag"], (D, D)),
            "i_bias": _as2d(inputs["L1_ffn_bias"], (D, 1))}
    l2_w = {"i_Wq": _as2d(inputs["L2_attn_Wq"], (D, D)),
            "i_bq": _as2d(inputs["L2_attn_bq"], (D, 1)),
            "i_Wk": _as2d(inputs["L2_attn_Wk"], (D, D)),
            "i_bk": _as2d(inputs["L2_attn_bk"], (D, 1)),
            "i_Wv": _as2d(inputs["L2_attn_Wv"], (D, D)),
            "i_bv": _as2d(inputs["L2_attn_bv"], (D, 1)),
            "i_w_up": _as2d(inputs["L2_ffn_w_up"], (4096, D)),
            "i_w_down": _as2d(inputs["L2_ffn_w_down"], (D, 4096))}
    l3_w = {"i_Wq": _as2d(inputs["L3_attn_Wq"], (D, D)),
            "i_bq": _as2d(inputs["L3_attn_bq"], (D, 1)),
            "i_Wk": _as2d(inputs["L3_attn_Wk"], (D, D)),
            "i_bk": _as2d(inputs["L3_attn_bk"], (D, 1)),
            "i_Wv": _as2d(inputs["L3_attn_Wv"], (D, D)),
            "i_bv": _as2d(inputs["L3_attn_bv"], (D, 1)),
            "i_Wg": _as2d(inputs["L3_attn_Wg"], (D, D)),
            "i_bg": _as2d(inputs["L3_attn_bg"], (D, 1)),
            "i_Wo": _as2d(inputs["L3_attn_Wo"], (D, D)),
            "i_bo": _as2d(inputs["L3_attn_bo"], (D, 1)),
            "i_gn_w": _as2d(inputs["L3_attn_gn_w"], (16, 1)),
            "i_gn_b": _as2d(inputs["L3_attn_gn_b"], (16, 1)),
            "i_wqkv": _as2d(inputs["L3_ffn_wqkv"], (2 * D, 1536)),
            "i_q_scale": _as2d(inputs["L3_ffn_q_scale"], (128, 1)),
            "i_k_scale": _as2d(inputs["L3_ffn_k_scale"], (128, 1)),
            "i_wo2": _as2d(inputs["L3_ffn_wo"], (2 * 512, D)),
            "i_ln_g": _as2d(inputs["L3_ffn_ln_g"], (2 * D, 1)),
            "i_ln_b": _as2d(inputs["L3_ffn_ln_b"], (2 * D, 1)),
            "i_w1": _as2d(inputs["L3_ffn_w1"], (2 * D, 5460)),
            "i_b1": _as2d(inputs["L3_ffn_b1"], (2 * 5460, 1)),
            "i_w2": _as2d(inputs["L3_ffn_w2"], (2 * 2730, D)),
            "i_b2": _as2d(inputs["L3_ffn_b2"], (2 * D, 1))}

    wall = {}
    for pfx, wmap in (("l0", l0_w), ("l1", l1_w), ("l2", l2_w), ("l3", l3_w)):
        for k, v in wmap.items():
            wall["i_%s_%s" % (pfx, k[2:])] = v
    nc = build_full_program()
    in_maps = [dict(wall, x_in=np.ascontiguousarray(x[c])) for c in range(ncores)]
    res = run_bass_kernel_spmd(nc, in_maps, core_ids=list(range(ncores)))
    return np.stack([np.asarray(res.results[c]["y_out"], dtype=np.float32)
                     for c in range(ncores)], axis=0)


# version 03
# speedup vs baseline: 1.0122x; 1.0122x over previous
import math
import numpy as np
import ml_dtypes
from contextlib import ExitStack

import concourse.bass as bass
import concourse.mybir as mybir
from concourse.bass_utils import run_bass_kernel_spmd

F32 = mybir.dt.float32
BF16 = mybir.dt.bfloat16
AF = mybir.ActivationFunctionType
ALU = mybir.AluOpType
AX = mybir.AxisListType

T = 4096
D = 1024
NC = 8
TT = 512
NTT = T // TT

ENGS = ("pe", "dve", "act", "pool", "sp")
DMA_ENGS = ("sp", "act", "pool")


class TL:
    __slots__ = ("name", "sem", "count")

    def __init__(self, name, sem):
        self.name = name
        self.sem = sem
        self.count = 0


class Buf:
    __slots__ = ("name", "last_w", "readers")

    def __init__(self, name):
        self.name = name
        self.last_w = None
        self.readers = {}

    def reset(self):
        self.last_w = None
        self.readers = {}


class Prog:
    def __init__(self, nc, stack):
        self.nc = nc
        self.ops = {k: [] for k in ENGS}
        self.tls = {}
        for name in ("pe", "dve", "act", "pool", "sp", "d_sp", "d_act", "d_pool"):
            sem = stack.enter_context(nc.semaphore("tl_" + name))
            self.tls[name] = TL(name, sem)
        self.waited = {k: {} for k in ENGS}
        self.n_units = 0

    def emit(self, ek, fn, r=(), w=(), dma=False, deps=(), tag=""):
        tl_exec = self.tls[ek]
        tl_done = self.tls["d_" + ek] if dma else tl_exec
        waited = self.waited[ek]
        waits = {}

        def need(dep):
            if dep is None:
                return
            tl, c = dep
            if tl is tl_exec and ek == "pe" and not dma:
                return
            if waited.get(tl, 0) >= c:
                return
            if waits.get(tl, 0) >= c:
                return
            waits[tl] = c

        for b in r:
            need(b.last_w)
        for b in w:
            need(b.last_w)
            for tl, c in b.readers.items():
                need((tl, c))
        for dep in deps:
            need(dep)

        for tl, c in waits.items():
            waited[tl] = max(waited.get(tl, 0), c)

        inc = 16 if dma else 1
        tl_done.count += inc
        cnt = tl_done.count
        wl = [(tl.sem, c) for tl, c in waits.items()]
        sem_done = tl_done.sem

        def run(eng, wl=wl, fn=fn, sem_done=sem_done, inc=inc, tag=tag):
            for sem, c in wl:
                eng.wait_ge(sem, c)
            inst = fn(eng)
            assert inst is not None, f"unit {tag} returned no instruction"
            inst.then_inc(sem_done, inc)

        self.ops[ek].append(run)
        self.n_units += 1

        for b in w:
            b.last_w = (tl_done, cnt)
            b.readers = {}
        for b in r:
            prev = b.readers.get(tl_done, 0)
            if cnt > prev:
                b.readers[tl_done] = cnt
        return (tl_done, cnt)

    def fence_deps(self):
        return [(tl, tl.count) for tl in self.tls.values()]

    def finalize(self):
        nc = self.nc
        with nc.Block() as block:
            @block.tensor
            def _(eng):
                for f in self.ops["pe"]:
                    f(eng)

            @block.vector
            def _(eng):
                for f in self.ops["dve"]:
                    f(eng)

            @block.scalar
            def _(eng):
                for f in self.ops["act"]:
                    f(eng)

            @block.gpsimd
            def _(eng):
                for f in self.ops["pool"]:
                    f(eng)

            @block.sync
            def _(eng):
                for f in self.ops["sp"]:
                    f(eng)


def chunks_of(total, size):
    out = []
    off = 0
    while off < total:
        out.append((off, min(size, total - off)))
        off += size
    return out


class Ctx:

    def __init__(self, nc, stack):
        self.nc = nc
        self.stack = stack
        self.p = Prog(nc, stack)
        self.bufs = {}
        self.tensors = {}
        self.consts_np = {}
        self._const_handles = {}
        self._ccol_vals = {}
        self.uid = 0

    def sb(self, name, shape, dtype):
        t = self.stack.enter_context(self.nc.sbuf_tensor(name, list(shape), dtype))
        self.tensors[name] = t
        self.bufs[name] = Buf(name)
        return t

    def ps(self, name, shape, dtype=F32):
        t = self.stack.enter_context(self.nc.psum_tensor(name, list(shape), dtype))
        self.tensors[name] = t
        self.bufs[name] = Buf(name)
        return t

    def dram(self, name, shape, dtype, kind="Internal"):
        t = self.nc.dram_tensor(name, list(shape), dtype, kind=kind)
        self.tensors[name] = t
        self.bufs[name] = Buf(name)
        return t

    def buf(self, name):
        if name not in self.bufs:
            self.bufs[name] = Buf(name)
        return self.bufs[name]

    def const(self, name, arr):
        if name not in self._const_handles:
            h = self.nc.inline_tensor(np.ascontiguousarray(arr), name="c_" + name)
            self._const_handles[name] = h
            self.bufs["c_" + name] = Buf("c_" + name)
            self.bufs["c_" + name].last_w = None
        return self._const_handles[name]

    def emit(self, *a, **k):
        return self.p.emit(*a, **k)

    def load(self, dst_ap, src_ap, wbuf, rbufs=(), q="sp", tag="load"):
        return self.emit(q, lambda e: e.dma_start(dst_ap, src_ap),
                         r=list(rbufs), w=[wbuf], dma=True, tag=tag)

    def store(self, dst_ap, src_ap, rbuf, wbufs=(), q="act", tag="store"):
        return self.emit(q, lambda e: e.dma_start(dst_ap, src_ap),
                         r=[rbuf], w=list(wbufs), dma=True, tag=tag)



def cast_weight(ctx, name, src_dram, src_buf, rows, cols):
    dst = ctx.dram("wbf_" + name, (rows, cols), BF16)
    dbuf = ctx.buf("wbf_" + name)
    step = max(1, (1 << 21) // (cols * 4))
    for r0 in range(0, rows, step):
        r1 = min(rows, r0 + step)
        ctx.emit("pool",
                 lambda e, r0=r0, r1=r1: e.dma_start(dst[r0:r1, :], src_dram[r0:r1, :]),
                 r=[src_buf], w=[dbuf], dma=True, tag=f"cast_{name}_{r0}")
    return dst, dbuf


def linear(ctx, *, tag, lhsT, rhs, K, M, N, evac,
           m_tile=1024, n_tile=TT, wslots=None, psums=None,
           k_chunk=128, m_base=0, k_base=0, k_base_rhs=None):
    kchunks = chunks_of(K, k_chunk)
    kidx = {k0: i for i, (k0, _) in enumerate(kchunks)}
    if k_base_rhs is None:
        k_base_rhs = k_base
    if wslots is None:
        wslots = ("wsl0", "wsl1")
    if psums is None:
        psums = ("pf0", "pf1", "pf2", "pf3")
    pcycle = [(ctx.tensors[s], ctx.bufs[s]) for s in psums]
    pi = 0
    ri = 0

    rhs_kind = rhs[0]
    if rhs_kind == "sbuf":
        _, rt0, rb0, rcol0 = rhs
        rbufs0 = list(rb0) if isinstance(rb0, (list, tuple)) else [rb0]

    lk = lhsT[0]
    if lk == "sbuf":
        _, wt, wb, wcol = lhsT

    mslices = chunks_of(M, m_tile)
    for si, (m0, msz) in enumerate(mslices):
        if lk in ("dram_km", "dram_mk"):
            _, wh, whb = lhsT
            sname = wslots[si % len(wslots)]
            wt = ctx.tensors[sname]
            wb = ctx.bufs[sname]
            assert len(kchunks) * m_tile <= wt.shape[1], f"{tag}: wslot too small"
            for ki, (k0, ksz) in enumerate(kchunks):
                if lk == "dram_km":
                    ctx.emit("sp", lambda e, k0=k0 + k_base, ksz=ksz, ki=ki, m0=m0 + m_base, msz=msz, wt=wt, wh=wh: e.dma_start(
                        wt[0:ksz, ki * m_tile: ki * m_tile + msz],
                        wh[k0:k0 + ksz, m0:m0 + msz]),
                        r=[whb], w=[wb], dma=True, tag=tag + "_wld")
                else:
                    ctx.emit("sp", lambda e, k0=k0 + k_base, ksz=ksz, ki=ki, m0=m0 + m_base, msz=msz, wt=wt, wh=wh: e.dma_start(
                        wt[0:ksz, ki * m_tile: ki * m_tile + msz],
                        wh[m0:m0 + msz, k0:k0 + ksz], transpose=True),
                        r=[whb], w=[wb], dma=True, tag=tag + "_wldT")

            def wcol(k_off, gm, m0=m0, m_tile=m_tile, kidx=kidx):
                return kidx[k_off] * m_tile + (gm - m0)

        for n0, nsz in chunks_of(N, n_tile):
            if rhs_kind == "dram":
                _, rh, rhb = rhs
                rname = "rsl" + str(ri % 2)
                ri += 1
                rt = ctx.tensors[rname]
                rb = ctx.bufs[rname]
                assert len(kchunks) * n_tile <= rt.shape[1], f"{tag}: rslot too small"
                for ki, (k0, ksz) in enumerate(kchunks):
                    ctx.emit("sp", lambda e, k0=k0 + k_base_rhs, ksz=ksz, ki=ki, n0=n0, nsz=nsz, rt=rt, rh=rh: e.dma_start(
                        rt[0:ksz, ki * n_tile: ki * n_tile + nsz],
                        rh[k0:k0 + ksz, n0:n0 + nsz]),
                        r=[rhb], w=[rb], dma=True, tag=tag + "_rld")
                rbufs = [rb]

                def rcol(k_off, nn, n0=n0, n_tile=n_tile, kidx=kidx):
                    return kidx[k_off] * n_tile + (nn - n0)
            else:
                rt, rbufs, rcol = rt0, rbufs0, rcol0

            for mc0, mcsz in chunks_of(msz, 128):
                gm0 = m0 + mc0
                ps_t, ps_b = pcycle[pi % len(pcycle)]
                pi += 1

                def mmgroup(e, kchunks=kchunks, wt=wt, wcol=wcol, rt=rt, rcol=rcol,
                            gm0=gm0, mcsz=mcsz, n0=n0, nsz=nsz, ps_t=ps_t):
                    inst = None
                    nk = len(kchunks)
                    for ki, (k0, ksz) in enumerate(kchunks):
                        wc = wcol(k0, gm0)
                        rc = rcol(k0, n0)
                        inst = e.matmul(ps_t[0:mcsz, 0:nsz],
                                        wt[0:ksz, wc:wc + mcsz],
                                        rt[0:ksz, rc:rc + nsz],
                                        start=(ki == 0), stop=(ki == nk - 1))
                    return inst

                ctx.emit("pe", mmgroup, r=[wb] + list(rbufs), w=[ps_b], tag=tag + "_mm")
                evac(ctx, gm0, mcsz, n0, nsz, ps_t, ps_b)


def evac_simple(dst_fn, engine="act", func=None, bias=None, scale=1.0, out_extra=None):
    def evac(ctx, m0, msz, n0, nsz, ps_t, ps_b):
        dst_ap, dst_b, extra = dst_fn(m0, msz, n0, nsz)
        rbufs = [ps_b] + list(extra)
        if engine == "act":
            if bias is not None:
                bap, bb = bias(m0, msz)
                rbufs.append(bb)
                f = func if func is not None else AF.Identity
                ctx.emit("act", lambda e, f=f: e.activation(
                    dst_ap, ps_t[0:msz, 0:nsz], f, bias=bap, scale=scale),
                    r=rbufs, w=[dst_b], tag="evac")
            elif func is not None:
                ctx.emit("act", lambda e: e.activation(
                    dst_ap, ps_t[0:msz, 0:nsz], func, scale=scale),
                    r=rbufs, w=[dst_b], tag="evac")
            elif scale != 1.0:
                ctx.emit("act", lambda e: e.mul(dst_ap, ps_t[0:msz, 0:nsz], scale),
                         r=rbufs, w=[dst_b], tag="evac")
            else:
                ctx.emit("act", lambda e: e.copy(dst_ap, ps_t[0:msz, 0:nsz]),
                         r=rbufs, w=[dst_b], tag="evac")
        else:
            if scale != 1.0:
                ctx.emit("dve", lambda e: e.tensor_scalar_mul(dst_ap, ps_t[0:msz, 0:nsz], scale),
                         r=rbufs, w=[dst_b], tag="evac")
            else:
                ctx.emit("dve", lambda e: e.tensor_copy(dst_ap, ps_t[0:msz, 0:nsz]),
                         r=rbufs, w=[dst_b], tag="evac")
    return evac



def emit_transpose_in(ctx, xin, xin_buf, xh, xh_buf, n_rows=T, n_cols=D):
    af = ctx.tensors["af"]
    tin = af[:, 0:1024]; tin_b = ctx.buf("v_tp_in")
    tout = af[:, 1024:2048]; tout_b = ctx.buf("v_tp_out")
    ident = ctx.tensors["identf"]; ident_b = ctx.bufs["identf"]
    ps = ctx.tensors["pw"]; ps_b = ctx.bufs["pw"]
    for tc in range(n_rows // 128):
        ctx.load(tin[:, 0:n_cols], xin[tc * 128:(tc + 1) * 128, :], tin_b,
                 rbufs=[xin_buf], tag="tpin_ld")
        for c in range(n_cols // 128):
            ctx.emit("pe", lambda e, c=c: e.transpose(
                ps[0:128, 0:128], tin[0:128, c * 128:(c + 1) * 128], ident[:, 0:128]),
                r=[tin_b, ident_b], w=[ps_b], tag="tpin_tr")
            ctx.emit("dve", lambda e, c=c: e.tensor_copy(
                tout[:, c * 128:(c + 1) * 128], ps[0:128, 0:128]),
                r=[ps_b], w=[tout_b], tag="tpin_ev")
        for c in range(n_cols // 128):
            ctx.store(xh[c * 128:(c + 1) * 128, tc * 128:(tc + 1) * 128],
                      tout[:, c * 128:(c + 1) * 128], tout_b, wbufs=[xh_buf],
                      q="act", tag="tpin_st")


def emit_transpose_out(ctx, yh, yh_buf, yout, yout_buf, n_rows=D, n_cols=T):
    af = ctx.tensors["af"]
    tin = af[:, 0:1024]; tin_b = ctx.buf("v_tp_in")
    tout = af[:, 1024:2048]; tout_b = ctx.buf("v_tp_out")
    ident = ctx.tensors["identf"]; ident_b = ctx.bufs["identf"]
    ps = ctx.tensors["pw"]; ps_b = ctx.bufs["pw"]
    for tc in range(n_cols // 128):
        for c in range(n_rows // 128):
            ctx.load(tin[:, c * 128:(c + 1) * 128],
                     yh[c * 128:(c + 1) * 128, tc * 128:(tc + 1) * 128], tin_b,
                     rbufs=[yh_buf], tag="tpo_ld")
        for c in range(n_rows // 128):
            ctx.emit("pe", lambda e, c=c: e.transpose(
                ps[0:128, 0:128], tin[0:128, c * 128:(c + 1) * 128], ident[:, 0:128]),
                r=[tin_b, ident_b], w=[ps_b], tag="tpo_tr")
            ctx.emit("dve", lambda e, c=c: e.tensor_copy(
                tout[:, c * 128:(c + 1) * 128], ps[0:128, 0:128]),
                r=[ps_b], w=[tout_b], tag="tpo_ev")
        ctx.store(yout[tc * 128:(tc + 1) * 128, 0:n_rows], tout[:, 0:n_rows],
                  tout_b, wbufs=[yout_buf], q="act", tag="tpo_st")



def alloc_standard(ctx):
    c = ctx
    c.sb("xn", (128, NC * T), BF16)
    c.sb("wsl0", (128, 8192), BF16)
    c.sb("wsl1", (128, 8192), BF16)
    c.sb("af", (128, 11776), F32)
    c.sb("ab", (128, 12288), BF16)
    c.sb("rsl0", (128, 4096), BF16)
    c.sb("rsl1", (128, 4096), BF16)
    c.sb("rows", (96, 5568), F32)
    c.sb("smalls", (128, 512), F32)
    c.sb("smallsb", (128, 512), BF16)
    c.ps("pf0", (128, 512), F32)
    c.ps("pf1", (128, 512), F32)
    c.ps("pf2", (128, 512), F32)
    c.ps("pf3", (128, 512), F32)
    c.ps("pf4", (128, 512), F32)
    c.ps("pf5", (128, 512), F32)
    c.ps("pw", (128, 512), F32)
    c.ps("pb", (128, 1024), BF16)
    rem = ctx.nc.sbuf_bytes_remaining
    print("SBUF bytes remaining after alloc:", rem)


def view(ctx, arena, name, col0, width, rows=128, row0=0):
    t = ctx.tensors[arena]
    ap = t[row0:row0 + rows, col0:col0 + width]
    b = ctx.buf("v_" + name)
    return ap, b


def new_phase(ctx):
    deps = ctx.p.fence_deps()
    ctx.phase_deps = deps
    for n, b in ctx.bufs.items():
        if n.startswith("v_"):
            b.last_w = None
            b.readers = {}
    return deps


def _ctx_emit(self, ek, fn, r=(), w=(), dma=False, deps=(), tag=""):
    pd = getattr(self, "phase_deps", ())
    if pd:
        deps = list(deps) + list(pd)
    return self.p.emit(ek, fn, r=r, w=w, dma=dma, deps=deps, tag=tag)

Ctx.emit = _ctx_emit



SM_ONESROW = 0

def emit_load_smalls(ctx):
    onesc = np.ones((128, 1), np.float32)
    h = ctx.const("ones128c", onesc)
    ctx.load(ctx.tensors["smalls"][:, 0:1], h[0:128, 0:1], ctx.buf("sm_ones"), tag="sm1")
    onesr = np.ones((1, 128), np.float32)
    h2 = ctx.const("ones128r", onesr)
    ctx.load(ctx.tensors["smalls"][0:1, 8:136], h2[0:1, 0:128], ctx.buf("sm_onesr"), tag="sm2")
    bp = np.zeros((2, 128), np.float32)
    bp[0, 0:64] = 1.0
    bp[1, 64:128] = 1.0
    h3 = ctx.const("blockpat", bp)
    ctx.load(ctx.tensors["smalls"][0:2, 136:264], h3[0:2, 0:128], ctx.buf("sm_bp"), tag="sm3")
    identf = np.eye(128, dtype=np.float32)
    h4 = ctx.const("identf", identf)
    ctx.sb("identf", (128, 128), F32)
    ctx.load(ctx.tensors["identf"][:], h4[0:128, 0:128], ctx.bufs["identf"], tag="sm4")
    identb = np.eye(128, dtype=ml_dtypes.bfloat16)
    h5 = ctx.const("identb", identb)
    ctx.sb("identb", (128, 128), BF16)
    ctx.load(ctx.tensors["identb"][:], h5[0:128, 0:128], ctx.bufs["identb"], tag="sm5")

def ones_col(ctx):
    return ctx.tensors["smalls"][:, 0:1], ctx.buf("sm_ones")

def ones_col_k(ctx, k):
    return ctx.tensors["smalls"][0:k, 0:1], ctx.buf("sm_ones")

def ones_row(ctx, m=128):
    return ctx.tensors["smalls"][0:1, 8:8 + m], ctx.buf("sm_onesr")

def block_pat(ctx):
    return ctx.tensors["smalls"][0:2, 136:264], ctx.buf("sm_bp")



def r_ts(ctx, out_ap, out_b, in_ap, in_b, s1, op0, s2=None, op1=ALU.bypass, eng="dve", extra_r=()):
    ctx.emit(eng, lambda e: e.tensor_scalar(out_ap, in_ap, s1, s2, op0, op1),
             r=[in_b] + list(extra_r), w=[out_b], tag="r_ts")

def r_tt(ctx, out_ap, out_b, a_ap, a_b, b_ap, b_b, op, eng="dve"):
    ctx.emit(eng, lambda e: e.tensor_tensor(out_ap, a_ap, b_ap, op),
             r=[a_b, b_b], w=[out_b], tag="r_tt")

def r_stt(ctx, out_ap, out_b, in0_ap, in0_b, scalar, in1_ap, in1_b, op0, op1, eng="dve", extra_r=()):
    ctx.emit(eng, lambda e: e.scalar_tensor_tensor(out_ap, in0_ap, scalar, in1_ap, op0, op1),
             r=[in0_b, in1_b] + list(extra_r), w=[out_b], tag="r_stt")

def r_act(ctx, out_ap, out_b, in_ap, in_b, func, bias=0.0, scale=1.0, extra_r=()):
    if isinstance(bias, float) and bias == 0.0:
        ctx.emit("act", lambda e: e.activation(out_ap, in_ap, func, scale=scale),
                 r=[in_b] + list(extra_r), w=[out_b], tag="r_act")
    else:
        ctx.emit("act", lambda e: e.activation(out_ap, in_ap, func, bias=bias, scale=scale),
                 r=[in_b] + list(extra_r), w=[out_b], tag="r_act")

def r_recip(ctx, out_ap, out_b, in_ap, in_b):
    ctx.emit("dve", lambda e: e.reciprocal(out_ap, in_ap), r=[in_b], w=[out_b], tag="r_recip")


def bcast_cols(ctx, ps_name, pairs, nsz=TT):
    ps_t = ctx.tensors[ps_name]
    ps_b = ctx.bufs[ps_name]
    rb = []
    for (ca, cb), (ra, rb_) in pairs:
        rb += [cb, rb_]

    def fn(e, pairs=pairs, ps_t=ps_t, nsz=nsz):
        inst = None
        n = len(pairs)
        for i, ((ca, cb), (ra, rb_)) in enumerate(pairs):
            P = ca.shape[1]
            inst = e.matmul(ps_t[0:P, 0:nsz], ca, ra, start=(i == 0), stop=(i == n - 1))
        return inst
    ctx.emit("pe", fn, r=rb, w=[ps_b], tag="bcast")
    return ps_t, ps_b



def emit_ln(ctx, src_fn, n_tok=T, eps=1e-5, dst_col=None, affine=None, dst=None):
    af = ctx.tensors["af"]
    if dst is None:
        xnt = ctx.tensors["xn"]
        xnb = ctx.bufs["xn"]
        dst = (xnt, xnb, lambda c, t0: c * T + t0)
    dt, db, dcol = dst
    rows = ctx.tensors["rows"]
    xb_b = ctx.buf("v_ln_xblk")
    sq_b = ctx.buf("v_ln_sq")
    r_sum = ctx.buf("v_ln_rsum")
    r_msq = ctx.buf("v_ln_rmsq")
    r_a = ctx.buf("v_ln_ra")
    r_b = ctx.buf("v_ln_rb")
    oc, ocb = ones_col(ctx)
    orow, orb = ones_row(ctx, 128)
    pf4 = ctx.tensors["pf4"]; pf4b = ctx.bufs["pf4"]
    pf5 = ctx.tensors["pf5"]; pf5b = ctx.bufs["pf5"]
    pf2 = ctx.tensors["pf2"]; pf2b = ctx.bufs["pf2"]
    pf3 = ctx.tensors["pf3"]; pf3b = ctx.bufs["pf3"]

    for ti, (t0, tsz) in enumerate(chunks_of(n_tok, TT)):
        for c in range(NC):
            xa = af[:, c * TT:(c + 1) * TT]
            src_fn(c, t0, tsz, xa, xb_b)
            sq = af[:, NC * TT: NC * TT + TT]
            ctx.emit("act", lambda e, xa=xa, sq=sq: e.activation(sq[:, 0:tsz], xa[:, 0:tsz], AF.Square),
                     r=[xb_b], w=[sq_b], tag="ln_sq")
            ctx.emit("pe", lambda e, xa=xa, c=c, tsz=tsz: e.matmul(
                pf4[0:1, 0:tsz], oc, xa[:, 0:tsz], start=(c == 0), stop=(c == NC - 1)),
                r=[xb_b, ocb], w=[pf4b], tag="ln_mmA")
            ctx.emit("pe", lambda e, sq=sq, c=c, tsz=tsz: e.matmul(
                pf5[0:1, 0:tsz], oc, sq[:, 0:tsz], start=(c == 0), stop=(c == NC - 1)),
                r=[sq_b, ocb], w=[pf5b], tag="ln_mmB")
        mu = rows[0:1, 0:tsz]
        msq = rows[0:1, 512:512 + tsz]
        ra = rows[0:1, 1536:1536 + tsz]
        rb2 = rows[0:1, 2048:2048 + tsz]
        r_ts(ctx, mu, r_sum, pf4[0:1, 0:tsz], pf4b, 1.0 / D, ALU.mult)
        r_ts(ctx, msq, r_msq, pf5[0:1, 0:tsz], pf5b, 1.0 / D, ALU.mult)
        tmp = rows[0:1, 1024:1024 + tsz]
        tmpb = ctx.buf("v_ln_tmp")
        r_act(ctx, tmp, tmpb, mu, r_sum, AF.Square)
        r_tt(ctx, ra, r_a, msq, r_msq, tmp, tmpb, ALU.subtract)
        r_ts(ctx, ra, r_a, ra, r_a, eps, ALU.add)
        r_recip(ctx, ra, r_a, ra, r_a)
        r_act(ctx, ra, r_a, ra, r_a, AF.Sqrt)
        r_stt(ctx, rb2, r_b, mu, r_sum, -1.0, ra, r_a, ALU.mult, ALU.mult)
        bcast_cols(ctx, "pf2", [((orow, orb), (ra, r_a))], nsz=tsz)
        bcast_cols(ctx, "pf3", [((orow, orb), (rb2, r_b))], nsz=tsz)
        for c in range(NC):
            xa = af[:, c * TT:(c + 1) * TT]
            tmpt = af[:, NC * TT + TT: NC * TT + 2 * TT]
            tmpt_b = ctx.buf("v_ln_tmpt")
            ctx.emit("dve", lambda e, xa=xa, tsz=tsz, tmpt=tmpt: e.tensor_tensor(
                tmpt[:, 0:tsz], xa[:, 0:tsz], pf2[0:128, 0:tsz], ALU.mult),
                r=[xb_b, pf2b], w=[tmpt_b], tag="ln_mul")
            dap = dt[:, dcol(c, t0): dcol(c, t0) + tsz]
            if affine is None:
                ctx.emit("dve", lambda e, tmpt=tmpt, dap=dap, tsz=tsz: e.tensor_tensor(
                    dap, tmpt[:, 0:tsz], pf3[0:128, 0:tsz], ALU.add),
                    r=[tmpt_b, pf3b], w=[db], tag="ln_add")
            else:
                gap, gb = affine[0](c)
                bap, bb = affine[1](c)
                ctx.emit("dve", lambda e, tmpt=tmpt, tsz=tsz: e.tensor_tensor(
                    tmpt[:, 0:tsz], tmpt[:, 0:tsz], pf3[0:128, 0:tsz], ALU.add),
                    r=[tmpt_b, pf3b], w=[tmpt_b], tag="ln_add2")
                ctx.emit("dve", lambda e, tmpt=tmpt, dap=dap, tsz=tsz, gap=gap, bap=bap: e.tensor_scalar(
                    dap, tmpt[:, 0:tsz], gap, bap, ALU.mult, ALU.add),
                    r=[tmpt_b, gb, bb], w=[db], tag="ln_aff")



def dft_consts():
    n = np.arange(D)
    ang = 2.0 * np.pi * np.outer(n, n) / D
    C = np.cos(ang)
    S = np.sin(ang)
    return C.astype(ml_dtypes.bfloat16), S.astype(ml_dtypes.bfloat16)


def emit_layer1(ctx, H, xh, xhb, xh2, xh2b):
    af = ctx.tensors["af"]
    ab = ctx.tensors["ab"]
    rows = ctx.tensors["rows"]
    smalls = ctx.tensors["smalls"]
    smallsb = ctx.tensors["smallsb"]

    wbf = {}
    for nm in ("Wr", "Wk", "Wv", "Ww", "Wo", "Wfr", "Wfi"):
        wbf[nm] = cast_weight(ctx, "l1" + nm, H[nm][0], H[nm][1], D, D)
    bias_b = ctx.buf("v_l1_bias")
    for c in range(NC):
        ctx.load(smalls[:, 280 + c:281 + c], H["bias"][0][c * 128:(c + 1) * 128, 0:1],
                 bias_b, rbufs=[H["bias"][1]], tag="l1_biasld")

    Cb, Sb = dft_consts()
    hC = ctx.const("dftC", Cb)
    hS = ctx.const("dftS", Sb)
    cCb = ctx.buf("c_dftC")
    cSb = ctx.buf("c_dftS")
    bh = np.zeros((D, 16), np.float32)
    for h in range(16):
        bh[h * 64:(h + 1) * 64, h] = 1.0
    hBH = ctx.const("bh16", bh.astype(ml_dtypes.bfloat16))
    cBHb = ctx.buf("c_bh16")
    bpall = np.zeros((16, NC * 128), np.float32)
    for c in range(NC):
        bpall[0, c * 128:c * 128 + 64] = 0
    for c in range(NC):
        for j in range(2):
            bpall[2 * c + j, c * 128 + j * 64: c * 128 + (j + 1) * 64] = 1.0
    hBP = ctx.const("bpall", bpall)

    new_phase(ctx)
    def src_x(c, t0, tsz, dst_ap, dst_b):
        ctx.load(dst_ap[:, 0:tsz], xh[c * 128:(c + 1) * 128, t0:t0 + tsz], dst_b,
                 rbufs=[xhb], tag="ln_ld")
    emit_ln(ctx, src_x)

    if DEBUG:
        xnt0 = ctx.tensors["xn"]
        for c in range(NC):
            tap(ctx, "xn_c%d" % c, xnt0[:, c * T: c * T + 512], ctx.bufs["xn"], (128, 512))

    new_phase(ctx)
    arf_b = ctx.buf("v_l1_arf")
    stg_b = [ctx.buf("v_l1_stg0"), ctx.buf("v_l1_stg1")]
    Arbf = ctx.dram("l1_Arbf", (D, D), BF16, kind=("ExternalOutput" if DEBUG else "Internal")); Arbf_b = ctx.buf("l1_Arbf")
    Aibf = ctx.dram("l1_Aibf", (D, D), BF16, kind=("ExternalOutput" if DEBUG else "Internal")); Aibf_b = ctx.buf("l1_Aibf")
    Mbf = ctx.dram("l1_Mbf", (D, D), BF16, kind=("ExternalOutput" if DEBUG else "Internal")); Mbf_b = ctx.buf("l1_Mbf")

    def build_sum(tag, lhs1, rhs1, lhs2, rhs2, outdram, outdram_b, sign2):
        def evac1(c2, m0, msz, n0, nsz, ps_t, ps_b):
            dst = af[:, (m0 // 128) * 1024 + n0: (m0 // 128) * 1024 + n0 + nsz]
            c2.emit("dve", lambda e: e.tensor_scalar_mul(dst[0:msz, 0:nsz], ps_t[0:msz, 0:nsz], 1.0 / D),
                    r=[ps_b], w=[arf_b], tag=tag + "_ev1")
        linear(ctx, tag=tag + "a", lhsT=lhs1, rhs=rhs1, K=D, M=D, N=D,
               evac=evac1, m_tile=1024, n_tile=512)
        si = [0]
        def evac2(c2, m0, msz, n0, nsz, ps_t, ps_b):
            src = af[:, (m0 // 128) * 1024 + n0: (m0 // 128) * 1024 + n0 + nsz]
            s = si[0] % 2
            si[0] += 1
            dst = ab[:, s * 512: s * 512 + nsz]
            c2.emit("dve", lambda e: e.scalar_tensor_tensor(
                dst[0:msz, 0:nsz], ps_t[0:msz, 0:nsz], sign2 / D, src[0:msz, 0:nsz],
                ALU.mult, ALU.add), r=[ps_b, arf_b], w=[stg_b[s]], tag=tag + "_ev2")
            c2.store(outdram[m0:m0 + msz, n0:n0 + nsz], dst[0:msz, 0:nsz], stg_b[s],
                     wbufs=[outdram_b], q="act", tag=tag + "_st")
        linear(ctx, tag=tag + "b", lhsT=lhs2, rhs=rhs2, K=D, M=D, N=D,
               evac=evac2, m_tile=1024, n_tile=512)

    build_sum("l1Ar", ("dram_km", hC, cCb), ("dram", wbf["Wfr"][0], wbf["Wfr"][1]),
              ("dram_km", hS, cSb), ("dram", wbf["Wfi"][0], wbf["Wfi"][1]), Arbf, Arbf_b, +1.0)
    build_sum("l1Ai", ("dram_km", hC, cCb), ("dram", wbf["Wfi"][0], wbf["Wfi"][1]),
              ("dram_km", hS, cSb), ("dram", wbf["Wfr"][0], wbf["Wfr"][1]), Aibf, Aibf_b, -1.0)

    new_phase(ctx)
    def evacM1(c2, m0, msz, n0, nsz, ps_t, ps_b):
        dst = af[:, (m0 // 128) * 1024 + n0: (m0 // 128) * 1024 + n0 + nsz]
        c2.emit("dve", lambda e: e.tensor_copy(dst[0:msz, 0:nsz], ps_t[0:msz, 0:nsz]),
                r=[ps_b], w=[arf_b], tag="l1M_ev1")
    linear(ctx, tag="l1Ma", lhsT=("dram_mk", Arbf, Arbf_b), rhs=("dram", hC, cCb),
           K=D, M=D, N=D, evac=evacM1, m_tile=1024, n_tile=512)
    sj = [0]
    def evacM2(c2, m0, msz, n0, nsz, ps_t, ps_b):
        src = af[:, (m0 // 128) * 1024 + n0: (m0 // 128) * 1024 + n0 + nsz]
        s = sj[0] % 2
        sj[0] += 1
        dst = ab[:, s * 512: s * 512 + nsz]
        c2.emit("dve", lambda e: e.scalar_tensor_tensor(
            dst[0:msz, 0:nsz], ps_t[0:msz, 0:nsz], -1.0, src[0:msz, 0:nsz],
            ALU.mult, ALU.add), r=[ps_b, arf_b], w=[stg_b[s]], tag="l1M_ev2")
        c2.store(Mbf[m0:m0 + msz, n0:n0 + nsz], dst[0:msz, 0:nsz], stg_b[s],
                 wbufs=[Mbf_b], q="act", tag="l1M_st")
    linear(ctx, tag="l1Mb", lhsT=("dram_mk", Aibf, Aibf_b), rhs=("dram", hS, cSb),
           K=D, M=D, N=D, evac=evacM2, m_tile=1024, n_tile=512)

    wks_b = ctx.buf("v_l1_wks")
    def evac_wks(c2, m0, msz, n0, nsz, ps_t, ps_b):
        dst = smallsb[:, (m0 // 128) * 16 + n0: (m0 // 128) * 16 + n0 + nsz]
        c2.emit("dve", lambda e: e.tensor_copy(dst[0:msz, 0:nsz], ps_t[0:msz, 0:nsz]),
                r=[ps_b], w=[wks_b], tag="l1_wks_ev")
    linear(ctx, tag="l1wks", lhsT=("dram_km", wbf["Wk"][0], wbf["Wk"][1]),
           rhs=("dram", hBH, cBHb), K=D, M=D, N=16, evac=evac_wks,
           m_tile=1024, n_tile=16)

    kap_b = ctx.buf("v_l1_kap")
    xnt = ctx.tensors["xn"]; xnb = ctx.bufs["xn"]
    def evac_kap(c2, m0, msz, n0, nsz, ps_t, ps_b):
        c2.emit("dve", lambda e: e.tensor_copy(rows[32:48, 1024 + n0:1024 + n0 + nsz], ps_t[0:16, 0:nsz]),
                r=[ps_b], w=[kap_b], tag="l1_kap_ev")
    linear(ctx, tag="l1kap",
           lhsT=("sbuf", smallsb, wks_b, lambda k, m: (k // 128) * 16 + m),
           rhs=("sbuf", xnt, xnb, lambda k, n: (k // 128) * T + n),
           K=D, M=16, N=T, evac=evac_kap, m_tile=16, n_tile=512)

    new_phase(ctx)
    ctx.load(rows[32:48, 0:NC * 128], hBP[0:16, :], ctx.buf("v_bpall"), tag="l1_bpld")
    sc32 = ctx.dram("l1_out32", (D, T), F32, kind=("ExternalOutput" if DEBUG else "Internal"))
    sc32_b = ctx.buf("l1_out32")
    HT = 2048
    r_t = af[:, 0:HT]; r_tb = ctx.buf("v_l1_rt")
    w_t = af[:, HT:2 * HT]; w_tb = ctx.buf("v_l1_wt")
    vu_t = af[:, 2 * HT:3 * HT]; vu_tb = ctx.buf("v_l1_vut")
    sg_t = af[:, 3 * HT:4 * HT]; sg_tb = ctx.buf("v_l1_sgt")
    ms_b = ctx.buf("v_l1_ms")
    init_col = af[:, 4 * HT: 4 * HT + 1]
    init_b = ctx.buf("v_l1_init")
    pf2 = ctx.tensors["pf2"]; pf2b = ctx.bufs["pf2"]
    pf4 = ctx.tensors["pf4"]; pf4b = ctx.bufs["pf4"]
    oc, ocb = ones_col(ctx)

    for c in range(NC):
        for hf in range(2):
            nb = hf * HT
            def mk_evac(dst, dstb, func=None, scale=1.0):
                def ev(c2, m0, msz, n0, nsz, ps_t, ps_b):
                    if func is None:
                        c2.emit("act", lambda e: e.copy(dst[:, n0:n0 + nsz], ps_t[0:msz, 0:nsz]),
                                r=[ps_b], w=[dstb], tag="l1_pj_ev")
                    else:
                        c2.emit("act", lambda e: e.activation(dst[:, n0:n0 + nsz], ps_t[0:msz, 0:nsz], func, scale=scale),
                                r=[ps_b], w=[dstb], tag="l1_pj_ev")
                return ev
            rhs = ("sbuf", xnt, xnb, lambda k, n, nb=nb: (k // 128) * T + nb + n)
            linear(ctx, tag="l1_pr", lhsT=("dram_mk", wbf["Wr"][0], wbf["Wr"][1]), rhs=rhs,
                   K=D, M=128, N=HT, evac=mk_evac(r_t, r_tb, AF.Sigmoid),
                   m_tile=128, m_base=c * 128, wslots=("wsl0",), psums=("pf0", "pf1"))
            linear(ctx, tag="l1_pw", lhsT=("dram_mk", wbf["Ww"][0], wbf["Ww"][1]), rhs=rhs,
                   K=D, M=128, N=HT, evac=mk_evac(w_t, w_tb, AF.Sigmoid),
                   m_tile=128, m_base=c * 128, wslots=("wsl1",), psums=("pf0", "pf1"))
            linear(ctx, tag="l1_pv", lhsT=("dram_mk", wbf["Wv"][0], wbf["Wv"][1]), rhs=rhs,
                   K=D, M=128, N=HT, evac=mk_evac(vu_t, vu_tb),
                   m_tile=128, m_base=c * 128, wslots=("wsl0",), psums=("pf0", "pf1"))
            ctx.emit("dve", lambda e: e.tensor_scalar_mul(w_t, w_t, 0.99),
                     r=[w_tb], w=[w_tb], tag="l1_w99")
            for tt in range(HT // TT):
                n0 = tt * TT
                gt0 = nb + n0
                ctx.emit("pe", lambda e, c=c, gt0=gt0: e.matmul(
                    pf2[0:128, 0:TT], rows[32:48, c * 128:(c + 1) * 128],
                    rows[32:48, 1024 + gt0:1024 + gt0 + TT], start=True, stop=True),
                    r=[ctx.buf("v_bpall"), kap_b], w=[pf2b], tag="l1_kapB")
                ctx.emit("dve", lambda e, n0=n0: e.tensor_tensor(
                    vu_t[:, n0:n0 + TT], vu_t[:, n0:n0 + TT], pf2[0:128, 0:TT], ALU.mult),
                    r=[vu_tb, pf2b], w=[vu_tb], tag="l1_umul")
            if hf == 0:
                ctx.emit("dve", lambda e: e.tensor_tensor_scan(
                    sg_t, w_t, vu_t, 0.0, ALU.mult, ALU.add),
                    r=[w_tb, vu_tb], w=[sg_tb], tag="l1_scan")
            else:
                ctx.emit("dve", lambda e: e.tensor_tensor_scan(
                    sg_t, w_t, vu_t, init_col, ALU.mult, ALU.add),
                    r=[w_tb, vu_tb, init_b], w=[sg_tb], tag="l1_scan")
            ctx.emit("dve", lambda e: e.tensor_copy(init_col, sg_t[:, HT - 1:HT]),
                     r=[sg_tb], w=[init_b], tag="l1_init")
            ctx.emit("dve", lambda e: e.tensor_tensor(r_t, r_t, sg_t, ALU.mult),
                     r=[r_tb, sg_tb], w=[r_tb], tag="l1_rout")
            ctx.store(sc32[c * 128:(c + 1) * 128, nb:nb + HT], r_t, r_tb,
                      wbufs=[sc32_b], q="act", tag="l1_spill")
            ctx.emit("act", lambda e: e.activation(w_t, r_t, AF.Square),
                     r=[r_tb], w=[w_tb], tag="l1_sq")
            for tt in range(HT // TT):
                n0 = tt * TT
                gt0 = nb + n0
                ctx.emit("pe", lambda e, n0=n0: e.matmul(
                    pf4[0:1, 0:TT], oc, w_t[:, n0:n0 + TT], start=True, stop=True),
                    r=[w_tb, ocb], w=[pf4b], tag="l1_msmm")
                if c == 0:
                    ctx.emit("dve", lambda e, gt0=gt0: e.tensor_copy(
                        rows[0:1, gt0:gt0 + TT], pf4[0:1, 0:TT]),
                        r=[pf4b], w=[ms_b], tag="l1_msev")
                else:
                    ctx.emit("dve", lambda e, gt0=gt0: e.tensor_tensor(
                        rows[0:1, gt0:gt0 + TT], rows[0:1, gt0:gt0 + TT],
                        pf4[0:1, 0:TT], ALU.add),
                        r=[ms_b, pf4b], w=[ms_b], tag="l1_msacc")

    if DEBUG:
        tap(ctx, "kap", ctx.tensors["rows"][32:48, 1024:1024 + 512], kap_b, (16, 512))
        tap(ctx, "ms", ctx.tensors["rows"][0:1, 0:4096], ms_b, (1, 4096))

    eps32 = float(np.finfo(np.float32).eps)
    arow = rows[0:1, 0:T]
    arow_b = ctx.buf("v_l1_arow")
    for t0, tsz in chunks_of(T, TT):
        r_ts(ctx, arow[:, t0:t0 + tsz], arow_b, rows[0:1, t0:t0 + tsz], ms_b,
             1.0 / D, ALU.mult, eps32, ALU.add)
        r_recip(ctx, arow[:, t0:t0 + tsz], arow_b, arow[:, t0:t0 + tsz], arow_b)
        r_act(ctx, arow[:, t0:t0 + tsz], arow_b, arow[:, t0:t0 + tsz], arow_b, AF.Sqrt)

    if DEBUG:
        tap(ctx, "arow", arow[:, 0:4096], arow_b, (1, 4096))

    new_phase(ctx)
    wsl0 = ctx.tensors["wsl0"]; wsl0b = ctx.bufs["wsl0"]
    wsl1 = ctx.tensors["wsl1"]; wsl1b = ctx.bufs["wsl1"]
    for k in range(NC):
        ctx.emit("sp", lambda e, k=k: e.dma_start(
            wsl0[:, k * 1024:(k + 1) * 1024], wbf["Wo"][0][0:D, k * 128:(k + 1) * 128],
            transpose=True), r=[wbf["Wo"][1]], w=[wsl0b], dma=True, tag="l1_woT")
        ctx.emit("sp", lambda e, k=k: e.dma_start(
            wsl1[:, k * 1024:(k + 1) * 1024], Mbf[k * 128:(k + 1) * 128, 0:D]),
            r=[Mbf_b], w=[wsl1b], dma=True, tag="l1_mld")
    orow, orb = ones_row(ctx, 128)
    pcyc = [("pf0", "pf1"), ]
    normo_b = ctx.buf("v_l1_normo")
    xres_b = ctx.buf("v_l1_xres")
    tmp_b = ctx.buf("v_l1_tmp2")
    out_b = ctx.buf("v_l1_outt")
    for ti in range(NTT):
        t0 = ti * TT
        bcast_cols(ctx, "pf2", [((orow, orb), (arow[:, t0:t0 + TT], arow_b))], nsz=TT)
        for c in range(NC):
            xa = af[:, c * TT:(c + 1) * TT]
            ctx.load(xa, sc32[c * 128:(c + 1) * 128, t0:t0 + TT], ctx.buf("v_l1_oin"),
                     rbufs=[sc32_b], tag="l1_tl_ld")
            ctx.emit("dve", lambda e, xa=xa, c=c: e.tensor_tensor(
                ab[:, c * TT:(c + 1) * TT], xa, pf2[0:128, 0:TT], ALU.mult),
                r=[ctx.buf("v_l1_oin"), pf2b], w=[normo_b], tag="l1_tl_norm")
        for m in range(NC):
            psn = ("pf0", "pf1")[m % 2]
            ps_t = ctx.tensors[psn]; ps_b = ctx.bufs[psn]
            def mmg(e, m=m, t0=t0, ps_t=ps_t):
                inst = None
                for k in range(NC):
                    inst = e.matmul(ps_t[0:128, 0:TT],
                                    wsl0[:, k * 1024 + m * 128: k * 1024 + (m + 1) * 128],
                                    ab[:, k * TT:(k + 1) * TT],
                                    start=(k == 0), stop=False)
                for k in range(NC):
                    inst = e.matmul(ps_t[0:128, 0:TT],
                                    wsl1[:, k * 1024 + m * 128: k * 1024 + (m + 1) * 128],
                                    xnt[:, k * T + t0: k * T + t0 + TT],
                                    start=False, stop=(k == NC - 1))
                return inst
            ctx.emit("pe", mmg, r=[wsl0b, wsl1b, normo_b, xnb], w=[ps_b], tag="l1_tl_mm")
            xr = af[:, NC * TT: NC * TT + TT]
            ctx.load(xr, xh[m * 128:(m + 1) * 128, t0:t0 + TT], xres_b, rbufs=[xhb], tag="l1_tl_x")
            tmp = af[:, NC * TT + TT: NC * TT + 2 * TT]
            ctx.emit("act", lambda e, tmp=tmp, ps_t=ps_t, m=m: e.activation(
                tmp, ps_t[0:128, 0:TT], AF.Identity, bias=smalls[:, 280 + m:281 + m]),
                r=[ps_b, bias_b], w=[tmp_b], tag="l1_tl_bias")
            outt = af[:, NC * TT + 2 * TT: NC * TT + 3 * TT]
            ctx.emit("dve", lambda e, outt=outt, tmp=tmp, xr=xr: e.tensor_tensor(
                outt, tmp, xr, ALU.add), r=[tmp_b, xres_b], w=[out_b], tag="l1_tl_add")
            ctx.store(xh2[m * 128:(m + 1) * 128, t0:t0 + TT], outt, out_b,
                      wbufs=[xh2b], q="act", tag="l1_tl_st")



def finish(ctx):
    deps = ctx.p.fence_deps()
    ctx.emit("sp", lambda e: e.nop(), deps=deps, tag="final_fence")
    ctx.p.finalize()


def build_l1_program():
    nc = bass.Bass("TRN2", target_bir_lowering=False)
    with ExitStack() as stk:
        ctx = Ctx(nc, stk)
        xin = ctx.dram("x_in", (T, D), F32, kind="ExternalInput")
        yout = ctx.dram("y_out", (T, D), F32, kind="ExternalOutput")
        H = {}
        for nm, shape in [("Wr", (D, D)), ("Wk", (D, D)), ("Wv", (D, D)), ("Ww", (D, D)),
                          ("Wo", (D, D)), ("Wfr", (D, D)), ("Wfi", (D, D)), ("bias", (D, 1))]:
            h = ctx.dram("i_" + nm, shape, F32, kind="ExternalInput")
            H[nm] = (h, ctx.buf("i_" + nm))
        xh = ctx.dram("xh", (D, T), F32); xhb = ctx.buf("xh")
        xh2 = ctx.dram("xh2", (D, T), F32); xh2b = ctx.buf("xh2")
        alloc_standard(ctx)
        emit_load_smalls(ctx)
        emit_transpose_in(ctx, xin, ctx.buf("x_in"), xh, xhb)
        new_phase(ctx)
        emit_layer1(ctx, H, xh, xhb, xh2, xh2b)
        new_phase(ctx)
        emit_transpose_out(ctx, xh2, xh2b, yout, ctx.buf("y_out"))
        finish(ctx)
    return nc


DEBUG = False

def tap(ctx, name, src_ap, src_buf, shape):
    if not DEBUG:
        return
    h = ctx.dram("tap_" + name, shape, F32, kind="ExternalOutput")
    ctx.store(h[0:shape[0], 0:shape[1]], src_ap, src_buf, wbufs=[ctx.buf("tap_" + name)],
              q="pool", tag="tap_" + name)



def l2_mask_consts():
    NEG = -1e30
    out = []
    for qs in range(2):
        i = (np.arange(128) + qs * 128)[:, None]
        cc = np.arange(256)[None, :]
        m = np.where(cc < 256 - i, NEG, 0.0)
        out.append(m.astype(ml_dtypes.bfloat16))
    for qs in range(2):
        i = (np.arange(128) + qs * 128)[:, None]
        jb = np.arange(256)[None, :]
        m = np.where(jb >= 255 - i, NEG, 0.0)
        out.append(m.astype(ml_dtypes.bfloat16))
    return np.stack(out)


def emit_ternary_prep(ctx, tag, src, src_b, rows_n, cols_n, outdram, outdram_b):
    af = ctx.tensors["af"]
    ab = ctx.tensors["ab"]
    smalls = ctx.tensors["smalls"]
    oc, ocb = ones_col(ctx)
    orow, orb = ones_row(ctx, 128)
    pf4 = ctx.tensors["pf4"]; pf4b = ctx.bufs["pf4"]
    wt_b = ctx.buf("v_tp_w")
    acc_col = af[:, 0:1]
    acc_b = ctx.buf("v_tp_acc")
    red_col = af[:, 1:2]
    red_b = ctx.buf("v_tp_red")
    thr_col = smalls[:, 300:301]
    thr_b = ctx.buf("v_tp_thr")
    trow = ctx.tensors["rows"][0:1, 4096:4097]
    trow_b = ctx.buf("v_tp_trow")
    first = True
    for r0 in range(0, rows_n, 128):
        for c0 in range(0, cols_n, 1024):
            csz = min(1024, cols_n - c0)
            wt = af[:, 2048:2048 + csz]
            ctx.load(wt[:, 0:csz], src[r0:r0 + 128, c0:c0 + csz], wt_b, rbufs=[src_b], tag=tag + "_p1ld")
            ctx.emit("dve", lambda e, wt=wt, csz=csz: e.reduce_sum(
                red_col, wt[:, 0:csz], axis=AX.X, apply_absolute_value=True),
                r=[wt_b], w=[red_b], tag=tag + "_red")
            if first:
                ctx.emit("dve", lambda e: e.tensor_copy(acc_col, red_col),
                         r=[red_b], w=[acc_b], tag=tag + "_acc0")
                first = False
            else:
                ctx.emit("dve", lambda e: e.tensor_add(acc_col, acc_col, red_col),
                         r=[acc_b, red_b], w=[acc_b], tag=tag + "_acc")
    ctx.emit("pe", lambda e: e.matmul(pf4[0:1, 0:1], oc, acc_col, start=True, stop=True),
             r=[acc_b, ocb], w=[pf4b], tag=tag + "_tot")
    r_ts(ctx, trow, trow_b, pf4[0:1, 0:1], pf4b, 0.5 / (rows_n * cols_n), ALU.mult)
    ctx.emit("pe", lambda e: e.matmul(pf4[0:128, 0:1], orow, trow, start=True, stop=True),
             r=[trow_b, orb], w=[pf4b], tag=tag + "_thrB")
    ctx.emit("dve", lambda e: e.tensor_copy(thr_col, pf4[0:128, 0:1]),
             r=[pf4b], w=[thr_b], tag=tag + "_thrc")
    si = [0]
    for r0 in range(0, rows_n, 128):
        for c0 in range(0, cols_n, 1024):
            csz = min(1024, cols_n - c0)
            wt = af[:, 2048:2048 + csz]
            u = af[:, 4096:4096 + csz]
            u_b = ctx.buf("v_tp_u")
            sg = af[:, 6144:6144 + csz]
            sg_b = ctx.buf("v_tp_sg")
            st = ab[:, (si[0] % 2) * 1024: (si[0] % 2) * 1024 + csz]
            st_b = ctx.buf("v_tp_st" + str(si[0] % 2))
            si[0] += 1
            ctx.load(wt[:, 0:csz], src[r0:r0 + 128, c0:c0 + csz], wt_b, rbufs=[src_b], tag=tag + "_p2ld")
            ctx.emit("act", lambda e, wt=wt, u=u, csz=csz: e.activation(
                u[:, 0:csz], wt[:, 0:csz], AF.Abs),
                r=[wt_b], w=[u_b], tag=tag + "_uabs")
            ctx.emit("dve", lambda e, u=u, csz=csz: e.tensor_single_scalar(
                u[:, 0:csz], u[:, 0:csz], thr_col[:, 0:1], ALU.is_gt),
                r=[u_b, thr_b], w=[u_b], tag=tag + "_u")
            ctx.emit("act", lambda e, wt=wt, sg=sg, csz=csz: e.activation(
                sg[:, 0:csz], wt[:, 0:csz], AF.Sign),
                r=[wt_b], w=[sg_b], tag=tag + "_sg")
            ctx.emit("dve", lambda e, u=u, sg=sg, st=st, csz=csz: e.tensor_tensor(
                st[:, 0:csz], u[:, 0:csz], sg[:, 0:csz], ALU.mult),
                r=[u_b, sg_b], w=[st_b], tag=tag + "_wq")
            ctx.store(outdram[r0:r0 + 128, c0:c0 + csz], st[:, 0:csz], st_b,
                      wbufs=[outdram_b], q="act", tag=tag + "_st")


def emit_rms_norm_spill(ctx, tag, n_tok, dst_dram, dst_b, eps=1e-8):
    af = ctx.tensors["af"]
    ab = ctx.tensors["ab"]
    rows = ctx.tensors["rows"]
    xnt = ctx.tensors["xn"]; xnb = ctx.bufs["xn"]
    oc, ocb = ones_col(ctx)
    orow, orb = ones_row(ctx, 128)
    pf4 = ctx.tensors["pf4"]; pf4b = ctx.bufs["pf4"]
    pf2 = ctx.tensors["pf2"]; pf2b = ctx.bufs["pf2"]
    sq_b = ctx.buf("v_rn_sq")
    r_r = rows[0:1, 4200:4200 + 512]
    r_rb = ctx.buf("v_rn_r")
    st_b = [ctx.buf("v_rn_st0"), ctx.buf("v_rn_st1")]
    for ti, (t0, tsz) in enumerate(chunks_of(n_tok, TT)):
        for c in range(NC):
            sq = af[:, 0:TT]
            ctx.emit("act", lambda e, c=c, t0=t0, tsz=tsz, sq=sq: e.activation(
                sq[:, 0:tsz], xnt[:, c * T + t0: c * T + t0 + tsz], AF.Square),
                r=[xnb], w=[sq_b], tag=tag + "_sq")
            ctx.emit("pe", lambda e, sq=sq, c=c, tsz=tsz: e.matmul(
                pf4[0:1, 0:tsz], oc, sq[:, 0:tsz], start=(c == 0), stop=(c == NC - 1)),
                r=[sq_b, ocb], w=[pf4b], tag=tag + "_mm")
        r_act(ctx, r_r[:, 0:tsz], r_rb, pf4[0:1, 0:tsz], ctx.bufs["pf4"], AF.Sqrt)
        r_ts(ctx, r_r[:, 0:tsz], r_rb, r_r[:, 0:tsz], r_rb, 1.0 / math.sqrt(D), ALU.mult, eps, ALU.add)
        r_recip(ctx, r_r[:, 0:tsz], r_rb, r_r[:, 0:tsz], r_rb)
        bcast_cols(ctx, "pf2", [((orow, orb), (r_r[:, 0:tsz], r_rb))], nsz=tsz)
        for c in range(NC):
            st = ab[:, (c % 2) * TT: (c % 2) * TT + TT]
            ctx.emit("dve", lambda e, c=c, t0=t0, tsz=tsz, st=st: e.tensor_tensor(
                st[:, 0:tsz], xnt[:, c * T + t0:c * T + t0 + tsz], pf2[0:128, 0:tsz], ALU.mult),
                r=[xnb, pf2b], w=[st_b[c % 2]], tag=tag + "_mul")
            ctx.store(dst_dram[c * 128:(c + 1) * 128, t0:t0 + tsz], st[:, 0:tsz],
                      st_b[c % 2], wbufs=[dst_b], q="act", tag=tag + "_st")


def emit_layer2(ctx, H, xh, xhb, xh2, xh2b):
    af = ctx.tensors["af"]
    ab = ctx.tensors["ab"]
    rows = ctx.tensors["rows"]
    smalls = ctx.tensors["smalls"]
    xnt = ctx.tensors["xn"]; xnb = ctx.bufs["xn"]
    oc, ocb = ones_col(ctx)
    orow, orb = ones_row(ctx, 128)
    DFF = 4096
    W_ = 256

    wbf = {}
    for nm in ("Wq", "Wk", "Wv"):
        wbf[nm] = cast_weight(ctx, "l2" + nm, H[nm][0], H[nm][1], D, D)
    wqu = ctx.dram("l2_wqu", (DFF, D), BF16); wqu_b = ctx.buf("l2_wqu")
    wqd = ctx.dram("l2_wqd", (D, DFF), BF16); wqd_b = ctx.buf("l2_wqd")
    emit_ternary_prep(ctx, "l2tu", H["w_up"][0], H["w_up"][1], DFF, D, wqu, wqu_b)
    emit_ternary_prep(ctx, "l2td", H["w_down"][0], H["w_down"][1], D, DFF, wqd, wqd_b)

    bias_b = ctx.buf("v_l2_bias")
    for c in range(NC):
        ctx.load(smalls[:, 280 + c:281 + c], H["bq"][0][c * 128:(c + 1) * 128, 0:1],
                 bias_b, rbufs=[H["bq"][1]], tag="l2_bq")
        ctx.load(smalls[:, 288 + c:289 + c], H["bk"][0][c * 128:(c + 1) * 128, 0:1],
                 bias_b, rbufs=[H["bk"][1]], tag="l2_bk")
    ctx.emit("dve", lambda e: e.tensor_scalar_mul(smalls[:, 280:288], smalls[:, 280:288], 0.125),
             r=[bias_b], w=[bias_b], tag="l2_bqs")
    bv_row = rows[0:1, 4500:4500 + D]
    bv_b = ctx.buf("v_l2_bv")
    ctx.emit("sp", lambda e: e.dma_start(bv_row, H["bv"][0].ap().rearrange("d one -> one d")),
             r=[H["bv"][1]], w=[bv_b], dma=True, tag="l2_bv")

    mk = l2_mask_consts()
    hMK = ctx.const("l2masks", mk.reshape(4 * 128, 256))
    mk_sb = ab[:, 9216:9216 + 1024]
    mk_b = ctx.buf("v_l2_mk")

    new_phase(ctx)
    def src_x(c, t0, tsz, dst_ap, dst_b):
        ctx.load(dst_ap[:, 0:tsz], xh[c * 128:(c + 1) * 128, t0:t0 + tsz], dst_b,
                 rbufs=[xhb], tag="l2ln_ld")
    emit_ln(ctx, src_x)

    new_phase(ctx)
    qT = ctx.dram("l2_qT", (D, T), BF16); qT_b = ctx.buf("l2_qT")
    kT = ctx.dram("l2_kT", (D, T), BF16); kT_b = ctx.buf("l2_kT")
    vtok = ctx.dram("l2_v", (T, D), BF16); vtok_b = ctx.buf("l2_v")
    stg_b = [ctx.buf("v_l2_st0"), ctx.buf("v_l2_st1")]
    sj = [0]

    def mk_proj_evac(dst, dst_b, bias_base, scale):
        def ev(c2, m0, msz, n0, nsz, ps_t, ps_b):
            s = sj[0] % 2; sj[0] += 1
            st = ab[:, s * 512: s * 512 + nsz]
            mchunk = m0 // 128
            c2.emit("act", lambda e: e.activation(
                st[0:msz, 0:nsz], ps_t[0:msz, 0:nsz], AF.Identity,
                bias=smalls[:, bias_base + mchunk:bias_base + mchunk + 1], scale=scale),
                r=[ps_b, bias_b], w=[stg_b[s]], tag="l2pj_ev")
            c2.store(dst[m0:m0 + msz, n0:n0 + nsz], st[0:msz, 0:nsz], stg_b[s],
                     wbufs=[dst_b], q="act", tag="l2pj_st")
        return ev
    rhs_xn = ("sbuf", xnt, xnb, lambda k, n: (k // 128) * T + n)
    linear(ctx, tag="l2q", lhsT=("dram_mk", wbf["Wq"][0], wbf["Wq"][1]), rhs=rhs_xn,
           K=D, M=D, N=T, evac=mk_proj_evac(qT, qT_b, 280, 0.125), m_tile=1024)
    linear(ctx, tag="l2k", lhsT=("dram_mk", wbf["Wk"][0], wbf["Wk"][1]), rhs=rhs_xn,
           K=D, M=D, N=T, evac=mk_proj_evac(kT, kT_b, 288, 1.0), m_tile=1024)
    wsl0 = ctx.tensors["wsl0"]; wsl0b = ctx.bufs["wsl0"]
    for k in range(NC):
        ctx.emit("sp", lambda e, k=k: e.dma_start(
            wsl0[:, k * 1024:(k + 1) * 1024], wbf["Wv"][0][0:D, k * 128:(k + 1) * 128],
            transpose=True), r=[wbf["Wv"][1]], w=[wsl0b], dma=True, tag="l2_wvT")
    pcyc = ("pf0", "pf1")
    pi = [0]
    for tc in range(T // 128):
        for ot in range(2):
            psn = pcyc[pi[0] % 2]; pi[0] += 1
            ps_t = ctx.tensors[psn]; ps_b = ctx.bufs[psn]
            def mmv(e, tc=tc, ot=ot, ps_t=ps_t):
                inst = None
                for k in range(NC):
                    inst = e.matmul(ps_t[0:128, 0:512],
                                    xnt[:, k * T + tc * 128: k * T + (tc + 1) * 128],
                                    wsl0[:, k * 1024 + ot * 512: k * 1024 + ot * 512 + 512],
                                    start=(k == 0), stop=False)
                inst = e.matmul(ps_t[0:128, 0:512], orow,
                                bv_row[:, ot * 512: ot * 512 + 512], start=False, stop=True)
                return inst
            ctx.emit("pe", mmv, r=[xnb, wsl0b, orb, bv_b], w=[ps_b], tag="l2v_mm")
            s = sj[0] % 2; sj[0] += 1
            st = ab[:, s * 512: s * 512 + 512]
            ctx.emit("act", lambda e, st=st, ps_t=ps_t: e.copy(st, ps_t[0:128, 0:512]),
                     r=[ps_b], w=[stg_b[s]], tag="l2v_ev")
            ctx.store(vtok[tc * 128:(tc + 1) * 128, ot * 512:ot * 512 + 512], st,
                      stg_b[s], wbufs=[vtok_b], q="act", tag="l2v_st")

    new_phase(ctx)
    for vv_ in range(4):
        ctx.load(mk_sb[0:128, vv_ * 256:(vv_ + 1) * 256], hMK[vv_ * 128:(vv_ + 1) * 128, :],
                 mk_b, tag="l2_mkld")
    y2 = ctx.dram("l2_att", (D, T), F32); y2_b = ctx.buf("l2_att")
    AB_V = 0
    AB_K = 3072
    AB_Q = 6144
    AB_E = 7168
    AB_PT = 7936
    v_b = ctx.buf("v_l2a_v")
    k_b = ctx.buf("v_l2a_k")
    q_b = ctx.buf("v_l2a_q")
    e_b = ctx.buf("v_l2a_e")
    pt_b = ctx.buf("v_l2a_pt")
    o_b = ctx.buf("v_l2a_o")
    col_b = ctx.buf("v_l2a_cols")
    identb = ctx.tensors["identb"]; identb_b = ctx.bufs["identb"]
    pf2 = ctx.tensors["pf2"]; pf2b = ctx.bufs["pf2"]
    pf3 = ctx.tensors["pf3"]; pf3b = ctx.bufs["pf3"]
    pb = ctx.tensors["pb"]; pb_b = ctx.bufs["pb"]

    NCH = T // W_
    for c in range(NCH):
        kv0 = max(0, (c - 1) * W_)
        kv1 = min(T, (c + 2) * W_)
        for g in range(2):
            for tc6 in range((kv1 - kv0) // 128):
                ctx.load(ab[:, AB_V + tc6 * 512: AB_V + (tc6 + 1) * 512],
                         vtok[kv0 + tc6 * 128: kv0 + (tc6 + 1) * 128, g * 512:(g + 1) * 512],
                         v_b, rbufs=[vtok_b], tag="l2a_vld")
            for cc in range(4):
                ch = g * 4 + cc
                ctx.load(ab[:, AB_K + cc * 768: AB_K + cc * 768 + (kv1 - kv0)],
                         kT[ch * 128:(ch + 1) * 128, kv0:kv1], k_b, rbufs=[kT_b], tag="l2a_kld")
                ctx.load(ab[:, AB_Q + cc * 256: AB_Q + (cc + 1) * 256],
                         qT[ch * 128:(ch + 1) * 128, c * W_:(c + 1) * W_], q_b,
                         rbufs=[qT_b], tag="l2a_qld")
            for hh in range(8):
                h = g * 8 + hh
                cc = hh // 2
                hrow = (hh % 2) * 64
                for qs in range(2):
                    qap = ab[hrow:hrow + 64, AB_Q + cc * 256 + qs * 128: AB_Q + cc * 256 + qs * 128 + 128]
                    def kcol(gk, cc=cc, kv0=kv0):
                        return AB_K + cc * 768 + (gk - kv0)
                    SA0 = (c - 1) * W_
                    def mmSA(e, qap=qap, cc=cc, c=c, kcol=kcol, SA0=SA0, hrow=hrow):
                        inst = None
                        if c == 0:
                            inst = e.matmul(pf2[0:128, 256:512],
                                            qap, ab[hrow:hrow + 64, kcol(0):kcol(0) + 256],
                                            start=True, stop=True)
                        else:
                            inst = e.matmul(pf2[0:128, 0:512],
                                            qap, ab[hrow:hrow + 64, kcol(SA0):kcol(SA0) + 512],
                                            start=True, stop=True)
                        return inst
                    ctx.emit("pe", mmSA, r=[q_b, k_b], w=[pf2b], tag="l2a_mmA")
                    if c == 0:
                        ctx.emit("dve", lambda e, qs=qs: e.tensor_copy(
                            pf2[0:128, 0:256], mk_sb[0:128, qs * 256:(qs + 1) * 256]),
                            r=[mk_b], w=[pf2b], tag="l2a_mk0")
                    if c == NCH - 1:
                        ctx.emit("dve", lambda e, qs=qs: e.tensor_copy(
                            pf3[0:128, 0:256], mk_sb[0:128, 512 + qs * 256: 512 + (qs + 1) * 256]),
                            r=[mk_b], w=[pf3b], tag="l2a_mk15")
                    else:
                        NB0 = (c + 1) * W_
                        ctx.emit("pe", lambda e, qap=qap, kcol=kcol, NB0=NB0, hrow=hrow: e.matmul(
                            pf3[0:128, 0:256], qap, ab[hrow:hrow + 64, kcol(NB0):kcol(NB0) + 256],
                            start=True, stop=True), r=[q_b, k_b], w=[pf3b], tag="l2a_mmB")
                    c_m1 = af[:, 0:1]; c_m2 = af[:, 1:2]; c_nm = af[:, 2:3]
                    c_d1 = af[:, 3:4]; c_d2 = af[:, 4:5]; c_rd = af[:, 5:6]
                    ctx.emit("dve", lambda e: e.reduce_max(c_m1, pf2[0:128, 0:512], axis=AX.X, negate=True),
                             r=[pf2b], w=[col_b], tag="l2a_r1")
                    ctx.emit("dve", lambda e: e.reduce_max(c_m2, pf3[0:128, 0:256], axis=AX.X, negate=True),
                             r=[pf3b], w=[col_b], tag="l2a_r2")
                    ctx.emit("dve", lambda e: e.tensor_tensor(c_nm, c_m1, c_m2, ALU.min),
                             r=[col_b], w=[col_b], tag="l2a_nm")
                    ctx.emit("act", lambda e: e.activation(
                        ab[:, AB_E:AB_E + 512], pf2[0:128, 0:512], AF.Exp,
                        bias=c_nm, accum_out=c_d1), r=[pf2b, col_b], w=[e_b, col_b], tag="l2a_eA")
                    ctx.emit("act", lambda e: e.activation(
                        ab[:, AB_E + 512:AB_E + 768], pf3[0:128, 0:256], AF.Exp,
                        bias=c_nm, accum_out=c_d2), r=[pf3b, col_b], w=[e_b, col_b], tag="l2a_eB")
                    ctx.emit("dve", lambda e: e.tensor_add(c_rd, c_d1, c_d2),
                             r=[col_b], w=[col_b], tag="l2a_dd")
                    ctx.emit("dve", lambda e: e.reciprocal(c_rd, c_rd),
                             r=[col_b], w=[col_b], tag="l2a_rec")
                    ctx.emit("dve", lambda e: e.tensor_scalar_mul(
                        ab[:, AB_E:AB_E + 768], ab[:, AB_E:AB_E + 768], c_rd[:, 0:1]),
                        r=[e_b, col_b], w=[e_b], tag="l2a_pn")
                    for b6 in range(6):
                        ctx.emit("pe", lambda e, b6=b6: e.transpose(
                            pb[0:128, 0:128], ab[:, AB_E + b6 * 128: AB_E + (b6 + 1) * 128],
                            identb[:]), r=[e_b, identb_b], w=[pb_b], tag="l2a_tr")
                        eng = "dve" if b6 % 2 == 0 else "act"
                        if eng == "dve":
                            ctx.emit("dve", lambda e, b6=b6: e.tensor_copy(
                                ab[:, AB_PT + b6 * 128: AB_PT + (b6 + 1) * 128], pb[0:128, 0:128]),
                                r=[pb_b], w=[pt_b], tag="l2a_trev")
                        else:
                            ctx.emit("act", lambda e, b6=b6: e.copy(
                                ab[:, AB_PT + b6 * 128: AB_PT + (b6 + 1) * 128], pb[0:128, 0:128]),
                                r=[pb_b], w=[pt_b], tag="l2a_trev")
                    def mmav(e, c=c, hh=hh, kv0=kv0, kv1=kv1):
                        inst = None
                        first = True
                        for tc6 in range(6):
                            gk = (c - 1) * W_ + tc6 * 128
                            if gk < 0 or gk >= T:
                                continue
                            vslot = (gk - kv0) // 128
                            inst = e.matmul(
                                pf2[0:64, 0:128],
                                ab[:, AB_V + vslot * 512 + (hh % 8) * 64: AB_V + vslot * 512 + (hh % 8) * 64 + 64],
                                ab[:, AB_PT + tc6 * 128: AB_PT + (tc6 + 1) * 128],
                                start=first, stop=(tc6 == 5 or gk + 128 >= min(kv1, (c + 2) * W_)))
                            first = False
                        return inst
                    ctx.emit("pe", mmav, r=[v_b, pt_b], w=[pf2b], tag="l2a_av")
                    ost = af[:, 8 + (sj[0] % 2) * 128: 8 + (sj[0] % 2) * 128 + 128]
                    osb = stg_b[sj[0] % 2]; sj[0] += 1
                    ctx.emit("act", lambda e, ost=ost: e.copy(ost[0:64, 0:128], pf2[0:64, 0:128]),
                             r=[pf2b], w=[osb], tag="l2a_oev")
                    ctx.store(y2[h * 64:(h + 1) * 64, c * W_ + qs * 128: c * W_ + qs * 128 + 128],
                              ost[0:64, 0:128], osb, wbufs=[y2_b], q="act", tag="l2a_ost")

    new_phase(ctx)
    xnorm = ctx.dram("l2_xnorm", (D, T), BF16); xnorm_b = ctx.buf("l2_xnorm")
    emit_rms_norm_spill(ctx, "l2rn", T, xnorm, xnorm_b)
    new_phase(ctx)
    hbuf = ctx.dram("l2_h", (DFF, T), BF16); hbuf_b = ctx.buf("l2_h")
    ms2_b = ctx.buf("v_l2_ms2")
    pf4 = ctx.tensors["pf4"]; pf4b = ctx.bufs["pf4"]
    first_ms = [True] * NTT
    sq2_b = ctx.buf("v_l2_sq2")

    def evac_up(c2, m0, msz, n0, nsz, ps_t, ps_b):
        s = sj[0] % 2; sj[0] += 1
        st = ab[:, s * 512: s * 512 + nsz]
        c2.emit("act", lambda e: e.activation(st[0:msz, 0:nsz], ps_t[0:msz, 0:nsz], AF.Gelu),
                r=[ps_b], w=[stg_b[s]], tag="l2up_ev")
        c2.store(hbuf[m0:m0 + msz, n0:n0 + nsz], st[0:msz, 0:nsz], stg_b[s],
                 wbufs=[hbuf_b], q="act", tag="l2up_st")
        sq = af[:, 1024:1024 + nsz]
        c2.emit("act", lambda e: e.activation(sq[0:msz, 0:nsz], st[0:msz, 0:nsz], AF.Square),
                r=[stg_b[s]], w=[sq2_b], tag="l2up_sq")
        c2.emit("pe", lambda e: e.matmul(pf4[0:1, 0:nsz], oc[0:msz, 0:1], sq[0:msz, 0:nsz],
                                         start=True, stop=True),
                r=[sq2_b, ocb], w=[pf4b], tag="l2up_msmm")
        ti = n0 // TT
        if first_ms[ti]:
            first_ms[ti] = False
            c2.emit("dve", lambda e: e.tensor_copy(rows[0:1, n0:n0 + nsz], pf4[0:1, 0:nsz]),
                    r=[pf4b], w=[ms2_b], tag="l2up_ms0")
        else:
            c2.emit("dve", lambda e: e.tensor_tensor(rows[0:1, n0:n0 + nsz],
                                                     rows[0:1, n0:n0 + nsz], pf4[0:1, 0:nsz], ALU.add),
                    r=[ms2_b, pf4b], w=[ms2_b], tag="l2up_msa")

    linear(ctx, tag="l2up", lhsT=("dram_mk", wqu, wqu_b), rhs=("dram", xnorm, xnorm_b),
           K=D, M=DFF, N=T, evac=evac_up, m_tile=1024)

    r2_b = ctx.buf("v_l2_r2")
    for t0, tsz in chunks_of(T, TT):
        r_act(ctx, rows[0:1, t0:t0 + tsz], r2_b, rows[0:1, t0:t0 + tsz], ms2_b, AF.Sqrt)
        r_ts(ctx, rows[0:1, t0:t0 + tsz], r2_b, rows[0:1, t0:t0 + tsz], r2_b,
             1.0 / math.sqrt(DFF), ALU.mult, 1e-8, ALU.add)
        r_recip(ctx, rows[0:1, t0:t0 + tsz], r2_b, rows[0:1, t0:t0 + tsz], r2_b)

    new_phase(ctx)
    l2acc = ctx.dram("l2_acc", (D, T), F32); l2acc_b = ctx.buf("l2_acc")
    for kq in range(4):
        def evac_dn(c2, m0, msz, n0, nsz, ps_t, ps_b, kq=kq):
            s = sj[0] % 2; sj[0] += 1
            st = af[:, 1024 + s * 512: 1024 + s * 512 + nsz]
            c2.emit("act", lambda e: e.copy(st[0:msz, 0:nsz], ps_t[0:msz, 0:nsz]),
                    r=[ps_b], w=[stg_b[s]], tag="l2dn_ev")
            if kq == 0:
                c2.store(l2acc[m0:m0 + msz, n0:n0 + nsz], st[0:msz, 0:nsz], stg_b[s],
                         wbufs=[l2acc_b], q="pool", tag="l2dn_st")
            else:
                c2.emit("pool", lambda e: e.dma_start(
                    l2acc[m0:m0 + msz, n0:n0 + nsz], st[0:msz, 0:nsz], accum_op=ALU.add),
                    r=[stg_b[s]], w=[l2acc_b], dma=True, tag="l2dn_acc")
        linear(ctx, tag="l2dn%d" % kq, lhsT=("dram_mk", wqd, wqd_b), rhs=("dram", hbuf, hbuf_b),
               K=1024, M=D, N=T, evac=evac_dn, m_tile=1024, k_base=kq * 1024)

    new_phase(ctx)
    xr_b = ctx.buf("v_l2_xr")
    at_b = ctx.buf("v_l2_at")
    ac_b = ctx.buf("v_l2_ac")
    ot_b = ctx.buf("v_l2_ot")
    for ti in range(NTT):
        t0 = ti * TT
        bcast_cols(ctx, "pf2", [((orow, orb), (rows[0:1, t0:t0 + TT], r2_b))], nsz=TT)
        for m in range(NC):
            xr = af[:, 0:TT]
            at = af[:, TT:2 * TT]
            ac = af[:, 2 * TT:3 * TT]
            ot = af[:, 3 * TT:4 * TT]
            ctx.load(xr, xh[m * 128:(m + 1) * 128, t0:t0 + TT], xr_b, rbufs=[xhb], tag="l2t_x")
            ctx.load(at, y2[m * 128:(m + 1) * 128, t0:t0 + TT], at_b, rbufs=[y2_b], tag="l2t_a")
            ctx.load(ac, l2acc[m * 128:(m + 1) * 128, t0:t0 + TT], ac_b, rbufs=[l2acc_b], tag="l2t_c")
            ctx.emit("dve", lambda e, ac=ac, ot=ot: e.tensor_tensor(ot, ac, pf2[0:128, 0:TT], ALU.mult),
                     r=[ac_b, pf2b], w=[ot_b], tag="l2t_m")
            ctx.emit("dve", lambda e, ot=ot, xr=xr: e.tensor_tensor(ot, ot, xr, ALU.add),
                     r=[ot_b, xr_b], w=[ot_b], tag="l2t_a1")
            ctx.emit("dve", lambda e, ot=ot, at=at: e.tensor_tensor(ot, ot, at, ALU.add),
                     r=[ot_b, at_b], w=[ot_b], tag="l2t_a2")
            ctx.store(xh2[m * 128:(m + 1) * 128, t0:t0 + TT], ot, ot_b,
                      wbufs=[xh2b], q="act", tag="l2t_st")


def build_l2_program():
    nc = bass.Bass("TRN2", target_bir_lowering=False)
    with ExitStack() as stk:
        ctx = Ctx(nc, stk)
        xin = ctx.dram("x_in", (T, D), F32, kind="ExternalInput")
        yout = ctx.dram("y_out", (T, D), F32, kind="ExternalOutput")
        H = {}
        for nm, shape in [("Wq", (D, D)), ("bq", (D, 1)), ("Wk", (D, D)), ("bk", (D, 1)),
                          ("Wv", (D, D)), ("bv", (D, 1)), ("w_up", (4096, D)), ("w_down", (D, 4096))]:
            h = ctx.dram("i_" + nm, shape, F32, kind="ExternalInput")
            H[nm] = (h, ctx.buf("i_" + nm))
        xh = ctx.dram("xh", (D, T), F32); xhb = ctx.buf("xh")
        xh2 = ctx.dram("xh2", (D, T), F32); xh2b = ctx.buf("xh2")
        alloc_standard(ctx)
        emit_load_smalls(ctx)
        emit_transpose_in(ctx, xin, ctx.buf("x_in"), xh, xhb)
        new_phase(ctx)
        emit_layer2(ctx, H, xh, xhb, xh2, xh2b)
        new_phase(ctx)
        emit_transpose_out(ctx, xh2, xh2b, yout, ctx.buf("y_out"))
        finish(ctx)
    return nc



def l0_trig_consts():
    DHEAD = 64
    inv = 10000.0 ** (-(np.arange(DHEAD) / DHEAD))
    pos = np.arange(T)
    fr = np.outer(inv, pos)
    return (np.cos(fr).astype(np.float32), np.sin(fr).astype(np.float32),
            (np.cos(fr) / 8.0).astype(np.float32), (np.sin(fr) / 8.0).astype(np.float32))


def emit_softplus(ctx, out_ap, out_b, in_ap, in_b, tmp_ap, tmp_b, n):
    ctx.emit("act", lambda e: e.activation(tmp_ap, in_ap, AF.Abs), r=[in_b], w=[tmp_b], tag="sp_abs")
    ctx.emit("act", lambda e: e.activation(tmp_ap, tmp_ap, AF.Exp, scale=-1.0), r=[tmp_b], w=[tmp_b], tag="sp_exp")
    onec, onecb = ones_col(ctx)
    P = in_ap.shape[0]
    ctx.emit("act", lambda e: e.activation(tmp_ap, tmp_ap, AF.Ln, bias=onec[0:P, 0:1]),
             r=[tmp_b, onecb], w=[tmp_b], tag="sp_ln")
    ctx.emit("dve", lambda e: e.tensor_relu(out_ap, in_ap), r=[in_b], w=[out_b], tag="sp_relu")
    ctx.emit("dve", lambda e: e.tensor_add(out_ap, out_ap, tmp_ap), r=[out_b, tmp_b], w=[out_b], tag="sp_add")


def emit_layer0(ctx, H, xh, xhb, xh2, xh2b):
    af = ctx.tensors["af"]
    ab = ctx.tensors["ab"]
    rows = ctx.tensors["rows"]
    smalls = ctx.tensors["smalls"]
    xnt = ctx.tensors["xn"]; xnb = ctx.bufs["xn"]
    oc, ocb = ones_col(ctx)
    orow, orb = ones_row(ctx, 128)
    identf = ctx.tensors["identf"]; identf_b = ctx.bufs["identf"]
    pf0 = ctx.tensors["pf0"]; pf0b = ctx.bufs["pf0"]
    pf1 = ctx.tensors["pf1"]; pf1b = ctx.bufs["pf1"]
    pf2 = ctx.tensors["pf2"]; pf2b = ctx.bufs["pf2"]
    pf3 = ctx.tensors["pf3"]; pf3b = ctx.bufs["pf3"]
    pf4 = ctx.tensors["pf4"]; pf4b = ctx.bufs["pf4"]
    pf5 = ctx.tensors["pf5"]; pf5b = ctx.bufs["pf5"]
    NH, DH = 16, 64
    E, FF = 8, 2048

    wbf = {}
    for nm in ("wq", "wk", "wv", "wo"):
        wbf[nm] = cast_weight(ctx, "l0" + nm, H[nm][0], H[nm][1], D, D)
    wbf["router"] = cast_weight(ctx, "l0rt", H["router_w"][0], H["router_w"][1], E, D)
    wbf["wg"] = cast_weight(ctx, "l0wg", H["w_gate"][0], H["w_gate"][1], E * FF, D)
    wbf["wu"] = cast_weight(ctx, "l0wu", H["w_up"][0], H["w_up"][1], E * FF, D)
    wbf["wd"] = cast_weight(ctx, "l0wd", H["w_down"][0], H["w_down"][1], E * D, FF)

    cosf_np, sinf_np, qcos_np, qsin_np = l0_trig_consts()
    hCF = ctx.const("l0cosf", cosf_np)
    hSF = ctx.const("l0sinf", sinf_np)
    hQC = ctx.const("l0qcos", qcos_np)
    hQS = ctx.const("l0qsin", qsin_np)

    new_phase(ctx)
    def src_x(c, t0, tsz, dst_ap, dst_b):
        ctx.load(dst_ap[:, 0:tsz], xh[c * 128:(c + 1) * 128, t0:t0 + tsz], dst_b,
                 rbufs=[xhb], tag="l0ln_ld")
    emit_ln(ctx, src_x)

    new_phase(ctx)
    qT = ctx.dram("l0_qT", (D, T), BF16); qT_b = ctx.buf("l0_qT")
    kT = ctx.dram("l0_kT", (D, T), BF16); kT_b = ctx.buf("l0_kT")
    vtok = ctx.dram("l0_v", (T, D), BF16); vtok_b = ctx.buf("l0_v")
    stg_b = [ctx.buf("v_l0_st0"), ctx.buf("v_l0_st1")]
    sj = [0]

    def mk_spill_evac(dst, dst_b):
        def ev(c2, m0, msz, n0, nsz, ps_t, ps_b):
            s = sj[0] % 2; sj[0] += 1
            st = ab[:, s * 512: s * 512 + nsz]
            c2.emit("act", lambda e: e.copy(st[0:msz, 0:nsz], ps_t[0:msz, 0:nsz]),
                    r=[ps_b], w=[stg_b[s]], tag="l0pj_ev")
            c2.store(dst[m0:m0 + msz, n0:n0 + nsz], st[0:msz, 0:nsz], stg_b[s],
                     wbufs=[dst_b], q="act", tag="l0pj_st")
        return ev
    rhs_xn = ("sbuf", xnt, xnb, lambda k, n: (k // 128) * T + n)
    linear(ctx, tag="l0q", lhsT=("dram_mk", wbf["wq"][0], wbf["wq"][1]), rhs=rhs_xn,
           K=D, M=D, N=T, evac=mk_spill_evac(qT, qT_b), m_tile=1024)
    linear(ctx, tag="l0k", lhsT=("dram_mk", wbf["wk"][0], wbf["wk"][1]), rhs=rhs_xn,
           K=D, M=D, N=T, evac=mk_spill_evac(kT, kT_b), m_tile=1024)
    wsl0 = ctx.tensors["wsl0"]; wsl0b = ctx.bufs["wsl0"]
    wsl1 = ctx.tensors["wsl1"]; wsl1b = ctx.bufs["wsl1"]
    for k in range(NC):
        ctx.emit("sp", lambda e, k=k: e.dma_start(
            wsl0[:, k * 1024:(k + 1) * 1024], wbf["wv"][0][0:D, k * 128:(k + 1) * 128],
            transpose=True), r=[wbf["wv"][1]], w=[wsl0b], dma=True, tag="l0_wvT")
    pi = [0]
    for tc in range(T // 128):
        for ot in range(2):
            psn = ("pf0", "pf1")[pi[0] % 2]; pi[0] += 1
            ps_t = ctx.tensors[psn]; ps_b = ctx.bufs[psn]
            def mmv(e, tc=tc, ot=ot, ps_t=ps_t):
                inst = None
                for k in range(NC):
                    inst = e.matmul(ps_t[0:128, 0:512],
                                    xnt[:, k * T + tc * 128: k * T + (tc + 1) * 128],
                                    wsl0[:, k * 1024 + ot * 512: k * 1024 + ot * 512 + 512],
                                    start=(k == 0), stop=(k == NC - 1))
                return inst
            ctx.emit("pe", mmv, r=[xnb, wsl0b], w=[ps_b], tag="l0v_mm")
            s = sj[0] % 2; sj[0] += 1
            st = ab[:, s * 512: s * 512 + 512]
            ctx.emit("act", lambda e, st=st, ps_t=ps_t: e.copy(st, ps_t[0:128, 0:512]),
                     r=[ps_b], w=[stg_b[s]], tag="l0v_ev")
            ctx.store(vtok[tc * 128:(tc + 1) * 128, ot * 512:ot * 512 + 512], st,
                      stg_b[s], wbufs=[vtok_b], q="act", tag="l0v_st")

    new_phase(ctx)
    bang = af[0:16, 9984:10048]
    bang_b = ctx.buf("v_l0_bang")
    tmpa = af[0:16, 10048:10112]
    tmpa_b = ctx.buf("v_l0_tmpa")
    ctx.load(bang[0:16, 0:64], H["pope_bias"][0][0:16, 0:64], bang_b,
             rbufs=[H["pope_bias"][1]], tag="l0_bld")
    PI = math.pi
    ctx.emit("dve", lambda e: e.tensor_scalar(bang, bang, -2 * PI, 0.0, ALU.max, ALU.min),
             r=[bang_b], w=[bang_b], tag="l0_bclip")
    for which, bias_add, dstc in (("cb", PI / 2, 304), ("sb", 0.0, 320)):
        ctx.emit("dve", lambda e, bias_add=bias_add: e.tensor_single_scalar(
            tmpa, bang, bias_add, ALU.add), r=[bang_b], w=[tmpa_b], tag="l0_bu")
        msk = af[0:16, 10112:10176]
        msk_b = ctx.buf("v_l0_msk")
        ctx.emit("dve", lambda e: e.tensor_single_scalar(msk, tmpa, -PI, ALU.is_lt),
                 r=[tmpa_b], w=[msk_b], tag="l0_bm")
        ctx.emit("dve", lambda e: e.scalar_tensor_tensor(tmpa, msk, 2 * PI, tmpa, ALU.mult, ALU.add),
                 r=[msk_b, tmpa_b], w=[tmpa_b], tag="l0_bw")
        ctx.emit("act", lambda e: e.activation(tmpa, tmpa, AF.Sin), r=[tmpa_b], w=[tmpa_b], tag="l0_bsin")
        ctx.emit("pe", lambda e: e.transpose(pf5[0:64, 0:16], tmpa, identf[0:16, 0:16]),
                 r=[tmpa_b, identf_b], w=[pf5b], tag="l0_btr")
        ctx.emit("dve", lambda e, dstc=dstc: e.tensor_copy(smalls[0:64, dstc:dstc + 16], pf5[0:64, 0:16]),
                 r=[pf5b], w=[ctx.buf("v_l0_cbsb")], tag="l0_bev")

    ktab_b = ctx.buf("v_l0_ktab")
    ctx.load(af[0:64, 0:4096], hCF[0:64, 0:T], ktab_b, tag="l0_ktld")
    ctx.load(af[0:64, 4096:8192], hSF[0:64, 0:T], ktab_b, tag="l0_stld")

    att = ctx.dram("l0_att", (D, T), BF16); att_b = ctx.buf("l0_att")
    ke_hbm = ctx.dram("l0_ke", (NH * 128, T), BF16); ke_b = ctx.buf("l0_ke")
    qe_hbm = ctx.dram("l0_qe", (NH * 128, T), BF16); qe_hb = ctx.buf("l0_qe")
    nm_hbm = ctx.dram("l0_nm", (NH, T), F32); nm_hb = ctx.buf("l0_nm")
    km_b = ctx.buf("v_l0_km")
    qm_b = ctx.buf("v_l0_qm")
    scr_b = ctx.buf("v_l0_scr")
    kx_b = ctx.buf("v_l0_kx")
    sl_b = ctx.buf("v_l0_sl")
    stq_b = [ctx.buf("v_l0_stq0"), ctx.buf("v_l0_stq1")]
    VPW = 65

    for h in range(NH):
        ch, hrow = h // 2, (h % 2) * 64
        cbc = smalls[0:64, 304 + h:305 + h]
        sbc = smalls[0:64, 320 + h:321 + h]
        cb_b = ctx.buf("v_l0_cbsb")
        ksl = ab[0:64, 8192:8192 + 2048]
        for hf in range(2):
            c0 = hf * 2048
            ctx.load(ksl[0:64, 0:2048], kT[ch * 128 + hrow: ch * 128 + hrow + 64, c0:c0 + 2048],
                     sl_b, rbufs=[kT_b], tag="l0a_kld")
            emit_softplus(ctx, rows[0:64, c0:c0 + 2048], km_b, ksl[0:64, 0:2048], sl_b,
                          af[0:64, 8192:10240], scr_b, 2048)
        kxc = af[0:64, 11264 + h:11265 + h]
        ctx.emit("dve", lambda e, kxc=kxc: e.reduce_max(kxc, rows[0:64, 0:T], axis=AX.X),
                 r=[km_b], w=[kx_b], tag="l0a_kmax")
        ctx.emit("dve", lambda e, kxc=kxc: e.tensor_single_scalar(kxc, kxc, 0.125, ALU.mult),
                 r=[kx_b], w=[kx_b], tag="l0a_kmaxs")
        w1 = af[0:64, 8192:8704]
        w2 = af[0:64, 8704:9216]
        for tt in range(NTT):
            t0 = tt * TT
            stg = ab[:, 10240 + (tt % 2) * TT: 10240 + (tt % 2) * TT + TT]
            for half in range(2):
                a_tab = af[0:64, (0 if half == 0 else 4096) + t0: (0 if half == 0 else 4096) + t0 + TT]
                b_tab = af[0:64, (4096 if half == 0 else 0) + t0: (4096 if half == 0 else 0) + t0 + TT]
                op = ALU.subtract if half == 0 else ALU.add
                ctx.emit("dve", lambda e, a_tab=a_tab, cbc=cbc: e.tensor_scalar_mul(w1, a_tab, cbc),
                         r=[ktab_b, cb_b, scr_b], w=[scr_b], tag="l0a_ke1")
                ctx.emit("dve", lambda e, b_tab=b_tab, sbc=sbc: e.tensor_scalar_mul(w2, b_tab, sbc),
                         r=[ktab_b, cb_b, scr_b], w=[scr_b], tag="l0a_ke2")
                ctx.emit("dve", lambda e, op=op: e.tensor_tensor(w1, w1, w2, op),
                         r=[scr_b], w=[scr_b], tag="l0a_ke3")
                ctx.emit("dve", lambda e, stg=stg, half=half, t0=t0: e.tensor_tensor(
                    stg[half * 64:(half + 1) * 64, 0:TT], w1, rows[0:64, t0:t0 + TT], ALU.mult),
                    r=[scr_b, km_b], w=[stq_b[tt % 2]], tag="l0a_ke4")
            ctx.store(ke_hbm[h * 128:(h + 1) * 128, t0:t0 + TT], stg, stq_b[tt % 2],
                      wbufs=[ke_b], q="act", tag="l0a_kest")

    for h in range(NH):
        ch, hrow = h // 2, (h % 2) * 64
        kxc = af[0:64, 11264 + h:11265 + h]
        qsl = ab[0:64, 8192:8192 + 2048]
        for hf in range(2):
            c0 = hf * 2048
            ctx.load(qsl[0:64, 0:2048], qT[ch * 128 + hrow: ch * 128 + hrow + 64, c0:c0 + 2048],
                     sl_b, rbufs=[qT_b], tag="l0a_qld")
            emit_softplus(ctx, rows[0:64, c0:c0 + 2048], qm_b, qsl[0:64, 0:2048], sl_b,
                          af[0:64, 8192:10240], scr_b, 2048)
        for tt in range(NTT):
            t0 = tt * TT
            ctx.emit("pe", lambda e, t0=t0, kxc=kxc: e.matmul(
                pf3[0:1, 0:TT], kxc, rows[0:64, t0:t0 + TT], start=True, stop=True),
                r=[kx_b, qm_b], w=[pf3b], tag="l0a_negm_mm")
            nmr = af[0:1, 10240 + (tt % 2) * TT:10240 + (tt % 2) * TT + TT]
            ctx.emit("dve", lambda e, nmr=nmr: e.tensor_single_scalar(
                nmr, pf3[0:1, 0:TT], -1.0, ALU.mult), r=[pf3b], w=[ctx.buf("v_l0_nmr")], tag="l0a_negm")
            ctx.store(nm_hbm[h:h + 1, t0:t0 + TT], nmr, ctx.buf("v_l0_nmr"),
                      wbufs=[nm_hb], q="act", tag="l0a_nmst")
            stg = ab[:, 10240 + (tt % 2) * TT: 10240 + (tt % 2) * TT + TT]
            ctx.load(af[0:64, 8192:8704], hQC[0:64, t0:t0 + TT], scr_b, tag="l0a_qcld")
            ctx.load(af[0:64, 8704:9216], hQS[0:64, t0:t0 + TT], scr_b, tag="l0a_qsld")
            ctx.emit("dve", lambda e, stg=stg, t0=t0: e.tensor_tensor(
                stg[0:64, 0:TT], rows[0:64, t0:t0 + TT], af[0:64, 8192:8704], ALU.mult),
                r=[qm_b, scr_b], w=[stq_b[tt % 2]], tag="l0a_qe1")
            ctx.emit("dve", lambda e, stg=stg, t0=t0: e.tensor_tensor(
                stg[64:128, 0:TT], rows[0:64, t0:t0 + TT], af[0:64, 8704:9216], ALU.mult),
                r=[qm_b, scr_b], w=[stq_b[tt % 2]], tag="l0a_qe2")
            ctx.store(qe_hbm[h * 128:(h + 1) * 128, t0:t0 + TT], stg, stq_b[tt % 2],
                      wbufs=[qe_hb], q="act", tag="l0a_qest")

    new_phase(ctx)
    AB_KE = 0
    AB_VP = 4096
    AB_QE = 6208
    AB_E = 7232
    AB_OS = 8256
    ke2_b = ctx.buf("v_l0_ke2")
    vp_b = ctx.buf("v_l0_vp")
    qe2_b = [ctx.buf("v_l0_qe0"), ctx.buf("v_l0_qe1b")]
    nm2_b = [ctx.buf("v_l0_nm0"), ctx.buf("v_l0_nm1b")]
    eb2 = [ctx.buf("v_l0_E0"), ctx.buf("v_l0_E1")]
    oev_b = ctx.buf("v_l0_oev")
    for h in range(NH):
        ctx.load(ab[:, AB_KE:AB_KE + T], ke_hbm[h * 128:(h + 1) * 128, 0:T], ke2_b,
                 rbufs=[ke_b], tag="l0f_keld")
        ctx.emit("dve", lambda e: e.memset(ab[:, AB_VP:AB_VP + 32 * VPW], 1.0),
                 r=[], w=[vp_b], tag="l0f_vones")
        for kv in range(32):
            ctx.load(ab[:, AB_VP + kv * VPW: AB_VP + kv * VPW + 64],
                     vtok[kv * 128:(kv + 1) * 128, h * 64:(h + 1) * 64],
                     vp_b, rbufs=[vtok_b], tag="l0f_vld")
        for qt in range(NTT):
            t0 = qt * TT
            sq = qt % 2
            qetile = ab[:, AB_QE + sq * TT:AB_QE + sq * TT + TT]
            nmrow = af[0:1, 10240 + sq * TT:10240 + sq * TT + TT]
            ctx.load(qetile, qe_hbm[h * 128:(h + 1) * 128, t0:t0 + TT], qe2_b[sq],
                     rbufs=[qe_hb], tag="l0f_qeld")
            ctx.load(nmrow, nm_hbm[h:h + 1, t0:t0 + TT], nm2_b[sq], rbufs=[nm_hb], tag="l0f_nmld")
            for kv in range(33):
                if kv < 32:
                    psn, psb = (pf0, pf0b) if kv % 2 == 0 else (pf1, pf1b)
                    def mmS(e, kv=kv, psn=psn, qetile=qetile, nmrow=nmrow):
                        e.matmul(psn[0:128, 0:TT], ab[:, AB_KE + kv * 128: AB_KE + (kv + 1) * 128],
                                 qetile, start=True, stop=False)
                        return e.matmul(psn[0:128, 0:TT], orow, nmrow, start=False, stop=True)
                    ctx.emit("pe", mmS, r=[ke2_b, qe2_b[sq], orb, nm2_b[sq]], w=[psb], tag="l0f_mmS")
                    eslot = kv % 2
                    ctx.emit("act", lambda e, psn=psn, eslot=eslot: e.activation(
                        ab[:, AB_E + eslot * TT:AB_E + eslot * TT + TT], psn[0:128, 0:TT], AF.Exp),
                        r=[psb], w=[eb2[eslot]], tag="l0f_exp")
                if kv >= 1:
                    kvp = kv - 1
                    eslot = kvp % 2
                    ctx.emit("pe", lambda e, kvp=kvp, eslot=eslot: e.matmul(
                        pf2[0:VPW, 0:TT], ab[:, AB_VP + kvp * VPW: AB_VP + (kvp + 1) * VPW],
                        ab[:, AB_E + eslot * TT:AB_E + eslot * TT + TT],
                        start=(kvp == 0), stop=(kvp == 31)),
                        r=[vp_b, eb2[eslot]], w=[pf2b], tag="l0f_av")
            rd = af[0:1, 9728:9728 + TT]
            ctx.emit("dve", lambda e, rd=rd: e.reciprocal(rd, pf2[64:65, 0:TT]),
                     r=[pf2b], w=[ctx.buf("v_l0_rd")], tag="l0f_rd")
            ctx.emit("pe", lambda e, rd=rd: e.matmul(pf3[0:64, 0:TT], orow[0:1, 0:64], rd,
                                                     start=True, stop=True),
                     r=[ctx.buf("v_l0_rd"), orb], w=[pf3b], tag="l0f_rdB")
            ost = af[0:64, 9216:9216 + TT]
            ctx.emit("act", lambda e, ost=ost: e.copy(ost, pf2[0:64, 0:TT]),
                     r=[pf2b], w=[oev_b], tag="l0f_oev")
            stb = ab[0:64, AB_OS:AB_OS + TT]
            ctx.emit("dve", lambda e, ost=ost, stb=stb: e.tensor_tensor(
                stb, ost, pf3[0:64, 0:TT], ALU.mult), r=[oev_b, pf3b], w=[ctx.buf("v_l0_ostb")], tag="l0f_on")
            ctx.store(att[h * 64:(h + 1) * 64, t0:t0 + TT], stb, ctx.buf("v_l0_ostb"),
                      wbufs=[att_b], q="act", tag="l0f_ost")

    new_phase(ctx)
    rtT_b = ctx.buf("v_l0_rtT")
    rsl0 = ctx.tensors["rsl0"]; rsl0b = ctx.bufs["rsl0"]
    ctx.load(rsl0[0:8, 0:1024], wbf["router"][0][0:8, 0:D], rtT_b, rbufs=[wbf["router"][1]], tag="l0g_rld")
    smallsb = ctx.tensors["smallsb"]
    smb_b = ctx.buf("v_l0_smb")
    pb = ctx.tensors["pb"]; pb_b = ctx.bufs["pb"]
    identb = ctx.tensors["identb"]; identb_b = ctx.bufs["identb"]
    for c in range(NC):
        ctx.emit("pe", lambda e, c=c: e.transpose(
            pb[0:128, 0:8], rsl0[0:8, c * 128:(c + 1) * 128], identb[0:8, 0:8]),
            r=[rtT_b, identb_b], w=[pb_b], tag="l0g_rtr")
        ctx.emit("dve", lambda e, c=c: e.tensor_copy(smallsb[:, c * 8:(c + 1) * 8], pb[0:128, 0:8]),
                 r=[pb_b], w=[smb_b], tag="l0g_rev")
    selc = np.zeros((8, 8 * 128), np.float32)
    for e_ in range(8):
        selc[e_, e_ * 128:(e_ + 1) * 128] = 1.0
    hSEL = ctx.const("l0sel", selc)
    sel_b = ctx.buf("v_l0_sel")
    ctx.load(rows[0:8, 4200:4200 + 1024], hSEL[0:8, :], sel_b, tag="l0g_selld")
    dw_b = ctx.buf("v_l0_dw")
    g_b = ctx.buf("v_l0_g")
    JE = 0.01
    for tc in range(T // 128):
        def mmg(e, tc=tc):
            inst = None
            for k in range(NC):
                inst = e.matmul(pf4[0:128, 0:8],
                                xnt[:, k * T + tc * 128: k * T + (tc + 1) * 128],
                                smallsb[:, k * 8:(k + 1) * 8], start=(k == 0), stop=(k == NC - 1))
            return inst
        ctx.emit("pe", mmg, r=[xnb, smb_b], w=[pf4b], tag="l0g_mm")
        sG = af[:, 10688:10696]
        m1 = af[:, 10696:10697]
        f1 = af[:, 10697:10698]
        e1 = af[:, 10698:10706]
        o1 = af[:, 10706:10714]
        d1 = af[:, 10714:10715]
        s2 = af[:, 10715:10723]
        m2 = af[:, 10723:10724]
        e2 = af[:, 10724:10732]
        o2 = af[:, 10732:10740]
        d2 = af[:, 10740:10741]
        keep = af[:, 10741:10749]
        gg = af[:, 10749:10750]
        dw = af[:, 10750:10758]
        ctx.emit("dve", lambda e, sG=sG: e.tensor_copy(sG, pf4[0:128, 0:8]), r=[pf4b], w=[g_b], tag="l0g_cp")
        ctx.emit("dve", lambda e, sG=sG, m1=m1: e.reduce_max(m1, sG, axis=AX.X), r=[g_b], w=[g_b], tag="l0g_m1")
        ctx.emit("dve", lambda e, sG=sG, keep=keep, m1=m1: e.tensor_scalar(
            keep, sG, m1[:, 0:1], -1.0, ALU.subtract, ALU.mult), r=[g_b], w=[g_b], tag="l0g_d")
        ctx.emit("act", lambda e, sG=sG, e1=e1: e.activation(e1, sG, AF.Abs),
                 r=[g_b], w=[g_b], tag="l0g_abs")
        ctx.emit("dve", lambda e, e1=e1, m1=m1, o1=o1: e.tensor_scalar_max(o1, e1, m1[:, 0:1]),
                 r=[g_b], w=[g_b], tag="l0g_fac")
        ctx.emit("dve", lambda e, o1=o1, keep=keep: e.scalar_tensor_tensor(
            keep, o1, -2 * JE, keep, ALU.mult, ALU.add), r=[g_b], w=[g_b], tag="l0g_k1a")
        ctx.emit("dve", lambda e, keep=keep: e.tensor_single_scalar(keep, keep, 0.0, ALU.is_le),
                 r=[g_b], w=[g_b], tag="l0g_k1b")
        nm1 = af[:, 10758:10759]
        ctx.emit("dve", lambda e, m1=m1, nm1=nm1: e.tensor_single_scalar(nm1, m1, -1.0, ALU.mult),
                 r=[g_b], w=[g_b], tag="l0g_nm1")
        ctx.emit("act", lambda e, sG=sG, e1=e1, nm1=nm1: e.activation(
            e1, sG, AF.Exp, bias=nm1[:, 0:1]), r=[g_b], w=[g_b], tag="l0g_e1")
        ctx.emit("dve", lambda e, e1=e1, keep=keep: e.tensor_tensor(e1, e1, keep, ALU.mult),
                 r=[g_b], w=[g_b], tag="l0g_e1m")
        ctx.emit("dve", lambda e, e1=e1, d1=d1: e.reduce_sum(d1, e1, axis=AX.X), r=[g_b], w=[g_b], tag="l0g_d1")
        ctx.emit("dve", lambda e, d1=d1: e.reciprocal(d1, d1), r=[g_b], w=[g_b], tag="l0g_r1")
        ctx.emit("dve", lambda e, sG=sG, o1=o1, m1=m1: e.tensor_scalar(
            o1, sG, m1[:, 0:1], None, ALU.is_ge), r=[g_b], w=[g_b], tag="l0g_o1")
        ctx.emit("dve", lambda e, sG=sG, o1=o1, s2=s2: e.scalar_tensor_tensor(
            s2, o1, -1e30, sG, ALU.mult, ALU.add), r=[g_b], w=[g_b], tag="l0g_s2")
        ctx.emit("dve", lambda e, s2=s2, m2=m2: e.reduce_max(m2, s2, axis=AX.X), r=[g_b], w=[g_b], tag="l0g_m2")
        ctx.emit("dve", lambda e, sG=sG, keep=keep, m2=m2: e.tensor_scalar(
            keep, sG, m2[:, 0:1], -1.0, ALU.subtract, ALU.mult), r=[g_b], w=[g_b], tag="l0g_d2a")
        ctx.emit("act", lambda e, sG=sG, e2=e2: e.activation(e2, sG, AF.Abs),
                 r=[g_b], w=[g_b], tag="l0g_abs2")
        ctx.emit("dve", lambda e, e2=e2, m2=m2, o2=o2: e.tensor_scalar_max(o2, e2, m2[:, 0:1]),
                 r=[g_b], w=[g_b], tag="l0g_fac2")
        ctx.emit("dve", lambda e, o2=o2, keep=keep: e.scalar_tensor_tensor(
            keep, o2, -2 * JE, keep, ALU.mult, ALU.add), r=[g_b], w=[g_b], tag="l0g_k2a")
        ctx.emit("dve", lambda e, keep=keep: e.tensor_single_scalar(keep, keep, 0.0, ALU.is_le),
                 r=[g_b], w=[g_b], tag="l0g_k2b")
        nm2 = af[:, 10759:10760]
        ctx.emit("dve", lambda e, m2=m2, nm2=nm2: e.tensor_single_scalar(nm2, m2, -1.0, ALU.mult),
                 r=[g_b], w=[g_b], tag="l0g_nm2")
        ctx.emit("act", lambda e, s2=s2, e2=e2, nm2=nm2: e.activation(
            e2, s2, AF.Exp, bias=nm2[:, 0:1]), r=[g_b], w=[g_b], tag="l0g_e2")
        ctx.emit("dve", lambda e, e2=e2, keep=keep: e.tensor_tensor(e2, e2, keep, ALU.mult),
                 r=[g_b], w=[g_b], tag="l0g_e2m")
        ctx.emit("dve", lambda e, e2=e2, d2=d2: e.reduce_sum(d2, e2, axis=AX.X), r=[g_b], w=[g_b], tag="l0g_d2")
        ctx.emit("dve", lambda e, d2=d2: e.reciprocal(d2, d2), r=[g_b], w=[g_b], tag="l0g_r2")
        ctx.emit("dve", lambda e, s2=s2, o2=o2, m2=m2: e.tensor_scalar(
            o2, s2, m2[:, 0:1], None, ALU.is_ge), r=[g_b], w=[g_b], tag="l0g_o2")
        ctx.emit("dve", lambda e, o1=o1, d1=d1, dw=dw: e.tensor_scalar_mul(dw, o1, d1[:, 0:1]),
                 r=[g_b], w=[dw_b], tag="l0g_dw1")
        ctx.emit("dve", lambda e, o2=o2, d2=d2: e.tensor_scalar_mul(o2, o2, d2[:, 0:1]),
                 r=[g_b], w=[g_b], tag="l0g_dw2b")
        ctx.emit("dve", lambda e, o2=o2, dw=dw: e.tensor_tensor(dw, dw, o2, ALU.add),
                 r=[g_b, dw_b], w=[dw_b], tag="l0g_dw3")
        ctx.emit("pe", lambda e, dw=dw: e.transpose(pf5[0:8, 0:128], dw, identf[:]),
                 r=[dw_b, identf_b], w=[pf5b], tag="l0g_tr")
        ctx.emit("dve", lambda e, tc=tc: e.tensor_copy(rows[0:8, tc * 128:(tc + 1) * 128],
                                                       pf5[0:8, 0:128]),
                 r=[pf5b], w=[ctx.buf("v_l0_dwT")], tag="l0g_trev")


    new_phase(ctx)
    l0acc = ctx.dram("l0_acc", (D, T), F32); l0acc_b = ctx.buf("l0_acc")
    rsl0 = ctx.tensors["rsl0"]; rsl0b = ctx.bufs["rsl0"]
    rsl1 = ctx.tensors["rsl1"]; rsl1b = ctx.bufs["rsl1"]
    AB_H = 0
    gb_sb = af[:, 8192:8704]
    gb_b = ctx.buf("v_l0_gb")
    sact = af[:, 8704:9216]
    sact_b = ctx.buf("v_l0_sact")
    sg2 = af[:, 9216:9728]
    sg2_b = ctx.buf("v_l0_sg2")
    dwT_b = ctx.buf("v_l0_dwT")
    h_b = ctx.buf("v_l0_h")
    dst_b = ctx.buf("v_l0_dnst")
    first_acc = set()
    for e_ in range(E):
        for fh in range(2):
            for k in range(NC):
                ctx.emit("sp", lambda e, k=k, e_=e_, fh=fh: e.dma_start(
                    wsl0[:, k * 1024:(k + 1) * 1024],
                    wbf["wg"][0][e_ * FF + fh * 1024: e_ * FF + fh * 1024 + 1024,
                                 k * 128:(k + 1) * 128], transpose=True),
                    r=[wbf["wg"][1]], w=[wsl0b], dma=True, tag="l0m_wg")
                ctx.emit("sp", lambda e, k=k, e_=e_, fh=fh: e.dma_start(
                    wsl1[:, k * 1024:(k + 1) * 1024],
                    wbf["wu"][0][e_ * FF + fh * 1024: e_ * FF + fh * 1024 + 1024,
                                 k * 128:(k + 1) * 128], transpose=True),
                    r=[wbf["wu"][1]], w=[wsl1b], dma=True, tag="l0m_wu")
            for fi in range(8):
                rslt = rsl0 if fi < 4 else rsl1
                rslb = rsl0b if fi < 4 else rsl1b
                ctx.emit("sp", lambda e, fi=fi, e_=e_, fh=fh, rslt=rslt: e.dma_start(
                    rslt[:, (fi % 4) * 1024:(fi % 4) * 1024 + 1024],
                    wbf["wd"][0][e_ * D: (e_ + 1) * D,
                                 fh * 1024 + fi * 128: fh * 1024 + (fi + 1) * 128], transpose=True),
                    r=[wbf["wd"][1]], w=[rslb], dma=True, tag="l0m_wd")
            for tg in range(NTT):
                t0 = tg * TT
                ctx.emit("pe", lambda e, e_=e_, t0=t0: e.matmul(
                    pf3[0:128, 0:TT], rows[0:8, 4200 + e_ * 128: 4200 + (e_ + 1) * 128],
                    rows[0:8, t0:t0 + TT], start=True, stop=True),
                    r=[sel_b, dwT_b], w=[pf3b], tag="l0m_gB")
                ctx.emit("act", lambda e: e.copy(gb_sb, pf3[0:128, 0:TT]),
                         r=[pf3b], w=[gb_b], tag="l0m_gBe")
                for fi in range(8):
                    psa, psab = (pf0, pf0b) if fi % 2 == 0 else (pf1, pf1b)
                    psb_, psbb = (pf4, pf4b) if fi % 2 == 0 else (pf5, pf5b)
                    def mma(e, fi=fi, t0=t0, psa=psa):
                        inst = None
                        for k in range(NC):
                            inst = e.matmul(psa[0:128, 0:TT],
                                            wsl0[:, k * 1024 + fi * 128: k * 1024 + (fi + 1) * 128],
                                            xnt[:, k * T + t0: k * T + t0 + TT],
                                            start=(k == 0), stop=(k == NC - 1))
                        return inst
                    ctx.emit("pe", mma, r=[wsl0b, xnb], w=[psab], tag="l0m_mma")
                    def mmb(e, fi=fi, t0=t0, psb_=psb_):
                        inst = None
                        for k in range(NC):
                            inst = e.matmul(psb_[0:128, 0:TT],
                                            wsl1[:, k * 1024 + fi * 128: k * 1024 + (fi + 1) * 128],
                                            xnt[:, k * T + t0: k * T + t0 + TT],
                                            start=(k == 0), stop=(k == NC - 1))
                        return inst
                    ctx.emit("pe", mmb, r=[wsl1b, xnb], w=[psbb], tag="l0m_mmb")
                    ctx.emit("act", lambda e, psa=psa: e.activation(sact, psa[0:128, 0:TT], AF.Silu),
                             r=[psab], w=[sact_b], tag="l0m_silu")
                    ctx.emit("pool", lambda e: e.tensor_tensor(sg2, sact, gb_sb, ALU.mult),
                             r=[sact_b, gb_b], w=[sg2_b], tag="l0m_sg")
                    ctx.emit("dve", lambda e, fi=fi, psb_=psb_: e.tensor_tensor(
                        ab[:, AB_H + fi * TT: AB_H + (fi + 1) * TT], sg2, psb_[0:128, 0:TT], ALU.mult),
                        r=[sg2_b, psbb], w=[h_b], tag="l0m_h")
                for o in range(NC):
                    pso, psob = (pf0, pf0b) if o % 2 == 0 else (pf1, pf1b)
                    def mmd(e, o=o, pso=pso):
                        inst = None
                        for fi in range(8):
                            rslt = rsl0 if fi < 4 else rsl1
                            inst = e.matmul(pso[0:128, 0:TT],
                                            rslt[:, (fi % 4) * 1024 + o * 128: (fi % 4) * 1024 + (o + 1) * 128],
                                            ab[:, AB_H + fi * TT: AB_H + (fi + 1) * TT],
                                            start=(fi == 0), stop=(fi == 7))
                        return inst
                    ctx.emit("pe", mmd, r=[rsl0b, rsl1b, h_b], w=[psob], tag="l0m_mmd")
                    dstg = af[:, 9728 + (o % 2) * 512: 9728 + (o % 2) * 512 + TT]
                    ctx.emit("act", lambda e, dstg=dstg, pso=pso: e.copy(dstg, pso[0:128, 0:TT]),
                             r=[psob], w=[dst_b], tag="l0m_dev")
                    key = (o, tg)
                    if key not in first_acc:
                        first_acc.add(key)
                        ctx.emit("pool", lambda e, o=o, t0=t0, dstg=dstg: e.dma_start(
                            l0acc[o * 128:(o + 1) * 128, t0:t0 + TT], dstg),
                            r=[dst_b], w=[l0acc_b], dma=True, tag="l0m_dst0")
                    else:
                        ctx.emit("pool", lambda e, o=o, t0=t0, dstg=dstg: e.dma_start(
                            l0acc[o * 128:(o + 1) * 128, t0:t0 + TT], dstg, accum_op=ALU.add),
                            r=[dst_b], w=[l0acc_b], dma=True, tag="l0m_dacc")

    new_phase(ctx)
    for k in range(NC):
        ctx.emit("sp", lambda e, k=k: e.dma_start(
            wsl0[:, k * 1024:(k + 1) * 1024], wbf["wo"][0][0:D, k * 128:(k + 1) * 128],
            transpose=True), r=[wbf["wo"][1]], w=[wsl0b], dma=True, tag="l0t_woT")
    at_b = ctx.buf("v_l0t_at")
    xr_b = ctx.buf("v_l0t_xr")
    ac_b = ctx.buf("v_l0t_ac")
    tm_b = ctx.buf("v_l0t_tm")
    ot_b = ctx.buf("v_l0t_ot")
    for ti in range(NTT):
        t0 = ti * TT
        for c in range(NC):
            ctx.load(ab[:, c * TT:(c + 1) * TT], att[c * 128:(c + 1) * 128, t0:t0 + TT],
                     at_b, rbufs=[att_b], tag="l0t_atld")
        for m in range(NC):
            psn, psb_ = (pf0, pf0b) if m % 2 == 0 else (pf1, pf1b)
            def mmo(e, m=m, psn=psn):
                inst = None
                for k in range(NC):
                    inst = e.matmul(psn[0:128, 0:TT],
                                    wsl0[:, k * 1024 + m * 128: k * 1024 + (m + 1) * 128],
                                    ab[:, k * TT:(k + 1) * TT], start=(k == 0), stop=(k == NC - 1))
                return inst
            ctx.emit("pe", mmo, r=[wsl0b, at_b], w=[psb_], tag="l0t_mm")
            xr = af[:, 0:TT]
            ac = af[:, TT:2 * TT]
            tm = af[:, 2 * TT:3 * TT]
            ot = af[:, 3 * TT:4 * TT]
            ctx.load(xr, xh[m * 128:(m + 1) * 128, t0:t0 + TT], xr_b, rbufs=[xhb], tag="l0t_x")
            ctx.load(ac, l0acc[m * 128:(m + 1) * 128, t0:t0 + TT], ac_b, rbufs=[l0acc_b], tag="l0t_ac")
            ctx.emit("dve", lambda e, tm=tm, xr=xr, psn=psn: e.tensor_tensor(tm, xr, psn[0:128, 0:TT], ALU.add),
                     r=[xr_b, psb_], w=[tm_b], tag="l0t_a1")
            ctx.emit("dve", lambda e, tm=tm, ot=ot, ac=ac: e.tensor_tensor(ot, tm, ac, ALU.add),
                     r=[tm_b, ac_b], w=[ot_b], tag="l0t_a2")
            ctx.store(xh2[m * 128:(m + 1) * 128, t0:t0 + TT], ot, ot_b,
                      wbufs=[xh2b], q="act", tag="l0t_st")


def build_l0_program():
    nc = bass.Bass("TRN2", target_bir_lowering=False)
    with ExitStack() as stk:
        ctx = Ctx(nc, stk)
        xin = ctx.dram("x_in", (T, D), F32, kind="ExternalInput")
        yout = ctx.dram("y_out", (T, D), F32, kind="ExternalOutput")
        H = {}
        for nm, shape in [("wq", (D, D)), ("wk", (D, D)), ("wv", (D, D)), ("wo", (D, D)),
                          ("pope_bias", (16, 64)), ("router_w", (8, D)),
                          ("w_gate", (8 * 2048, D)), ("w_up", (8 * 2048, D)),
                          ("w_down", (8 * D, 2048))]:
            h = ctx.dram("i_" + nm, shape, F32, kind="ExternalInput")
            H[nm] = (h, ctx.buf("i_" + nm))
        xh = ctx.dram("xh", (D, T), F32); xhb = ctx.buf("xh")
        xh2 = ctx.dram("xh2", (D, T), F32); xh2b = ctx.buf("xh2")
        alloc_standard(ctx)
        emit_load_smalls(ctx)
        emit_transpose_in(ctx, xin, ctx.buf("x_in"), xh, xhb)
        new_phase(ctx)
        emit_layer0(ctx, H, xh, xhb, xh2, xh2b)
        new_phase(ctx)
        emit_transpose_out(ctx, xh2, xh2b, yout, ctx.buf("y_out"))
        finish(ctx)
    return nc



def l3_attn_consts():
    S = T
    ROT = 32
    inv = 1.0 / 10000.0 ** (np.arange(0, ROT, 2) / ROT)
    t = np.arange(S, dtype=np.float64)
    fr = np.repeat(t[:, None] * inv[None, :], 2, axis=1)
    base = (np.arange(0, ROT, 2) + 0.4 * ROT) / (1.4 * ROT)
    power = (t - S // 2) / 512.0
    scale = np.repeat(base[None, :] ** power[:, None], 2, axis=1)
    cos = np.cos(fr); sin = np.sin(fr)
    A = np.ones((64, S)); B = np.zeros((64, S))
    Ak = np.ones((64, S)); Bk = np.zeros((64, S))
    A[0:32] = (cos * scale).T
    B[0:32] = (sin * scale).T
    Ak[0:32] = (cos / scale).T
    Bk[0:32] = (sin / scale).T
    A128 = np.tile(A, (2, 1)).astype(np.float32)
    B128 = np.tile(B, (2, 1)).astype(np.float32)
    Ak128 = np.tile(Ak, (2, 1)).astype(np.float32)
    Bk128 = np.tile(Bk, (2, 1)).astype(np.float32)
    P = np.zeros((128, 128))
    for blk in range(2):
        o = blk * 64
        for i in range(16):
            P[o + 2 * i, o + 2 * i + 1] = -1.0
            P[o + 2 * i + 1, o + 2 * i] = 1.0
        for j in range(32, 64):
            P[o + j, o + j] = 1.0
    PT = P.T.astype(ml_dtypes.bfloat16)
    hh = np.arange(16)
    gamma = 1.0 - 2.0 ** (-5.0 - hh)
    idx = np.arange(S, dtype=np.float64)
    n_h = np.zeros((16, S))
    for h in range(16):
        rs = (1.0 - gamma[h] ** (idx + 1)) / (1.0 - gamma[h])
        n_h[h] = 1.0 / np.sqrt(rs)
    c_h = (gamma[:, None] ** (idx[None, :] % 128)) * n_h
    c_h = c_h.astype(np.float32)
    C4 = np.zeros((16, 64, 512), np.float64)
    for h in range(16):
        for b in range(4):
            C4[h, :, b * 128:(b + 1) * 128] = gamma[h] ** (128.0 * b)
    C4 = C4.astype(np.float32)
    GCOL = np.zeros((128, 16 * 32), np.float64)
    p = np.arange(128, dtype=np.float64)
    for h in range(16):
        for d2i, d2 in enumerate(range(-3, 29)):
            with np.errstate(under='ignore'):
                GCOL[:, h * 32 + d2i] = gamma[h] ** (-p) * gamma[h] ** (128.0 * d2)
    GCOL = GCOL.astype(np.float32)
    TRI = (np.arange(128)[:, None] <= np.arange(128)[None, :]).astype(np.float32)
    SEL16 = np.zeros((16, 16 * 64), np.float32)
    for h in range(16):
        SEL16[h, h * 64:(h + 1) * 64] = 1.0
    BH16 = np.zeros((16, 8 * 128), np.float32)
    for c in range(8):
        for j2 in range(2):
            BH16[2 * c + j2, c * 128 + j2 * 64: c * 128 + (j2 + 1) * 64] = 1.0
    return A128, B128, Ak128, Bk128, PT, c_h, C4, GCOL, TRI, SEL16, BH16, gamma


def emit_rotary_pass(ctx, tag, srcdram, src_b, hA, hB, htabs_b, hPT):
    af = ctx.tensors["af"]
    ab = ctx.tensors["ab"]
    pb = ctx.tensors["pb"]; pb_b = ctx.bufs["pb"]
    ident = None
    PT_sb = ctx.tensors["smallsb"]
    ctx.load(af[:, 0:4096], hA[0:128, 0:T], htabs_b, tag=tag + "_A")
    ctx.load(af[:, 4096:8192], hB[0:128, 0:T], htabs_b, tag=tag + "_B")
    ctx.load(PT_sb[:, 0:128], hPT[0:128, 0:128], ctx.buf("v_l3_PT"), tag=tag + "_P")
    ch_b = ctx.buf("v_l3_rotch")
    t1_b = ctx.buf("v_l3_rott")
    st_b = ctx.buf("v_l3_rotst")
    pf0 = ctx.tensors["pf0"]; pf0b = ctx.bufs["pf0"]
    for c in range(NC):
        ctx.load(ab[:, 0:4096], srcdram[c * 128:(c + 1) * 128, 0:T], ch_b,
                 rbufs=[src_b], tag=tag + "_ld")
        for tt in range(NTT):
            t0 = tt * TT
            ctx.emit("pe", lambda e, t0=t0: e.matmul(
                pf0[0:128, 0:TT], PT_sb[:, 0:128], ab[:, t0:t0 + TT], start=True, stop=True),
                r=[ctx.buf("v_l3_PT"), ch_b], w=[pf0b], tag=tag + "_mm")
            t1 = af[:, 8192:8704]
            ctx.emit("dve", lambda e, t0=t0, t1=t1: e.tensor_tensor(
                t1, ab[:, t0:t0 + TT], af[:, t0:t0 + TT], ALU.mult),
                r=[ch_b, htabs_b], w=[t1_b], tag=tag + "_m1")
            t2 = af[:, 8704:9216]
            ctx.emit("dve", lambda e, t0=t0, t2=t2: e.tensor_tensor(
                t2, pf0[0:128, 0:TT], af[:, 4096 + t0:4096 + t0 + TT], ALU.mult),
                r=[pf0b, htabs_b], w=[t1_b], tag=tag + "_m2")
            st = ab[:, 4096 + (tt % 2) * TT: 4096 + (tt % 2) * TT + TT]
            ctx.emit("dve", lambda e, t1=t1, t2=t2, st=st: e.tensor_tensor(st, t1, t2, ALU.add),
                     r=[t1_b], w=[st_b], tag=tag + "_m3")
            ctx.store(srcdram[c * 128:(c + 1) * 128, t0:t0 + TT], st, st_b,
                      wbufs=[src_b], q="act", tag=tag + "_st")


def emit_layer3_attn(ctx, H, xh, xhb, y3, y3_b):
    af = ctx.tensors["af"]
    ab = ctx.tensors["ab"]
    rows = ctx.tensors["rows"]
    smalls = ctx.tensors["smalls"]
    xnt = ctx.tensors["xn"]; xnb = ctx.bufs["xn"]
    oc, ocb = ones_col(ctx)
    orow, orb = ones_row(ctx, 128)
    pf0 = ctx.tensors["pf0"]; pf0b = ctx.bufs["pf0"]
    pf1 = ctx.tensors["pf1"]; pf1b = ctx.bufs["pf1"]
    pf2 = ctx.tensors["pf2"]; pf2b = ctx.bufs["pf2"]
    pf3 = ctx.tensors["pf3"]; pf3b = ctx.bufs["pf3"]
    pf4 = ctx.tensors["pf4"]; pf4b = ctx.bufs["pf4"]
    NH = 16

    A128, B128, Ak128, Bk128, PT, c_h, C4, GCOL, TRI, SEL16, BH16, gamma = l3_attn_consts()
    hA = ctx.const("l3A", A128); hB = ctx.const("l3B", B128)
    hAk = ctx.const("l3Ak", Ak128); hBk = ctx.const("l3Bk", Bk128)
    hPT = ctx.const("l3PT", PT)
    hch = ctx.const("l3ch", c_h)
    hC4 = ctx.const("l3C4", C4.reshape(16 * 64, 512))
    hGC = ctx.const("l3GCOL", GCOL)
    hTRI = ctx.const("l3TRI", TRI)
    hSEL = ctx.const("l3SEL16", SEL16)
    hBH = ctx.const("l3BH16", BH16)
    htabs_b = ctx.buf("v_l3_tabs")

    wbf = {}
    for nm in ("Wq", "Wk", "Wv", "Wg", "Wo"):
        wbf[nm] = cast_weight(ctx, "l3" + nm, H[nm][0], H[nm][1], D, D)
    bias_b = ctx.buf("v_l3_bias")
    for c in range(NC):
        ctx.load(smalls[:, 280 + c:281 + c], H["bq"][0][c * 128:(c + 1) * 128, 0:1],
                 bias_b, rbufs=[H["bq"][1]], tag="l3_bq")
        ctx.load(smalls[:, 288 + c:289 + c], H["bk"][0][c * 128:(c + 1) * 128, 0:1],
                 bias_b, rbufs=[H["bk"][1]], tag="l3_bk")
        ctx.load(smalls[:, 296 + c:297 + c], H["bg"][0][c * 128:(c + 1) * 128, 0:1],
                 bias_b, rbufs=[H["bg"][1]], tag="l3_bg")
        ctx.load(smalls[:, 340 + c:341 + c], H["bo"][0][c * 128:(c + 1) * 128, 0:1],
                 bias_b, rbufs=[H["bo"][1]], tag="l3_bo")
    bv_row = rows[0:1, 4500:4500 + D]
    bv_b = ctx.buf("v_l3_bv")
    ctx.emit("sp", lambda e: e.dma_start(bv_row, H["bv"][0].ap().rearrange("d one -> one d")),
             r=[H["bv"][1]], w=[bv_b], dma=True, tag="l3_bv")
    gn_b = ctx.buf("v_l3_gn")
    ctx.load(smalls[0:16, 336:337], H["gn_w"][0][0:16, 0:1], gn_b, rbufs=[H["gn_w"][1]], tag="l3_gnw")
    ctx.load(smalls[0:16, 337:338], H["gn_b"][0][0:16, 0:1], gn_b, rbufs=[H["gn_b"][1]], tag="l3_gnb")

    qT = ctx.dram("l3_qT", (D, T), BF16); qT_b = ctx.buf("l3_qT")
    kT = ctx.dram("l3_kT", (D, T), BF16); kT_b = ctx.buf("l3_kT")
    gT = ctx.dram("l3_gT", (D, T), BF16); gT_b = ctx.buf("l3_gT")
    vtok = ctx.dram("l3_v", (T, D), BF16); vtok_b = ctx.buf("l3_v")
    ret = ctx.dram("l3_ret", (D, T), BF16); ret_b = ctx.buf("l3_ret")
    stg_b = [ctx.buf("v_l3_st0"), ctx.buf("v_l3_st1")]
    sj = [0]

    def mk_proj_evac(dst, dst_b, bias_base, func=None):
        def ev(c2, m0, msz, n0, nsz, ps_t, ps_b):
            s = sj[0] % 2; sj[0] += 1
            st = ab[:, s * 512: s * 512 + nsz]
            mchunk = m0 // 128
            if func is None:
                c2.emit("act", lambda e: e.activation(
                    st[0:msz, 0:nsz], ps_t[0:msz, 0:nsz], AF.Identity,
                    bias=smalls[:, bias_base + mchunk:bias_base + mchunk + 1]),
                    r=[ps_b, bias_b], w=[stg_b[s]], tag="l3pj_ev")
            else:
                c2.emit("act", lambda e: e.activation(
                    st[0:msz, 0:nsz], ps_t[0:msz, 0:nsz], func,
                    bias=smalls[:, bias_base + mchunk:bias_base + mchunk + 1]),
                    r=[ps_b, bias_b], w=[stg_b[s]], tag="l3pj_ev")
            c2.store(dst[m0:m0 + msz, n0:n0 + nsz], st[0:msz, 0:nsz], stg_b[s],
                     wbufs=[dst_b], q="act", tag="l3pj_st")
        return ev
    rhs_xn = ("sbuf", xnt, xnb, lambda k, n: (k // 128) * T + n)
    linear(ctx, tag="l3q", lhsT=("dram_km", wbf["Wq"][0], wbf["Wq"][1]), rhs=rhs_xn,
           K=D, M=D, N=T, evac=mk_proj_evac(qT, qT_b, 280), m_tile=1024)
    linear(ctx, tag="l3k", lhsT=("dram_km", wbf["Wk"][0], wbf["Wk"][1]), rhs=rhs_xn,
           K=D, M=D, N=T, evac=mk_proj_evac(kT, kT_b, 288), m_tile=1024)
    linear(ctx, tag="l3g", lhsT=("dram_km", wbf["Wg"][0], wbf["Wg"][1]), rhs=rhs_xn,
           K=D, M=D, N=T, evac=mk_proj_evac(gT, gT_b, 296, AF.Silu), m_tile=1024)
    wsl0 = ctx.tensors["wsl0"]; wsl0b = ctx.bufs["wsl0"]
    for k in range(NC):
        ctx.emit("sp", lambda e, k=k: e.dma_start(
            wsl0[:, k * 1024:(k + 1) * 1024],
            wbf["Wv"][0][k * 128:(k + 1) * 128, 0:D]),
            r=[wbf["Wv"][1]], w=[wsl0b], dma=True, tag="l3_wv")
    pi = [0]
    for tc in range(T // 128):
        for ot in range(2):
            psn, psb_ = (pf0, pf0b) if pi[0] % 2 == 0 else (pf1, pf1b)
            pi[0] += 1
            def mmv(e, tc=tc, ot=ot, psn=psn):
                for k in range(NC):
                    e.matmul(psn[0:128, 0:512],
                             xnt[:, k * T + tc * 128: k * T + (tc + 1) * 128],
                             wsl0[:, k * 1024 + ot * 512: k * 1024 + ot * 512 + 512],
                             start=(k == 0), stop=False)
                return e.matmul(psn[0:128, 0:512], orow,
                                bv_row[:, ot * 512: ot * 512 + 512], start=False, stop=True)
            ctx.emit("pe", mmv, r=[xnb, wsl0b, orb, bv_b], w=[psb_], tag="l3v_mm")
            s = sj[0] % 2; sj[0] += 1
            st = ab[:, s * 512: s * 512 + 512]
            ctx.emit("act", lambda e, st=st, psn=psn: e.copy(st, psn[0:128, 0:512]),
                     r=[psb_], w=[stg_b[s]], tag="l3v_ev")
            ctx.store(vtok[tc * 128:(tc + 1) * 128, ot * 512:ot * 512 + 512], st,
                      stg_b[s], wbufs=[vtok_b], q="act", tag="l3v_st")

    new_phase(ctx)
    emit_rotary_pass(ctx, "l3rq", qT, qT_b, hA, hB, htabs_b, hPT)
    new_phase(ctx)
    emit_rotary_pass(ctx, "l3rk", kT, kT_b, hAk, hBk, htabs_b, hPT)

    new_phase(ctx)
    ctx.load(af[:, 8192:8192 + 512], hGC[0:128, 0:512], ctx.buf("v_l3_gc"), tag="l3_gcld")
    ctx.load(af[0:128, 8704:8704 + 128], hTRI[0:128, 0:128], ctx.buf("v_l3_tri"), tag="l3_trild")
    ctx.load(rows[0:16, 0:1024], hSEL[0:16, :], ctx.buf("v_l3_sel"), tag="l3_selld")
    ctx.load(rows[0:16, 1024:1024 + T], hch[0:16, 0:T], ctx.buf("v_l3_chld"), tag="l3_chld")
    gq_b = ctx.buf("v_l3_gq")
    kq_b = ctx.buf("v_l3_kq")
    vv_b = ctx.buf("v_l3_vv")
    eE_b = ctx.buf("v_l3_E")
    rt_b = ctx.buf("v_l3_rt")
    ac_b = ctx.buf("v_l3_acc")
    for h in range(NH):
        ch_, hrow = h // 2, (h % 2) * 64
        ctx.load(ab[0:64, 0:4096], kT[ch_ * 128 + hrow: ch_ * 128 + hrow + 64, 0:T],
                 kq_b, rbufs=[kT_b], tag="l3r_kld")
        ctx.load(ab[0:64, 4096:8192], qT[ch_ * 128 + hrow: ch_ * 128 + hrow + 64, 0:T],
                 gq_b, rbufs=[qT_b], tag="l3r_qld")
        for kv in range(32):
            ctx.load(ab[:, 8192 + kv * 64: 8192 + (kv + 1) * 64],
                     vtok[kv * 128:(kv + 1) * 128, h * 64:(h + 1) * 64],
                     vv_b, rbufs=[vtok_b], tag="l3r_vld")
        ctx.load(af[0:64, 11264:11776], hC4[h * 64:(h + 1) * 64, 0:512],
                 ctx.buf("v_l3_c4"), tag="l3r_c4ld")
        for qt in range(NTT):
            t0 = qt * TT
            c4ap = af[0:64, 11264:11776]
            qstg = ab[0:64, 10240:10752]
            ctx.emit("dve", lambda e, t0=t0, c4ap=c4ap, qstg=qstg: e.tensor_tensor(
                qstg, ab[0:64, 4096 + t0:4096 + t0 + TT], c4ap, ALU.mult),
                r=[gq_b, ctx.buf("v_l3_c4")], w=[ctx.buf("v_l3_qstg")], tag="l3r_qc4")
            kcmax = 4 * qt + 3
            for kc in range(kcmax + 1):
                psn, psb_ = (pf0, pf0b) if kc % 2 == 0 else (pf1, pf1b)
                ctx.emit("pe", lambda e, kc=kc, psn=psn, qstg=qstg: e.matmul(
                    psn[0:128, 0:TT], ab[0:64, kc * 128:(kc + 1) * 128], qstg,
                    start=True, stop=True), r=[kq_b, ctx.buf("v_l3_qstg")], w=[psb_], tag="l3r_mmS")
                d2 = 4 * qt - kc
                gcol = af[:, 8192 + h * 32 + (d2 + 3): 8192 + h * 32 + (d2 + 3) + 1]
                Et = ab[0:128, 10752 + (kc % 2) * 512: 10752 + (kc % 2) * 512 + 512]
                eb = ctx.buf("v_l3_E" + str(kc % 2))
                if kc < 4 * qt:
                    ctx.emit("act", lambda e, psn=psn, gcol=gcol, Et=Et: e.activation(
                        Et, psn[0:128, 0:TT], AF.Identity, scale=gcol),
                        r=[psb_, ctx.buf("v_l3_gc")], w=[eb], tag="l3r_Eev")
                else:
                    j = kc - 4 * qt
                    if j > 0:
                        ctx.emit("dve", lambda e, Et=Et, j=j: e.memset(Et[:, 0:j * 128], 0.0),
                                 r=[], w=[eb], tag="l3r_Ez")
                    ctx.emit("dve", lambda e, psn=psn, gcol=gcol, Et=Et, j=j: e.scalar_tensor_tensor(
                        Et[:, j * 128:(j + 1) * 128], psn[0:128, j * 128:(j + 1) * 128],
                        gcol, af[0:128, 8704:8704 + 128], ALU.mult, ALU.mult),
                        r=[psb_, ctx.buf("v_l3_gc"), ctx.buf("v_l3_tri")], w=[eb], tag="l3r_Etri")
                    if j < 3:
                        ctx.emit("act", lambda e, psn=psn, gcol=gcol, Et=Et, j=j: e.activation(
                            Et[:, (j + 1) * 128:512], psn[0:128, (j + 1) * 128:512],
                            AF.Identity, scale=gcol),
                            r=[psb_, ctx.buf("v_l3_gc")], w=[eb], tag="l3r_Eev2")
                ctx.emit("pe", lambda e, kc=kc, Et=Et, kcmax=kcmax: e.matmul(
                    pf2[0:64, 0:TT], ab[:, 8192 + kc * 64: 8192 + (kc + 1) * 64], Et,
                    start=(kc == 0), stop=(kc == kcmax)), r=[vv_b, eb], w=[pf2b], tag="l3r_av")
            ctx.emit("pe", lambda e, h=h, t0=t0: e.matmul(
                pf3[0:64, 0:TT], rows[0:16, h * 64:(h + 1) * 64],
                rows[0:16, 1024 + t0:1024 + t0 + TT], start=True, stop=True),
                r=[ctx.buf("v_l3_sel"), ctx.buf("v_l3_chld")], w=[pf3b], tag="l3r_chB")
            rt = af[:, 9216 + 0:9216 + 512]
            ctx.emit("act", lambda e, rt=rt: e.copy(rt[0:64, 0:TT], pf2[0:64, 0:TT]),
                     r=[pf2b], w=[rt_b], tag="l3r_rev")
            ctx.emit("dve", lambda e, rt=rt: e.tensor_tensor(rt[0:64, 0:TT], rt[0:64, 0:TT],
                                                             pf3[0:64, 0:TT], ALU.mult),
                     r=[rt_b, pf3b], w=[rt_b], tag="l3r_rmul")
            sumc = af[0:64, 9984 + h:9984 + h + 1]
            sqsumc = af[0:64, 10016 + h:10016 + h + 1]
            red = af[0:64, 10048:10049]
            sq = af[0:64, 10752:10752 + 512]
            ctx.emit("dve", lambda e, red=red, rt=rt: e.reduce_sum(red[0:64, 0:1], rt[0:64, 0:TT], axis=AX.X),
                     r=[rt_b], w=[ac_b], tag="l3r_s1")
            if qt == 0:
                ctx.emit("dve", lambda e, sumc=sumc, red=red: e.tensor_copy(sumc, red[0:64, 0:1]),
                         r=[ac_b], w=[ac_b], tag="l3r_s2")
            else:
                ctx.emit("dve", lambda e, sumc=sumc, red=red: e.tensor_add(sumc, sumc, red[0:64, 0:1]),
                         r=[ac_b], w=[ac_b], tag="l3r_s2b")
            ctx.emit("act", lambda e, sq=sq, rt=rt: e.activation(sq[0:64, 0:TT], rt[0:64, 0:TT],
                                                                 AF.Square, accum_out=red[0:64, 0:1]),
                     r=[rt_b], w=[ac_b], tag="l3r_sq")
            if qt == 0:
                ctx.emit("dve", lambda e, sqsumc=sqsumc, red=red: e.tensor_copy(sqsumc, red[0:64, 0:1]),
                         r=[ac_b], w=[ac_b], tag="l3r_s3")
            else:
                ctx.emit("dve", lambda e, sqsumc=sqsumc, red=red: e.tensor_add(sqsumc, sqsumc, red[0:64, 0:1]),
                         r=[ac_b], w=[ac_b], tag="l3r_s3b")
            rst = ab[0:64, 11776:12288]
            ctx.emit("dve", lambda e, rst=rst, rt=rt: e.tensor_copy(rst, rt[0:64, 0:TT]),
                     r=[rt_b], w=[ctx.buf("v_l3_rst")], tag="l3r_rstg")
            ctx.store(ret[h * 64:(h + 1) * 64, t0:t0 + TT], rst, ctx.buf("v_l3_rst"),
                      wbufs=[ret_b], q="act", tag="l3r_rst")
    return wbf, qT, qT_b, kT, kT_b, gT, gT_b, vtok, vtok_b, ret, ret_b


def emit_layer3_attn_tail(ctx, H, wbf, gT, gT_b, ret, ret_b, y3, y3_b, hBH):
    af = ctx.tensors["af"]
    ab = ctx.tensors["ab"]
    rows = ctx.tensors["rows"]
    smalls = ctx.tensors["smalls"]
    oc, ocb = ones_col(ctx)
    pf4 = ctx.tensors["pf4"]; pf4b = ctx.bufs["pf4"]
    pf5 = ctx.tensors["pf5"]; pf5b = ctx.bufs["pf5"]
    pf0 = ctx.tensors["pf0"]; pf0b = ctx.bufs["pf0"]
    pf1 = ctx.tensors["pf1"]; pf1b = ctx.bufs["pf1"]
    identf = ctx.tensors["identf"]; identf_b = ctx.bufs["identf"]
    ac_b = ctx.buf("v_l3_acc")
    gn_b = ctx.buf("v_l3_gn")
    bias_b = ctx.buf("v_l3_bias")
    new_phase(ctx)
    rowsum = rows[0:1, 5120:5136]
    rowsq = rows[0:1, 5136:5152]
    tb = ctx.buf("v_l3_tot")
    ctx.emit("pe", lambda e: e.matmul(pf4[0:1, 0:16], oc[0:64, 0:1], af[0:64, 9984:10000],
                                      start=True, stop=True), r=[ac_b, ocb], w=[pf4b], tag="l3t_t1")
    ctx.emit("dve", lambda e: e.tensor_copy(rowsum, pf4[0:1, 0:16]), r=[pf4b], w=[tb], tag="l3t_t1e")
    ctx.emit("pe", lambda e: e.matmul(pf4[0:1, 0:16], oc[0:64, 0:1], af[0:64, 10016:10032],
                                      start=True, stop=True), r=[ac_b, ocb], w=[pf4b], tag="l3t_t2")
    ctx.emit("dve", lambda e: e.tensor_copy(rowsq, pf4[0:1, 0:16]), r=[pf4b], w=[tb], tag="l3t_t2e")
    scol = af[0:16, 10048:10049]
    sqcol = af[0:16, 10049:10050]
    ctx.emit("pe", lambda e: e.transpose(pf5[0:16, 0:1], rowsum, identf[0:1, 0:1]),
             r=[tb, identf_b], w=[pf5b], tag="l3t_tr1")
    ctx.emit("dve", lambda e: e.tensor_copy(scol, pf5[0:16, 0:1]), r=[pf5b], w=[tb], tag="l3t_tr1e")
    ctx.emit("pe", lambda e: e.transpose(pf5[0:16, 0:1], rowsq, identf[0:1, 0:1]),
             r=[tb, identf_b], w=[pf5b], tag="l3t_tr2")
    ctx.emit("dve", lambda e: e.tensor_copy(sqcol, pf5[0:16, 0:1]), r=[pf5b], w=[tb], tag="l3t_tr2e")
    NTOT = float(T * 64)
    mu = af[0:16, 10050:10051]
    msq = af[0:16, 10051:10052]
    va = af[0:16, 10052:10053]
    acol = af[0:16, 10053:10054]
    bcol = af[0:16, 10054:10055]
    ctx.emit("dve", lambda e: e.tensor_single_scalar(mu, scol, 1.0 / NTOT, ALU.mult),
             r=[tb], w=[tb], tag="l3t_mu")
    ctx.emit("dve", lambda e: e.tensor_single_scalar(msq, sqcol, 1.0 / NTOT, ALU.mult),
             r=[tb], w=[tb], tag="l3t_msq")
    ctx.emit("act", lambda e: e.activation(va, mu, AF.Square), r=[tb], w=[tb], tag="l3t_mu2")
    ctx.emit("dve", lambda e: e.tensor_tensor(va, msq, va, ALU.subtract), r=[tb], w=[tb], tag="l3t_var")
    ctx.emit("dve", lambda e: e.tensor_single_scalar(va, va, 1e-5, ALU.add), r=[tb], w=[tb], tag="l3t_vej")
    ctx.emit("dve", lambda e: e.reciprocal(va, va), r=[tb], w=[tb], tag="l3t_rec")
    ctx.emit("act", lambda e: e.activation(va, va, AF.Sqrt), r=[tb], w=[tb], tag="l3t_rstd")
    ctx.emit("dve", lambda e: e.tensor_tensor(acol, va, smalls[0:16, 336:337], ALU.mult),
             r=[tb, gn_b], w=[tb], tag="l3t_a")
    ctx.emit("dve", lambda e: e.scalar_tensor_tensor(bcol, mu, -1.0, acol, ALU.mult, ALU.mult),
             r=[tb], w=[tb], tag="l3t_b1")
    ctx.emit("dve", lambda e: e.tensor_tensor(bcol, bcol, smalls[0:16, 337:338], ALU.add),
             r=[tb, gn_b], w=[tb], tag="l3t_b2")
    ctx.load(rows[0:16, 0:1024], hBH[0:16, :], ctx.buf("v_l3_bh"), tag="l3t_bhld")
    AC = af[:, 10056:10064]
    BC = af[:, 10064:10072]
    acb = ctx.buf("v_l3_ACB")
    for c in range(NC):
        ctx.emit("pe", lambda e, c=c: e.matmul(pf5[0:128, 0:1], rows[0:16, c * 128:(c + 1) * 128],
                                               acol, start=True, stop=True),
                 r=[ctx.buf("v_l3_bh"), tb], w=[pf5b], tag="l3t_ac")
        ctx.emit("dve", lambda e, c=c: e.tensor_copy(AC[:, c:c + 1], pf5[0:128, 0:1]),
                 r=[pf5b], w=[acb], tag="l3t_ace")
        ctx.emit("pe", lambda e, c=c: e.matmul(pf5[0:128, 0:1], rows[0:16, c * 128:(c + 1) * 128],
                                               bcol, start=True, stop=True),
                 r=[ctx.buf("v_l3_bh"), tb], w=[pf5b], tag="l3t_bc")
        ctx.emit("dve", lambda e, c=c: e.tensor_copy(BC[:, c:c + 1], pf5[0:128, 0:1]),
                 r=[pf5b], w=[acb], tag="l3t_bce")
    wsl0 = ctx.tensors["wsl0"]; wsl0b = ctx.bufs["wsl0"]
    for k in range(NC):
        ctx.emit("sp", lambda e, k=k: e.dma_start(
            wsl0[:, k * 1024:(k + 1) * 1024], wbf["Wo"][0][k * 128:(k + 1) * 128, 0:D]),
            r=[wbf["Wo"][1]], w=[wsl0b], dma=True, tag="l3t_wo")
    rt_b2 = ctx.buf("v_l3t_rt")
    gt_b2 = ctx.buf("v_l3t_gt")
    st_b2 = ctx.buf("v_l3t_st")
    for ti in range(NTT):
        t0 = ti * TT
        for c in range(NC):
            ctx.load(ab[:, c * TT:(c + 1) * TT], ret[c * 128:(c + 1) * 128, t0:t0 + TT],
                     rt_b2, rbufs=[ret_b], tag="l3t_rld")
            ctx.load(ab[:, 4096 + c * TT:4096 + (c + 1) * TT], gT[c * 128:(c + 1) * 128, t0:t0 + TT],
                     gt_b2, rbufs=[gT_b], tag="l3t_gld")
            ctx.emit("dve", lambda e, c=c: e.tensor_scalar(
                ab[:, c * TT:(c + 1) * TT], ab[:, c * TT:(c + 1) * TT],
                AC[:, c:c + 1], BC[:, c:c + 1], ALU.mult, ALU.add),
                r=[rt_b2, acb], w=[rt_b2], tag="l3t_norm")
            ctx.emit("dve", lambda e, c=c: e.tensor_tensor(
                ab[:, c * TT:(c + 1) * TT], ab[:, c * TT:(c + 1) * TT],
                ab[:, 4096 + c * TT:4096 + (c + 1) * TT], ALU.mult),
                r=[rt_b2, gt_b2], w=[rt_b2], tag="l3t_gmul")
        for m in range(NC):
            psn, psb_ = (pf0, pf0b) if m % 2 == 0 else (pf1, pf1b)
            def mmo(e, m=m, psn=psn):
                inst = None
                for k in range(NC):
                    inst = e.matmul(psn[0:128, 0:TT],
                                    wsl0[:, k * 1024 + m * 128: k * 1024 + (m + 1) * 128],
                                    ab[:, k * TT:(k + 1) * TT], start=(k == 0), stop=(k == NC - 1))
                return inst
            ctx.emit("pe", mmo, r=[wsl0b, rt_b2], w=[psb_], tag="l3t_mm")
            st = af[:, 10752 + (m % 2) * 512:10752 + (m % 2) * 512 + TT]
            ctx.emit("act", lambda e, st=st, psn=psn, m=m: e.activation(
                st, psn[0:128, 0:TT], AF.Identity, bias=smalls[:, 340 + m:341 + m]),
                r=[psb_, bias_b], w=[st_b2], tag="l3t_ev")
            ctx.store(y3[m * 128:(m + 1) * 128, t0:t0 + TT], st, st_b2,
                      wbufs=[y3_b], q="act", tag="l3t_yst")


def l3_ffn_consts():
    DH = 64
    WIN = 512
    half_idx = np.arange(0, DH, 2, dtype=np.float64)
    inv = 1.0 / 10000.0 ** (half_idx / DH)
    t = np.arange(T, dtype=np.float64)
    fr = t[:, None] * inv[None, :]
    fr = np.concatenate([fr, fr], axis=1)
    base = (half_idx + 0.4 * DH) / (1.4 * DH)
    power = (t - T // 2) / float(WIN // 2)
    sc = base[None, :] ** power[:, None]
    sc = np.concatenate([sc, sc], axis=1)
    cos = np.cos(fr); sin = np.sin(fr)
    Aq = (cos * sc).T; Bq = (sin * sc).T
    Ak = (cos / sc).T; Bk = (sin / sc).T
    tile2 = lambda M: np.tile(M, (2, 1)).astype(np.float32)
    P = np.zeros((128, 128))
    for blk in range(2):
        o = blk * 64
        for i in range(32):
            P[o + i, o + 32 + i] = -1.0
            P[o + 32 + i, o + i] = 1.0
    PT2 = P.T.astype(ml_dtypes.bfloat16)
    masks = np.zeros((8, 128, 1024), np.float32)
    NEG = -1e30
    for w0 in range(2):
        for b in range(4):
            i_l = b * 128 + np.arange(128)[:, None]
            jj = np.arange(1024)[None, :]
            bad = (jj < i_l) | (jj > i_l + 512)
            if w0 == 0:
                bad = bad | (jj < 512)
            masks[w0 * 4 + b] = np.where(bad, NEG, 0.0)
    BP8 = np.zeros((8, 4 * 128), np.float32)
    for c in range(4):
        for j2 in range(2):
            BP8[2 * c + j2, c * 128 + j2 * 64: c * 128 + (j2 + 1) * 64] = 1.0
    BP8T = BP8.T.astype(ml_dtypes.bfloat16)
    return tile2(Aq), tile2(Bq), tile2(Ak), tile2(Bk), PT2, masks, BP8, BP8T


def emit_l2norm_rotary(ctx, tag, srcdram, src_b, hA, hB, hPT2, hBP8, hBP8T, sc_col, sc_col_b, eps_clip=1e-12):
    af = ctx.tensors["af"]
    ab = ctx.tensors["ab"]
    rows = ctx.tensors["rows"]
    smallsb = ctx.tensors["smallsb"]
    oc, ocb = ones_col(ctx)
    pf0 = ctx.tensors["pf0"]; pf0b = ctx.bufs["pf0"]
    pf1 = ctx.tensors["pf1"]; pf1b = ctx.bufs["pf1"]
    pf4 = ctx.tensors["pf4"]; pf4b = ctx.bufs["pf4"]
    htabs_b = ctx.buf("v_l3f_tabs")
    bp_b = ctx.buf("v_l3f_bp")
    nb = ctx.buf("v_l3f_n")
    chb = ctx.buf("v_l3f_ch")
    t1b = ctx.buf("v_l3f_t1")
    stb = ctx.buf("v_l3f_st")
    ctx.load(af[:, 0:4096], hA[0:128, 0:T], htabs_b, tag=tag + "_A")
    ctx.load(af[:, 4096:8192], hB[0:128, 0:T], htabs_b, tag=tag + "_B")
    ctx.load(smallsb[:, 128:256], hPT2[0:128, 0:128], ctx.buf("v_l3_PT2"), tag=tag + "_P")
    ctx.load(rows[0:8, 0:512], hBP8[0:8, :], bp_b, tag=tag + "_BP")
    for c4_ in range(4):
        ctx.load(smallsb[0:128, 256 + c4_ * 8:256 + (c4_ + 1) * 8],
                 hBP8T[c4_ * 128:(c4_ + 1) * 128, 0:8], bp_b, tag=tag + "_BPT")
    for c in range(4):
        ctx.load(ab[:, 0:4096], srcdram[c * 128:(c + 1) * 128, 0:T], chb,
                 rbufs=[src_b], tag=tag + "_ld1")
        for tt in range(NTT):
            t0 = tt * TT
            sq = ab[:, 4096:4608]
            ctx.emit("act", lambda e, t0=t0, sq=sq: e.activation(sq, ab[:, t0:t0 + TT], AF.Square),
                     r=[chb], w=[t1b], tag=tag + "_sq")
            ctx.emit("pe", lambda e, c=c, sq=sq: e.matmul(
                pf4[0:8, 0:TT], smallsb[:, 256 + c * 8:256 + (c + 1) * 8], sq, start=True, stop=True),
                r=[bp_b, t1b], w=[pf4b], tag=tag + "_nmm")
            if c == 0:
                ctx.emit("dve", lambda e, t0=t0: e.tensor_copy(rows[0:8, 1024 + t0:1024 + t0 + TT],
                                                               pf4[0:8, 0:TT]),
                         r=[pf4b], w=[nb], tag=tag + "_nev")
            else:
                ctx.emit("dve", lambda e, t0=t0: e.tensor_tensor(
                    rows[0:8, 1024 + t0:1024 + t0 + TT], rows[0:8, 1024 + t0:1024 + t0 + TT],
                    pf4[0:8, 0:TT], ALU.add), r=[nb, pf4b], w=[nb], tag=tag + "_nacc")
    for t0, tsz in chunks_of(T, TT):
        r_act(ctx, rows[0:8, 1024 + t0:1024 + t0 + tsz], nb, rows[0:8, 1024 + t0:1024 + t0 + tsz], nb, AF.Sqrt)
        r_ts(ctx, rows[0:8, 1024 + t0:1024 + t0 + tsz], nb, rows[0:8, 1024 + t0:1024 + t0 + tsz], nb,
             eps_clip, ALU.max)
        r_recip(ctx, rows[0:8, 1024 + t0:1024 + t0 + tsz], nb, rows[0:8, 1024 + t0:1024 + t0 + tsz], nb)
    for c in range(4):
        ctx.load(ab[:, 0:4096], srcdram[c * 128:(c + 1) * 128, 0:T], chb,
                 rbufs=[src_b], tag=tag + "_ld2")
        for tt in range(NTT):
            t0 = tt * TT
            ctx.emit("pe", lambda e, c=c, t0=t0: e.matmul(
                pf0[0:128, 0:TT], rows[0:8, c * 128:(c + 1) * 128],
                rows[0:8, 1024 + t0:1024 + t0 + TT], start=True, stop=True),
                r=[bp_b, nb], w=[pf0b], tag=tag + "_rqB")
            t1 = af[:, 8192:8704]
            ctx.emit("dve", lambda e, t0=t0, t1=t1: e.tensor_scalar_mul(
                t1, ab[:, t0:t0 + TT], sc_col), r=[chb, sc_col_b], w=[t1b], tag=tag + "_sc")
            ctx.emit("dve", lambda e, t1=t1: e.tensor_tensor(t1, t1, pf0[0:128, 0:TT], ALU.mult),
                     r=[t1b, pf0b], w=[t1b], tag=tag + "_rq")
            nstg = ab[:, 4096:4608]
            ctx.emit("dve", lambda e, t1=t1, nstg=nstg: e.tensor_copy(nstg, t1),
                     r=[t1b], w=[t1b], tag=tag + "_nst")
            ctx.emit("pe", lambda e, nstg=nstg: e.matmul(
                pf1[0:128, 0:TT], smallsb[:, 128:256], nstg, start=True, stop=True),
                r=[ctx.buf("v_l3_PT2"), t1b], w=[pf1b], tag=tag + "_rot")
            t2 = af[:, 8704:9216]
            ctx.emit("dve", lambda e, t0=t0, t1=t1, t2=t2: e.tensor_tensor(
                t2, t1, af[:, t0:t0 + TT], ALU.mult), r=[t1b, htabs_b], w=[t1b], tag=tag + "_m1")
            t3 = af[:, 9216:9728]
            ctx.emit("dve", lambda e, t0=t0, t3=t3: e.tensor_tensor(
                t3, pf1[0:128, 0:TT], af[:, 4096 + t0:4096 + t0 + TT], ALU.mult),
                r=[pf1b, htabs_b], w=[t1b], tag=tag + "_m2")
            st = ab[:, 4608 + (tt % 2) * TT:4608 + (tt % 2) * TT + TT]
            ctx.emit("dve", lambda e, t2=t2, t3=t3, st=st: e.tensor_tensor(st, t2, t3, ALU.add),
                     r=[t1b], w=[stb], tag=tag + "_m3")
            ctx.store(srcdram[c * 128:(c + 1) * 128, t0:t0 + TT], st, stb,
                      wbufs=[src_b], q="act", tag=tag + "_st")


def emit_layer3_ffn(ctx, H, xinner, xinner_b):
    af = ctx.tensors["af"]
    ab = ctx.tensors["ab"]
    rows = ctx.tensors["rows"]
    smalls = ctx.tensors["smalls"]
    smallsb = ctx.tensors["smallsb"]
    xnt = ctx.tensors["xn"]; xnb = ctx.bufs["xn"]
    oc, ocb = ones_col(ctx)
    orow, orb = ones_row(ctx, 128)
    identb = ctx.tensors["identb"]; identb_b = ctx.bufs["identb"]
    pf0 = ctx.tensors["pf0"]; pf0b = ctx.bufs["pf0"]
    pf1 = ctx.tensors["pf1"]; pf1b = ctx.bufs["pf1"]
    pf2 = ctx.tensors["pf2"]; pf2b = ctx.bufs["pf2"]
    pb = ctx.tensors["pb"]; pb_b = ctx.bufs["pb"]
    INNER, HID = 512, 2730
    WIN = 512
    NW = T // WIN
    Aq, Bq, Ak, Bk, PT2, masks, BP8, BP8T = l3_ffn_consts()
    hBP8T = ctx.const("l3fBP8T", BP8T)
    hAq = ctx.const("l3fAq", Aq); hBq = ctx.const("l3fBq", Bq)
    hAk = ctx.const("l3fAk", Ak); hBk = ctx.const("l3fBk", Bk)
    hPT2 = ctx.const("l3fPT2", PT2)
    hMK = ctx.const("l3fmasks", masks.reshape(8 * 128, 1024))
    hBP8 = ctx.const("l3fBP8", BP8)

    wbf = {}
    wbf["wqkv"] = cast_weight(ctx, "l3fqkv", H["wqkv"][0], H["wqkv"][1], 2 * D, 3 * INNER)
    wbf["wo"] = cast_weight(ctx, "l3fwo", H["wo"][0], H["wo"][1], 2 * INNER, D)
    wbf["w1"] = cast_weight(ctx, "l3fw1", H["w1"][0], H["w1"][1], 2 * D, 2 * HID)
    wbf["w2"] = cast_weight(ctx, "l3fw2", H["w2"][0], H["w2"][1], 2 * HID, D)

    q2 = ctx.dram("l3f_q", (INNER, T), BF16); q2_b = ctx.buf("l3f_q")
    k2 = ctx.dram("l3f_k", (INNER, T), BF16); k2_b = ctx.buf("l3f_k")
    v2 = ctx.dram("l3f_v", (T, INNER), BF16); v2_b = ctx.buf("l3f_v")
    o2 = ctx.dram("l3f_o", (INNER, T), BF16); o2_b = ctx.buf("l3f_o")
    hg = ctx.dram("l3f_hg", (HID, T), BF16); hg_b = ctx.buf("l3f_hg")
    h2 = ctx.dram("l3f_h2", (HID, T), BF16); h2_b = ctx.buf("l3f_h2")
    stg_b = [ctx.buf("v_l3f_st0"), ctx.buf("v_l3f_st1")]
    sj = [0]

    for li in range(2):
        new_phase(ctx)
        hx_b = ctx.buf("v_l3f_hx")
        for c in range(NC):
            for tt in range(NTT):
                t0 = tt * TT
                if c < 4:
                    ctx.emit("pool", lambda e, c=c, t0=t0: e.dma_start(
                        xnt[:, c * T + t0: c * T + t0 + TT],
                        xinner[c * 128:(c + 1) * 128, t0:t0 + TT]),
                        r=[xinner_b], w=[hx_b], dma=True, tag="l3f_hx")
                elif t0 > 0:
                    ctx.emit("pool", lambda e, c=c, t0=t0: e.dma_start(
                        xnt[:, c * T + t0: c * T + t0 + TT],
                        xinner[c * 128:(c + 1) * 128, t0 - 1:t0 + TT - 1]),
                        r=[xinner_b], w=[hx_b], dma=True, tag="l3f_hxs")
                else:
                    ctx.emit("pool", lambda e, c=c: e.dma_start(
                        xnt[:, c * T + 1: c * T + TT],
                        xinner[c * 128:(c + 1) * 128, 0:TT - 1]),
                        r=[xinner_b], w=[hx_b], dma=True, tag="l3f_hx0")
                    ctx.emit("dve", lambda e, c=c: e.memset(xnt[:, c * T: c * T + 1], 0.0),
                             r=[], w=[hx_b], tag="l3f_hx0z")
        rhs_hx = ("sbuf", xnt, hx_b, lambda k, n: (k // 128) * T + n)

        def mk_sp_evac(dst, dst_b):
            def ev(c2, m0, msz, n0, nsz, ps_t, ps_b):
                s = sj[0] % 2; sj[0] += 1
                st = ab[:, s * 512: s * 512 + nsz]
                c2.emit("act", lambda e: e.copy(st[0:msz, 0:nsz], ps_t[0:msz, 0:nsz]),
                        r=[ps_b], w=[stg_b[s]], tag="l3f_pjev")
                c2.store(dst[m0:m0 + msz, n0:n0 + nsz], st[0:msz, 0:nsz], stg_b[s],
                         wbufs=[dst_b], q="act", tag="l3f_pjst")
            return ev
        linear(ctx, tag="l3fq", lhsT=("dram_km", wbf["wqkv"][0], wbf["wqkv"][1]), rhs=rhs_hx,
               K=D, M=INNER, N=T, evac=mk_sp_evac(q2, q2_b), m_tile=512, k_base=li * D)
        linear(ctx, tag="l3fk", lhsT=("dram_km", wbf["wqkv"][0], wbf["wqkv"][1]), rhs=rhs_hx,
               K=D, M=INNER, N=T, evac=mk_sp_evac(k2, k2_b), m_tile=512, k_base=li * D, m_base=INNER)
        wsl0 = ctx.tensors["wsl0"]; wsl0b = ctx.bufs["wsl0"]
        for k in range(NC):
            ctx.emit("sp", lambda e, k=k, li=li: e.dma_start(
                wsl0[:, k * 512:(k + 1) * 512],
                wbf["wqkv"][0][li * D + k * 128: li * D + (k + 1) * 128, 2 * INNER:3 * INNER]),
                r=[wbf["wqkv"][1]], w=[wsl0b], dma=True, tag="l3f_wv")
        for tc in range(T // 128):
            psn, psb_ = (pf0, pf0b) if tc % 2 == 0 else (pf1, pf1b)
            def mmv(e, tc=tc, psn=psn):
                inst = None
                for k in range(NC):
                    inst = e.matmul(psn[0:128, 0:512],
                                    xnt[:, k * T + tc * 128: k * T + (tc + 1) * 128],
                                    wsl0[:, k * 512:(k + 1) * 512],
                                    start=(k == 0), stop=(k == NC - 1))
                return inst
            ctx.emit("pe", mmv, r=[hx_b, wsl0b], w=[psb_], tag="l3f_vmm")
            s = sj[0] % 2; sj[0] += 1
            st = ab[:, s * 512: s * 512 + 512]
            ctx.emit("act", lambda e, st=st, psn=psn: e.copy(st, psn[0:128, 0:512]),
                     r=[psb_], w=[stg_b[s]], tag="l3f_vev")
            ctx.store(v2[tc * 128:(tc + 1) * 128, 0:INNER], st, stg_b[s],
                      wbufs=[v2_b], q="act", tag="l3f_vst")

        qs_col = smalls[:, 460:461]
        ks_col = smalls[:, 461:462]
        sc_b = ctx.buf("v_l3f_qs")
        ctx.load(smalls[0:64, 460:461], H["q_scale"][0][li * 64:(li + 1) * 64, 0:1], sc_b,
                 rbufs=[H["q_scale"][1]], tag="l3f_qs1")
        ctx.emit("sp", lambda e: e.dma_start(smalls[64:128, 460:461],
                                             H["q_scale"][0][li * 64:(li + 1) * 64, 0:1]),
                 r=[H["q_scale"][1]], w=[sc_b], dma=True, tag="l3f_qs3")
        ctx.load(smalls[0:64, 461:462], H["k_scale"][0][li * 64:(li + 1) * 64, 0:1], sc_b,
                 rbufs=[H["k_scale"][1]], tag="l3f_ks1")
        ctx.emit("sp", lambda e: e.dma_start(smalls[64:128, 461:462],
                                             H["k_scale"][0][li * 64:(li + 1) * 64, 0:1]),
                 r=[H["k_scale"][1]], w=[sc_b], dma=True, tag="l3f_ks2")
        ctx.emit("dve", lambda e: e.tensor_single_scalar(qs_col, qs_col, 8.0, ALU.mult),
                 r=[sc_b], w=[sc_b], tag="l3f_qs8")

        new_phase(ctx)
        emit_l2norm_rotary(ctx, "l3fnq", q2, q2_b, hAq, hBq, hPT2, hBP8, hBP8T, qs_col, sc_b)
        new_phase(ctx)
        emit_l2norm_rotary(ctx, "l3fnk", k2, k2_b, hAk, hBk, hPT2, hBP8, hBP8T, ks_col, sc_b)

        new_phase(ctx)
        mk_b = ctx.buf("v_l3f_mk")
        for v_ in range(8):
            ctx.load(af[:, v_ * 1024:(v_ + 1) * 1024], hMK[v_ * 128:(v_ + 1) * 128, :],
                     mk_b, tag="l3f_mkld")
        q_b2 = ctx.buf("v_l3f_aq")
        k_b2 = ctx.buf("v_l3f_ak")
        v_b2 = ctx.buf("v_l3f_av")
        e_b2 = ctx.buf("v_l3f_aE")
        pt_b2 = ctx.buf("v_l3f_aPT")
        col_b2 = ctx.buf("v_l3f_acol")
        AB_Q, AB_K, AB_V, AB_E, AB_PT = 0, 2048, 6144, 10240, 11264
        for w in range(NW):
            kv0 = max(0, (w - 1) * WIN)
            for cc in range(4):
                ctx.load(ab[:, AB_Q + cc * 512:AB_Q + (cc + 1) * 512],
                         q2[cc * 128:(cc + 1) * 128, w * WIN:(w + 1) * WIN], q_b2,
                         rbufs=[q2_b], tag="l3f_aqld")
                koff = AB_K + cc * 1024 + (512 if w == 0 else 0)
                ctx.load(ab[:, koff:AB_K + (cc + 1) * 1024],
                         k2[cc * 128:(cc + 1) * 128, kv0:(w + 1) * WIN], k_b2,
                         rbufs=[k2_b], tag="l3f_akld")
            nv = 4 if w == 0 else 8
            for v_ in range(nv):
                voff = AB_V + (v_ + (4 if w == 0 else 0)) * 512
                ctx.load(ab[:, voff:voff + 512],
                         v2[kv0 + v_ * 128:kv0 + (v_ + 1) * 128, 0:INNER], v_b2,
                         rbufs=[v2_b], tag="l3f_avld")
            for h in range(8):
                ch_, hrow = h // 2, (h % 2) * 64
                for qs in range(4):
                    qap = ab[hrow:hrow + 64, AB_Q + ch_ * 512 + qs * 128:AB_Q + ch_ * 512 + qs * 128 + 128]
                    if w > 0:
                        ctx.emit("pe", lambda e, qap=qap, ch_=ch_, hrow=hrow: e.matmul(
                            pf0[0:128, 0:512], qap,
                            ab[hrow:hrow + 64, AB_K + ch_ * 1024:AB_K + ch_ * 1024 + 512],
                            start=True, stop=True), r=[q_b2, k_b2], w=[pf0b], tag="l3f_mmA")
                    ctx.emit("pe", lambda e, qap=qap, ch_=ch_, hrow=hrow: e.matmul(
                        pf1[0:128, 0:512], qap,
                        ab[hrow:hrow + 64, AB_K + ch_ * 1024 + 512:AB_K + (ch_ + 1) * 1024],
                        start=True, stop=True), r=[q_b2, k_b2], w=[pf1b], tag="l3f_mmB")
                    mi = (0 if w == 0 else 4) + qs
                    if w == 0:
                        ctx.emit("dve", lambda e, qs=qs: e.tensor_copy(
                            pf0[0:128, 0:512], af[:, (0 * 4 + qs) * 1024:(0 * 4 + qs) * 1024 + 512]),
                            r=[mk_b], w=[pf0b], tag="l3f_mkc")
                        ctx.emit("dve", lambda e, qs=qs: e.tensor_tensor(
                            pf1[0:128, 0:512], pf1[0:128, 0:512],
                            af[:, (0 * 4 + qs) * 1024 + 512:(0 * 4 + qs) * 1024 + 1024], ALU.add),
                            r=[mk_b, pf1b], w=[pf1b], tag="l3f_mka")
                    else:
                        ctx.emit("dve", lambda e, qs=qs: e.tensor_tensor(
                            pf0[0:128, 0:512], pf0[0:128, 0:512],
                            af[:, (4 + qs) * 1024:(4 + qs) * 1024 + 512], ALU.add),
                            r=[mk_b, pf0b], w=[pf0b], tag="l3f_mka0")
                        ctx.emit("dve", lambda e, qs=qs: e.tensor_tensor(
                            pf1[0:128, 0:512], pf1[0:128, 0:512],
                            af[:, (4 + qs) * 1024 + 512:(4 + qs) * 1024 + 1024], ALU.add),
                            r=[mk_b, pf1b], w=[pf1b], tag="l3f_mka1")
                    m1 = af[:, 8192:8193]; m2 = af[:, 8193:8194]; nm = af[:, 8194:8195]
                    d1 = af[:, 8195:8196]; d2 = af[:, 8196:8197]; rd = af[:, 8197:8198]
                    ctx.emit("dve", lambda e, m1=m1: e.reduce_max(m1, pf0[0:128, 0:512], axis=AX.X, negate=True),
                             r=[pf0b], w=[col_b2], tag="l3f_r1")
                    ctx.emit("dve", lambda e, m2=m2: e.reduce_max(m2, pf1[0:128, 0:512], axis=AX.X, negate=True),
                             r=[pf1b], w=[col_b2], tag="l3f_r2")
                    ctx.emit("dve", lambda e, m1=m1, m2=m2, nm=nm: e.tensor_tensor(nm, m1, m2, ALU.min),
                             r=[col_b2], w=[col_b2], tag="l3f_nm")
                    ctx.emit("act", lambda e, nm=nm, d1=d1: e.activation(
                        ab[:, AB_E:AB_E + 512], pf0[0:128, 0:512], AF.Exp, bias=nm, accum_out=d1),
                        r=[pf0b, col_b2], w=[e_b2, col_b2], tag="l3f_eA")
                    ctx.emit("act", lambda e, nm=nm, d2=d2: e.activation(
                        ab[:, AB_E + 512:AB_E + 1024], pf1[0:128, 0:512], AF.Exp, bias=nm, accum_out=d2),
                        r=[pf1b, col_b2], w=[e_b2, col_b2], tag="l3f_eB")
                    ctx.emit("dve", lambda e, d1=d1, d2=d2, rd=rd: e.tensor_add(rd, d1, d2),
                             r=[col_b2], w=[col_b2], tag="l3f_dd")
                    ctx.emit("dve", lambda e, rd=rd: e.reciprocal(rd, rd),
                             r=[col_b2], w=[col_b2], tag="l3f_rec")
                    ctx.emit("dve", lambda e, rd=rd: e.tensor_scalar_mul(
                        ab[:, AB_E:AB_E + 1024], ab[:, AB_E:AB_E + 1024], rd[:, 0:1]),
                        r=[e_b2, col_b2], w=[e_b2], tag="l3f_pn")
                    for b8 in range(8):
                        if w == 0 and b8 < 4:
                            continue
                        ctx.emit("pe", lambda e, b8=b8: e.transpose(
                            pb[0:128, 0:128], ab[:, AB_E + b8 * 128:AB_E + (b8 + 1) * 128],
                            identb[:]), r=[e_b2, identb_b], w=[pb_b], tag="l3f_tr")
                        eng = "dve" if b8 % 2 == 0 else "act"
                        if eng == "dve":
                            ctx.emit("dve", lambda e, b8=b8: e.tensor_copy(
                                ab[:, AB_PT + b8 * 128:AB_PT + (b8 + 1) * 128], pb[0:128, 0:128]),
                                r=[pb_b], w=[pt_b2], tag="l3f_trev")
                        else:
                            ctx.emit("act", lambda e, b8=b8: e.copy(
                                ab[:, AB_PT + b8 * 128:AB_PT + (b8 + 1) * 128], pb[0:128, 0:128]),
                                r=[pb_b], w=[pt_b2], tag="l3f_trev")
                    def mmav(e, w=w, h=h):
                        inst = None
                        first = True
                        rng = range(4, 8) if w == 0 else range(8)
                        last = 7
                        for b8 in rng:
                            inst = e.matmul(
                                pf2[0:64, 0:128],
                                ab[:, AB_V + b8 * 512 + h * 64:AB_V + b8 * 512 + h * 64 + 64],
                                ab[:, AB_PT + b8 * 128:AB_PT + (b8 + 1) * 128],
                                start=first, stop=(b8 == last))
                            first = False
                        return inst
                    ctx.emit("pe", mmav, r=[v_b2, pt_b2], w=[pf2b], tag="l3f_av")
                    ost = ab[0:64, AB_E + 896:AB_E + 1024]
                    ctx.emit("act", lambda e, ost=ost: e.copy(ost, pf2[0:64, 0:128]),
                             r=[pf2b], w=[e_b2], tag="l3f_oev")
                    ctx.store(o2[h * 64:(h + 1) * 64, w * WIN + qs * 128:w * WIN + qs * 128 + 128],
                              ost, e_b2, wbufs=[o2_b], q="act", tag="l3f_ost")

        new_phase(ctx)
        def evac_wo(c2, m0, msz, n0, nsz, ps_t, ps_b):
            s = sj[0] % 2; sj[0] += 1
            st = af[:, 8704 + s * 512:8704 + s * 512 + nsz]
            c2.emit("act", lambda e: e.copy(st[0:msz, 0:nsz], ps_t[0:msz, 0:nsz]),
                    r=[ps_b], w=[stg_b[s]], tag="l3f_woev")
            c2.emit("pool", lambda e: e.dma_start(
                xinner[m0:m0 + msz, n0:n0 + nsz], st[0:msz, 0:nsz], accum_op=ALU.add),
                r=[stg_b[s]], w=[xinner_b], dma=True, tag="l3f_woacc")
        linear(ctx, tag="l3fwo", lhsT=("dram_km", wbf["wo"][0], wbf["wo"][1]),
               rhs=("dram", o2, o2_b), K=INNER, M=D, N=T, evac=evac_wo,
               m_tile=1024, k_base=li * INNER, k_base_rhs=0)

        new_phase(ctx)
        lnb_b = ctx.buf("v_l3f_lnb")
        for c in range(NC):
            ctx.load(smalls[:, 348 + c:349 + c],
                     H["ln_g"][0][li * D + c * 128: li * D + (c + 1) * 128, 0:1],
                     lnb_b, rbufs=[H["ln_g"][1]], tag="l3f_lng")
            ctx.load(smalls[:, 356 + c:357 + c],
                     H["ln_b"][0][li * D + c * 128: li * D + (c + 1) * 128, 0:1],
                     lnb_b, rbufs=[H["ln_b"][1]], tag="l3f_lnbld")
        def src_shift(c, t0, tsz, dst_ap, dst_b):
            if c < 4:
                ctx.load(dst_ap[:, 0:tsz], xinner[c * 128:(c + 1) * 128, t0:t0 + tsz], dst_b,
                         rbufs=[xinner_b], tag="l3fln_ld")
            elif t0 > 0:
                ctx.load(dst_ap[:, 0:tsz], xinner[c * 128:(c + 1) * 128, t0 - 1:t0 + tsz - 1], dst_b,
                         rbufs=[xinner_b], tag="l3fln_lds")
            else:
                ctx.load(dst_ap[:, 1:tsz], xinner[c * 128:(c + 1) * 128, 0:tsz - 1], dst_b,
                         rbufs=[xinner_b], tag="l3fln_ld0")
                ctx.emit("dve", lambda e: e.memset(dst_ap[:, 0:1], 0.0), r=[], w=[dst_b], tag="l3fln_z")
        emit_ln(ctx, src_shift,
                affine=(lambda c: (smalls[:, 348 + c:349 + c], lnb_b),
                        lambda c: (smalls[:, 356 + c:357 + c], lnb_b)))

        new_phase(ctx)
        rhs_xn2 = ("sbuf", xnt, xnb, lambda k, n: (k // 128) * T + n)
        b1_b = ctx.buf("v_l3f_b1")
        def evac_g(c2, m0, msz, n0, nsz, ps_t, ps_b):
            bcol = af[0:msz, 9728:9729]
            c2.load(bcol, H["b1"][0][li * 2 * HID + HID + m0: li * 2 * HID + HID + m0 + msz, 0:1],
                    b1_b, rbufs=[H["b1"][1]], q="sp", tag="l3f_b1g")
            s = sj[0] % 2; sj[0] += 1
            st = ab[:, s * 512: s * 512 + nsz]
            c2.emit("act", lambda e: e.activation(st[0:msz, 0:nsz], ps_t[0:msz, 0:nsz],
                                                  AF.Gelu, bias=bcol),
                    r=[ps_b, b1_b], w=[stg_b[s]], tag="l3f_gev")
            c2.store(hg[m0:m0 + msz, n0:n0 + nsz], st[0:msz, 0:nsz], stg_b[s],
                     wbufs=[hg_b], q="act", tag="l3f_gst")
        linear(ctx, tag="l3fg", lhsT=("dram_km", wbf["w1"][0], wbf["w1"][1]), rhs=rhs_xn2,
               K=D, M=HID, N=T, evac=evac_g, m_tile=512, k_base=li * D, m_base=HID)
        hgt_b = ctx.buf("v_l3f_hgt")
        def evac_a(c2, m0, msz, n0, nsz, ps_t, ps_b):
            bcol = af[0:msz, 9729:9730]
            c2.load(bcol, H["b1"][0][li * 2 * HID + m0: li * 2 * HID + m0 + msz, 0:1],
                    b1_b, rbufs=[H["b1"][1]], q="sp", tag="l3f_b1a")
            hgt = ab[0:msz, 4096 + (sj[0] % 2) * 512:4096 + (sj[0] % 2) * 512 + nsz]
            c2.load(hgt, hg[m0:m0 + msz, n0:n0 + nsz], hgt_b, rbufs=[hg_b], q="sp", tag="l3f_hgld")
            tmp = af[0:msz, 9984:9984 + nsz]
            c2.emit("act", lambda e: e.activation(tmp[0:msz, 0:nsz], ps_t[0:msz, 0:nsz],
                                                  AF.Identity, bias=bcol),
                    r=[ps_b, b1_b], w=[ctx.buf("v_l3f_tmp")], tag="l3f_aev")
            s = sj[0] % 2; sj[0] += 1
            st = ab[:, s * 512: s * 512 + nsz]
            c2.emit("dve", lambda e, hgt=hgt: e.tensor_tensor(st[0:msz, 0:nsz], tmp[0:msz, 0:nsz],
                                                              hgt[0:msz, 0:nsz], ALU.mult),
                    r=[ctx.buf("v_l3f_tmp"), hgt_b], w=[stg_b[s]], tag="l3f_amul")
            c2.store(h2[m0:m0 + msz, n0:n0 + nsz], st[0:msz, 0:nsz], stg_b[s],
                     wbufs=[h2_b], q="act", tag="l3f_ast")
        linear(ctx, tag="l3fa", lhsT=("dram_km", wbf["w1"][0], wbf["w1"][1]), rhs=rhs_xn2,
               K=D, M=HID, N=T, evac=evac_a, m_tile=512, k_base=li * D, m_base=0)

        new_phase(ctx)
        b2_b = ctx.buf("v_l3f_b2")
        for c in range(NC):
            ctx.load(smalls[:, 364 + c:365 + c],
                     H["b2"][0][li * D + c * 128: li * D + (c + 1) * 128, 0:1],
                     b2_b, rbufs=[H["b2"][1]], tag="l3f_b2ld")
        for kp, (kb, ksz) in enumerate([(0, 1024), (1024, 1024), (2048, 682)]):
            def evac_w2(c2, m0, msz, n0, nsz, ps_t, ps_b, kp=kp):
                s = sj[0] % 2; sj[0] += 1
                st = af[:, 8704 + s * 512:8704 + s * 512 + nsz]
                if kp == 0:
                    mch = m0 // 128
                    c2.emit("act", lambda e, mch=mch: e.activation(
                        st[0:msz, 0:nsz], ps_t[0:msz, 0:nsz], AF.Identity,
                        bias=smalls[:, 364 + mch:365 + mch]),
                        r=[ps_b, b2_b], w=[stg_b[s]], tag="l3f_w2ev")
                else:
                    c2.emit("act", lambda e: e.copy(st[0:msz, 0:nsz], ps_t[0:msz, 0:nsz]),
                            r=[ps_b], w=[stg_b[s]], tag="l3f_w2ev2")
                c2.emit("pool", lambda e: e.dma_start(
                    xinner[m0:m0 + msz, n0:n0 + nsz], st[0:msz, 0:nsz], accum_op=ALU.add),
                    r=[stg_b[s]], w=[xinner_b], dma=True, tag="l3f_w2acc")
            linear(ctx, tag="l3fw2_%d_%d" % (li, kp),
                   lhsT=("dram_km", wbf["w2"][0], wbf["w2"][1]), rhs=("dram", h2, h2_b),
                   K=ksz, M=D, N=T, evac=evac_w2, m_tile=1024,
                   k_base=li * HID + kb, k_base_rhs=kb, n_tile=512)


def build_l3_program():
    nc = bass.Bass("TRN2", target_bir_lowering=False)
    with ExitStack() as stk:
        ctx = Ctx(nc, stk)
        xin = ctx.dram("x_in", (T, D), F32, kind="ExternalInput")
        yout = ctx.dram("y_out", (T, D), F32, kind="ExternalOutput")
        H = {}
        shapes = [("Wq", (D, D)), ("bq", (D, 1)), ("Wk", (D, D)), ("bk", (D, 1)),
                  ("Wv", (D, D)), ("bv", (D, 1)), ("Wg", (D, D)), ("bg", (D, 1)),
                  ("Wo", (D, D)), ("bo", (D, 1)), ("gn_w", (16, 1)), ("gn_b", (16, 1)),
                  ("wqkv", (2 * D, 1536)), ("q_scale", (2 * 64, 1)), ("k_scale", (2 * 64, 1)),
                  ("wo2", (2 * 512, D)), ("ln_g", (2 * D, 1)), ("ln_b", (2 * D, 1)),
                  ("w1", (2 * D, 5460)), ("b1", (2 * 5460, 1)), ("w2", (2 * 2730, D)),
                  ("b2", (2 * D, 1))]
        for nm, shape in shapes:
            h = ctx.dram("i_" + nm, shape, F32, kind="ExternalInput")
            H[nm] = (h, ctx.buf("i_" + nm))
        H["wo"] = H["wo2"]
        xh = ctx.dram("xh", (D, T), F32); xhb = ctx.buf("xh")
        xh2 = ctx.dram("xh2", (D, T), F32); xh2b = ctx.buf("xh2")
        y3 = ctx.dram("l3_y3", (D, T), F32); y3_b = ctx.buf("l3_y3")
        xinner = ctx.dram("l3_xi", (D, T), F32); xinner_b = ctx.buf("l3_xi")
        alloc_standard(ctx)
        emit_load_smalls(ctx)
        emit_transpose_in(ctx, xin, ctx.buf("x_in"), xh, xhb)
        new_phase(ctx)
        emit_layer3(ctx, H, xh, xhb, xh2, xh2b, y3, y3_b, xinner, xinner_b)
        new_phase(ctx)
        emit_transpose_out(ctx, xh2, xh2b, yout, ctx.buf("y_out"))
        finish(ctx)
    return nc


def emit_layer3(ctx, H, xh, xhb, xh2, xh2b, y3, y3_b, xinner, xinner_b):
    af = ctx.tensors["af"]
    xnt = ctx.tensors["xn"]; xnb = ctx.bufs["xn"]
    def src_x(c, t0, tsz, dst_ap, dst_b):
        ctx.load(dst_ap[:, 0:tsz], xh[c * 128:(c + 1) * 128, t0:t0 + tsz], dst_b,
                 rbufs=[xhb], tag="l3ln_ld")
    emit_ln(ctx, src_x)
    new_phase(ctx)
    Hattn = {k: H[k] for k in ("Wq", "bq", "Wk", "bk", "Wv", "bv", "Wg", "bg", "Wo", "bo",
                               "gn_w", "gn_b")}
    ret_pack = emit_layer3_attn(ctx, Hattn, xh, xhb, y3, y3_b)
    wbf_a, qT, qT_b, kT, kT_b, gT, gT_b, vtok, vtok_b, ret, ret_b = ret_pack
    from_consts = l3_attn_consts()
    hBH = ctx.const("l3BH16", from_consts[10])
    emit_layer3_attn_tail(ctx, Hattn, wbf_a, gT, gT_b, ret, ret_b, y3, y3_b, hBH)
    new_phase(ctx)
    xi_init_b = ctx.buf("v_l3_xii")
    for c in range(NC):
        ctx.emit("pool", lambda e, c=c: e.dma_start(
            xinner[c * 128:(c + 1) * 128, 0:T], xnt[:, c * T:(c + 1) * T]),
            r=[xnb], w=[xinner_b], dma=True, tag="l3_xiinit")
    Hffn = {k: H[k] for k in ("wqkv", "q_scale", "k_scale", "wo", "ln_g", "ln_b",
                              "w1", "b1", "w2", "b2")}
    emit_layer3_ffn(ctx, Hffn, xinner, xinner_b)
    new_phase(ctx)
    xr_b = ctx.buf("v_l3z_x")
    ya_b = ctx.buf("v_l3z_y")
    fi_b = ctx.buf("v_l3z_f")
    ot_b = ctx.buf("v_l3z_o")
    for ti in range(NTT):
        t0 = ti * TT
        for m in range(NC):
            xr = af[:, 0:TT]
            ya = af[:, TT:2 * TT]
            fi = af[:, 2 * TT:3 * TT]
            ot = af[:, 3 * TT:4 * TT]
            ctx.load(xr, xh[m * 128:(m + 1) * 128, t0:t0 + TT], xr_b, rbufs=[xhb], tag="l3z_x")
            ctx.load(ya, y3[m * 128:(m + 1) * 128, t0:t0 + TT], ya_b, rbufs=[y3_b], tag="l3z_y")
            ctx.load(fi, xinner[m * 128:(m + 1) * 128, t0:t0 + TT], fi_b, rbufs=[xinner_b], tag="l3z_f")
            ctx.emit("dve", lambda e, ot=ot, xr=xr, ya=ya: e.tensor_tensor(ot, xr, ya, ALU.add),
                     r=[xr_b, ya_b], w=[ot_b], tag="l3z_a1")
            ctx.emit("dve", lambda e, ot=ot, fi=fi: e.tensor_tensor(ot, ot, fi, ALU.add),
                     r=[ot_b, fi_b], w=[ot_b], tag="l3z_a2")
            ctx.store(xh2[m * 128:(m + 1) * 128, t0:t0 + TT], ot, ot_b,
                      wbufs=[xh2b], q="act", tag="l3z_st")


def build_full_program():
    nc = bass.Bass("TRN2", target_bir_lowering=False)
    with ExitStack() as stk:
        ctx = Ctx(nc, stk)
        xin = ctx.dram("x_in", (T, D), F32, kind="ExternalInput")
        yout = ctx.dram("y_out", (T, D), F32, kind="ExternalOutput")

        def ins(prefix, shapes):
            out = {}
            for nm, shape in shapes:
                h = ctx.dram("i_%s_%s" % (prefix, nm), shape, F32, kind="ExternalInput")
                out[nm] = (h, ctx.buf("i_%s_%s" % (prefix, nm)))
            return out
        H0 = ins("l0", [("wq", (D, D)), ("wk", (D, D)), ("wv", (D, D)), ("wo", (D, D)),
                        ("pope_bias", (16, 64)), ("router_w", (8, D)),
                        ("w_gate", (8 * 2048, D)), ("w_up", (8 * 2048, D)),
                        ("w_down", (8 * D, 2048))])
        H1 = ins("l1", [("Wr", (D, D)), ("Wk", (D, D)), ("Wv", (D, D)), ("Ww", (D, D)),
                        ("Wo", (D, D)), ("Wfr", (D, D)), ("Wfi", (D, D)), ("bias", (D, 1))])
        H2 = ins("l2", [("Wq", (D, D)), ("bq", (D, 1)), ("Wk", (D, D)), ("bk", (D, 1)),
                        ("Wv", (D, D)), ("bv", (D, 1)), ("w_up", (4096, D)),
                        ("w_down", (D, 4096))])
        H3 = ins("l3", [("Wq", (D, D)), ("bq", (D, 1)), ("Wk", (D, D)), ("bk", (D, 1)),
                        ("Wv", (D, D)), ("bv", (D, 1)), ("Wg", (D, D)), ("bg", (D, 1)),
                        ("Wo", (D, D)), ("bo", (D, 1)), ("gn_w", (16, 1)), ("gn_b", (16, 1)),
                        ("wqkv", (2 * D, 1536)), ("q_scale", (2 * 64, 1)), ("k_scale", (2 * 64, 1)),
                        ("wo2", (2 * 512, D)), ("ln_g", (2 * D, 1)), ("ln_b", (2 * D, 1)),
                        ("w1", (2 * D, 5460)), ("b1", (2 * 5460, 1)), ("w2", (2 * 2730, D)),
                        ("b2", (2 * D, 1))])
        H3["wo"] = H3["wo2"]

        xh_a = ctx.dram("xh_a", (D, T), F32); xh_ab = ctx.buf("xh_a")
        xh_b = ctx.dram("xh_b", (D, T), F32); xh_bb = ctx.buf("xh_b")
        y3 = ctx.dram("l3_y3", (D, T), F32); y3_b = ctx.buf("l3_y3")
        xinner = ctx.dram("l3_xi", (D, T), F32); xinner_b = ctx.buf("l3_xi")
        alloc_standard(ctx)
        emit_load_smalls(ctx)
        emit_transpose_in(ctx, xin, ctx.buf("x_in"), xh_a, xh_ab)
        new_phase(ctx)
        emit_layer0(ctx, H0, xh_a, xh_ab, xh_b, xh_bb)
        new_phase(ctx)
        emit_layer1(ctx, H1, xh_b, xh_bb, xh_a, xh_ab)
        new_phase(ctx)
        emit_layer2(ctx, H2, xh_a, xh_ab, xh_b, xh_bb)
        new_phase(ctx)
        emit_layer3(ctx, H3, xh_b, xh_bb, xh_a, xh_ab, y3, y3_b, xinner, xinner_b)
        new_phase(ctx)
        emit_transpose_out(ctx, xh_a, xh_ab, yout, ctx.buf("y_out"))
        finish(ctx)
        print("fused program units:", ctx.p.n_units)
    return nc



def _as2d(a, shape):
    return np.ascontiguousarray(np.asarray(a, dtype=np.float32).reshape(shape))


def kernel(**inputs):
    ncores = 8
    x = np.ascontiguousarray(np.asarray(inputs["L0_attn_x"], dtype=np.float32))

    l0_w = {"i_wq": _as2d(inputs["L0_attn_wq"], (D, D)),
            "i_wk": _as2d(inputs["L0_attn_wk"], (D, D)),
            "i_wv": _as2d(inputs["L0_attn_wv"], (D, D)),
            "i_wo": _as2d(inputs["L0_attn_wo"], (D, D)),
            "i_pope_bias": _as2d(inputs["L0_attn_pope_bias"], (16, 64)),
            "i_router_w": _as2d(inputs["L0_ffn_router_w"], (8, D)),
            "i_w_gate": _as2d(inputs["L0_ffn_w_gate"], (8 * 2048, D)),
            "i_w_up": _as2d(inputs["L0_ffn_w_up"], (8 * 2048, D)),
            "i_w_down": _as2d(inputs["L0_ffn_w_down"], (8 * D, 2048))}
    l1_w = {"i_Wr": _as2d(inputs["L1_attn_Wr"], (D, D)),
            "i_Wk": _as2d(inputs["L1_attn_Wk"], (D, D)),
            "i_Wv": _as2d(inputs["L1_attn_Wv"], (D, D)),
            "i_Ww": _as2d(inputs["L1_attn_Ww"], (D, D)),
            "i_Wo": _as2d(inputs["L1_attn_Wo"], (D, D)),
            "i_Wfr": _as2d(inputs["L1_ffn_w_freq_real"], (D, D)),
            "i_Wfi": _as2d(inputs["L1_ffn_w_freq_imag"], (D, D)),
            "i_bias": _as2d(inputs["L1_ffn_bias"], (D, 1))}
    l2_w = {"i_Wq": _as2d(inputs["L2_attn_Wq"], (D, D)),
            "i_bq": _as2d(inputs["L2_attn_bq"], (D, 1)),
            "i_Wk": _as2d(inputs["L2_attn_Wk"], (D, D)),
            "i_bk": _as2d(inputs["L2_attn_bk"], (D, 1)),
            "i_Wv": _as2d(inputs["L2_attn_Wv"], (D, D)),
            "i_bv": _as2d(inputs["L2_attn_bv"], (D, 1)),
            "i_w_up": _as2d(inputs["L2_ffn_w_up"], (4096, D)),
            "i_w_down": _as2d(inputs["L2_ffn_w_down"], (D, 4096))}
    l3_w = {"i_Wq": _as2d(inputs["L3_attn_Wq"], (D, D)),
            "i_bq": _as2d(inputs["L3_attn_bq"], (D, 1)),
            "i_Wk": _as2d(inputs["L3_attn_Wk"], (D, D)),
            "i_bk": _as2d(inputs["L3_attn_bk"], (D, 1)),
            "i_Wv": _as2d(inputs["L3_attn_Wv"], (D, D)),
            "i_bv": _as2d(inputs["L3_attn_bv"], (D, 1)),
            "i_Wg": _as2d(inputs["L3_attn_Wg"], (D, D)),
            "i_bg": _as2d(inputs["L3_attn_bg"], (D, 1)),
            "i_Wo": _as2d(inputs["L3_attn_Wo"], (D, D)),
            "i_bo": _as2d(inputs["L3_attn_bo"], (D, 1)),
            "i_gn_w": _as2d(inputs["L3_attn_gn_w"], (16, 1)),
            "i_gn_b": _as2d(inputs["L3_attn_gn_b"], (16, 1)),
            "i_wqkv": _as2d(inputs["L3_ffn_wqkv"], (2 * D, 1536)),
            "i_q_scale": _as2d(inputs["L3_ffn_q_scale"], (128, 1)),
            "i_k_scale": _as2d(inputs["L3_ffn_k_scale"], (128, 1)),
            "i_wo2": _as2d(inputs["L3_ffn_wo"], (2 * 512, D)),
            "i_ln_g": _as2d(inputs["L3_ffn_ln_g"], (2 * D, 1)),
            "i_ln_b": _as2d(inputs["L3_ffn_ln_b"], (2 * D, 1)),
            "i_w1": _as2d(inputs["L3_ffn_w1"], (2 * D, 5460)),
            "i_b1": _as2d(inputs["L3_ffn_b1"], (2 * 5460, 1)),
            "i_w2": _as2d(inputs["L3_ffn_w2"], (2 * 2730, D)),
            "i_b2": _as2d(inputs["L3_ffn_b2"], (2 * D, 1))}

    wall = {}
    for pfx, wmap in (("l0", l0_w), ("l1", l1_w), ("l2", l2_w), ("l3", l3_w)):
        for k, v in wmap.items():
            wall["i_%s_%s" % (pfx, k[2:])] = v
    nc = build_full_program()
    in_maps = [dict(wall, x_in=np.ascontiguousarray(x[c])) for c in range(ncores)]
    res = run_bass_kernel_spmd(nc, in_maps, core_ids=list(range(ncores)))
    return np.stack([np.asarray(res.results[c]["y_out"], dtype=np.float32)
                     for c in range(ncores)], axis=0)
